# version 04
# baseline (speedup 1.0000x reference)
import numpy as np
import ml_dtypes
from contextlib import ExitStack

import concourse.bass as bass
import concourse.mybir as mybir
from concourse.bass_utils import run_bass_kernel_spmd

F32 = mybir.dt.float32
BF16 = mybir.dt.bfloat16
FP8 = mybir.dt.float8e4
AF = mybir.ActivationFunctionType
ALU = mybir.AluOpType
AX = mybir.AxisListType

D = 2048
T = 1024
NT = T // 128
NKC = D // 128
MAGIC = 12582912.0

C_MAGIC, C_NMAGIC, C_EPS, C_ONE, C_NEG1E30, C_127, C_EPS9, C_ZERO = range(8)

NCORES = 8


_TAPS = [False]


class Sem:
    def __init__(self, h):
        self.h = h
        self.v = 0

    def inc(self, n=1):
        self.v += n
        return self.v


class Ctx:
    def __init__(self, nc, block, es):
        self.nc = nc
        self.block = block
        self.es = es
        self._sems = {}
        self.t = {}

    def sem(self, name) -> Sem:
        if name not in self._sems:
            h = self.es.enter_context(self.nc.semaphore(name))
            self._sems[name] = Sem(h)
        return self._sems[name]

    def sb(self, name, shape, dt, scope=None):
        h = (scope or self.es).enter_context(self.nc.sbuf_tensor(name, shape, dt))
        self.t[name] = h
        return h

    def ps(self, name, shape, dt, scope=None):
        h = (scope or self.es).enter_context(self.nc.psum_tensor(name, shape, dt))
        self.t[name] = h
        return h

    def tap(self, name, ap):
        if not _TAPS[0]:
            return
        nc, blk = self.nc, self.block
        d = nc.dram_tensor("tap_" + name, list(ap.shape), ap.dtype,
                           kind="ExternalOutput").ap()
        s_t = self.sem("tapq")
        t0 = s_t.inc(16)

        @blk.sync
        def _(s, d=d, ap=ap, t0=t0):
            s.dma_start(out=d, in_=ap).then_inc(s_t.h, 16)
            s.wait_ge(s_t.h, t0)

    def barrier(self, tag):
        bar = self.sem("bar")
        target = bar.v + 5
        blk = self.block
        dummy = self.t["sdummy"]

        @blk.sync
        def _(e):
            e.nop().then_inc(bar.h, 1)
            e.wait_ge(bar.h, target)

        @blk.tensor
        def _(e):
            e.nop().then_inc(bar.h, 1)
            e.wait_ge(bar.h, target)

        @blk.vector
        def _(e):
            e.engine_nop().then_inc(bar.h, 1)
            e.wait_ge(bar.h, target)

        @blk.scalar
        def _(e):
            e.copy(dummy[0:1, 0:1], dummy[0:1, 1:2]).then_inc(bar.h, 1)
            e.wait_ge(bar.h, target)

        @blk.gpsimd
        def _(e):
            e.engine_nop().then_inc(bar.h, 1)
            e.wait_ge(bar.h, target)

        bar.v = target



def emit_ln_tile(ctx, src_ap_f, dst_ap, sq_scratch, stats, cpack, wait_before=None,
                 done_sem=None, src_is_bf16=False):
    blk = ctx.block
    v_sem = ctx.sem("lnv")
    s_sem = ctx.sem("lns")
    vt = v_sem.inc()

    @blk.vector
    def _(v):
        if wait_before is not None:
            wait_before(v)
        v.reduce_sum(stats[:, 0:1], src_ap_f, axis=AX.X).then_inc(v_sem.h, 1)

    st = s_sem.inc()

    @blk.scalar
    def _(sc):
        if wait_before is not None:
            wait_before(sc)
        sc.activation(sq_scratch[:], src_ap_f, AF.Square,
                      accum_out=stats[:, 1:2]).then_inc(s_sem.h, 1)

    vt2 = v_sem.inc(3)

    @blk.vector
    def _(v):
        v.wait_ge(s_sem.h, st)
        v.wait_ge(v_sem.h, vt)
        v.tensor_scalar_mul(stats[:, 2:3], stats[:, 0:1], 1.0 / 2048.0).then_inc(v_sem.h, 1)
        v.wait_ge(v_sem.h, vt + 1)
        v.tensor_scalar(stats[:, 4:5], stats[:, 2:3], stats[:, 2:3], None,
                        op0=ALU.mult).then_inc(v_sem.h, 1)
        v.wait_ge(v_sem.h, vt + 2)
        v.tensor_scalar(stats[:, 5:6], stats[:, 1:2], 1.0 / 2048.0, stats[:, 4:5],
                        op0=ALU.mult, op1=ALU.subtract).then_inc(v_sem.h, 1)

    st2 = s_sem.inc()

    @blk.scalar
    def _(sc):
        sc.wait_ge(v_sem.h, vt2)
        sc.activation(stats[:, 6:7], stats[:, 5:6], AF.Sqrt,
                      bias=cpack[:, C_EPS:C_EPS + 1]).then_inc(s_sem.h, 1)

    vt3 = v_sem.inc()

    @blk.vector
    def _(v):
        v.wait_ge(s_sem.h, st2)
        v.reciprocal(stats[:, 7:8], stats[:, 6:7]).then_inc(v_sem.h, 1)
        v.wait_ge(v_sem.h, vt3)
        ts = v.tensor_scalar(dst_ap, src_ap_f, stats[:, 2:3], stats[:, 7:8],
                             op0=ALU.subtract, op1=ALU.mult)
        if done_sem is not None:
            ts.then_inc(done_sem.h, 1)
    if done_sem is not None:
        done_sem.inc()


def emit_tm_to_fm(ctx, tm_tiles, fm_dst, pT, ident, pe_sem, ev_sem, tag,
                  wait_pe=None, tile_done=None):
    blk = ctx.block
    ngroups = NT // 4
    plan = [(fc, g) for fc in range(NKC) for g in range(ngroups)]
    pe_base = pe_sem.v
    ev_base = ev_sem.v

    @blk.tensor
    def _(t):
        for i, (fc, g) in enumerate(plan):
            bank = pT[i % 2]
            if i >= 2:
                t.wait_ge(ev_sem.h, ev_base + i - 1)
            if wait_pe is not None:
                wait_pe(t, fc, g)
            last = None
            for j in range(4):
                tt = g * 4 + j
                src = tm_tiles[tt]
                last = t.transpose(bank[:, j * 128:(j + 1) * 128],
                                   src[:, fc * 128:(fc + 1) * 128], ident[:])
            last.then_inc(pe_sem.h, 1)
    pe_sem.inc(len(plan))

    @blk.scalar
    def _(sc):
        for i, (fc, g) in enumerate(plan):
            bank = pT[i % 2]
            sc.wait_ge(pe_sem.h, pe_base + i + 1)
            sc.copy(fm_dst[:, fc, g * 512:(g + 1) * 512],
                    bank[:, 0:512]).then_inc(ev_sem.h, 1)
    ev_sem.inc(len(plan))


def emit_fm_to_tm_tile(ctx, fm_src_col, tm_dst, pT, ident, pe_sem, ev_sem,
                       n_chunks=NKC, evac_engine="scalar"):
    blk = ctx.block
    ngroups = (n_chunks + 7) // 8
    pe_base = pe_sem.v
    ev_base = ev_sem.v

    @blk.tensor
    def _(t):
        for g in range(ngroups):
            bank = pT[g % 2]
            if g >= 2:
                t.wait_ge(ev_sem.h, ev_base + g - 1)
            last = None
            for j in range(min(8, n_chunks - g * 8)):
                fc = g * 8 + j
                last = t.transpose(bank[:, j * 128:(j + 1) * 128], fm_src_col(fc),
                                   ident[:])
            last.then_inc(pe_sem.h, 1)
    pe_sem.inc(ngroups)

    dec = blk.scalar if evac_engine == "scalar" else blk.vector

    @dec
    def _(e):
        for g in range(ngroups):
            bank = pT[g % 2]
            e.wait_ge(pe_sem.h, pe_base + g + 1)
            n = min(8, n_chunks - g * 8) * 128
            cp = e.copy if evac_engine == "scalar" else e.tensor_copy
            cp(tm_dst[:, g * 1024:g * 1024 + n], bank[:, 0:n]).then_inc(ev_sem.h, 1)
    ev_sem.inc(ngroups)


def emit_ln_to_fm(ctx, xn_fm, pT):
    blk = ctx.block
    x_src = ctx.x_src
    xtile = ctx.t["x_tile"]
    ident = ctx.t["s_ident"]
    cpack = ctx.t["s_cpack"]
    sq = ctx.t["sq_scratch"]
    stats = ctx.t["stats"]
    tmb = ctx.t["tm_scratch"]
    s_pe = ctx.sem("pe")
    s_ev = ctx.sem("ev")
    s_dv = ctx.sem("dv")
    s_ld = ctx.sem("ld")
    s_st = ctx.sem("st")
    for tt in range(NT):
        st_target = s_st.v
        dv_before = s_dv.v
        ld_t = s_ld.inc(16)

        @blk.sync
        def _(s, tt=tt, st_target=st_target, dv_before=dv_before):
            s.wait_ge(s_st.h, st_target)
            s.wait_ge(s_dv.h, dv_before)
            s.dma_start(out=xtile[:], in_=x_src[tt * 128:(tt + 1) * 128, :]
                        ).then_inc(s_ld.h, 16)

        def wait_x(e, ld_t=ld_t):
            e.wait_ge(s_ld.h, ld_t)

        emit_ln_tile(ctx, xtile[:], tmb[:], sq, stats, cpack,
                     done_sem=s_dv, wait_before=wait_x)
        dv_target = s_dv.v
        pe_base = s_pe.v
        ev_base = s_ev.v

        @blk.tensor
        def _(t, tt=tt, dv_target=dv_target, pe_base=pe_base, ev_base=ev_base):
            for g in range(2):
                bank = pT[g % 2]
                t.wait_ge(s_dv.h, dv_target)
                if tt > 0 or g > 0:
                    t.wait_ge(s_ev.h, max(0, ev_base + g - 1))
                last = None
                for j in range(8):
                    fc = g * 8 + j
                    last = t.transpose(bank[:, j * 128:(j + 1) * 128],
                                       tmb[:, fc * 128:(fc + 1) * 128], ident[:])
                last.then_inc(s_pe.h, 1)
        s_pe.inc(2)

        @blk.scalar
        def _(sc, tt=tt, pe_base=pe_base, ev_base=ev_base):
            for g in range(2):
                bank = pT[g % 2]
                sc.wait_ge(s_pe.h, pe_base + g + 1)
                sc.copy(xn_fm[:, g * 8:(g + 1) * 8, tt * 128:(tt + 1) * 128],
                        bank[:].rearrange("p (a b) -> p a b", a=8)
                        ).then_inc(s_ev.h, 1)
        s_ev.inc(2)
        ctx.barrier(f"lnfm_{tt}")


def emit_out_epilogue(ctx, o_fm_col, pT, tiles=None):
    blk = ctx.block
    x_src = ctx.x_src
    x_dst = ctx.x_dst
    xtile = ctx.t["x_tile"]
    ident = ctx.t["s_ident"]
    cpack = ctx.t["s_cpack"]
    sq = ctx.t["sq_scratch"]
    stats = ctx.t["stats"]
    tmb = ctx.t["tm_scratch"]
    s_pe = ctx.sem("pe")
    s_ev = ctx.sem("ev")
    s_dv = ctx.sem("dv")
    s_ld = ctx.sem("ld")
    s_st = ctx.sem("st")
    for tt in (tiles if tiles is not None else range(NT)):
        st_before = s_st.v
        ld_t = s_ld.inc(16)

        @blk.sync
        def _(s, tt=tt, st_before=st_before):
            s.wait_ge(s_st.h, st_before)
            s.dma_start(out=xtile[:], in_=x_src[tt * 128:(tt + 1) * 128, :]
                        ).then_inc(s_ld.h, 16)

        emit_fm_to_tm_tile(ctx, lambda fc, tt=tt: o_fm_col(fc, tt),
                           tmb[:], pT, ident, s_pe, s_ev)
        ctx.barrier(f"oe_t_{tt}")
        emit_ln_tile(ctx, tmb[:], sq[:], sq, stats, cpack, done_sem=s_dv)
        dv_t = s_dv.v
        dv_t2 = s_dv.inc()

        @blk.vector
        def _(v, tt=tt, dv_t=dv_t, ld_t=ld_t):
            v.wait_ge(s_dv.h, dv_t)
            v.wait_ge(s_ld.h, ld_t)
            v.tensor_add(xtile[:], xtile[:], sq[:]).then_inc(s_dv.h, 1)
        st_t = s_st.inc(16)

        @blk.scalar
        def _(s, tt=tt, dv_t2=dv_t2):
            s.wait_ge(s_dv.h, dv_t2)
            s.dma_start(out=x_dst[tt * 128:(tt + 1) * 128, :], in_=xtile[:]
                        ).then_inc(s_st.h, 16)
        ctx.barrier(f"oe_r_{tt}")



def emit_phase_C(ctx, hbm):
    nc, blk = ctx.nc, ctx.block
    es_ph = ExitStack()
    ident = ctx.t["s_ident"]
    cpack = ctx.t["s_cpack"]
    sq = ctx.t["sq_scratch"]
    stats = ctx.t["stats"]
    tmb = ctx.t["tm_scratch"]
    NST = 8

    xn_fm = ctx.sb("c_xnfm", [128, NKC, T], BF16, es_ph)
    state_fm = ctx.sb("c_state", [128, NST, T], BF16, es_ph)
    dpack = ctx.t["s_c_dpack"]

    pT = [ctx.ps("c_pt0", [128, 1024], BF16, es_ph),
          ctx.ps("c_pt1", [128, 1024], BF16, es_ph)]
    pb = [ctx.ps(f"c_pb{i}", [128, 512], F32, es_ph) for i in range(4)]

    s_pe = ctx.sem("pe")
    s_ev = ctx.sem("ev")
    s_dv = ctx.sem("dv")
    s_ld = ctx.sem("ld")
    s_cast = ctx.sem("cast")

    emit_ln_to_fm(ctx, xn_fm, pT)

    es_b = ExitStack()
    b_buf = ctx.sb("c_bbuf", [128, NST, 2048], BF16, es_b)
    bt = ctx.sb("c_bt", [128, NKC, 1024], BF16, es_b)

    cast_base = s_cast.v

    @blk.gpsimd
    def _(g, cast_base=cast_base):
        g.wait_ge(s_cast.h, cast_base)
        g.dma_start(out=b_buf[:],
                    in_=hbm["L1_attn_B"].rearrange("(sc p) d -> p sc d", p=128)
                    ).then_inc(s_cast.h, 16)
    s_cast.inc(16)

    pe_base = s_pe.v
    ev_base = s_ev.v

    @blk.tensor
    def _(t):
        t.wait_ge(s_cast.h, cast_base + 16)
        for kc in range(NKC):
            bank = pT[kc % 2]
            if kc >= 2:
                t.wait_ge(s_ev.h, ev_base + kc - 1)
            last = None
            for sc in range(NST):
                last = t.transpose(bank[:, sc * 128:(sc + 1) * 128],
                                   b_buf[:, sc, kc * 128:(kc + 1) * 128], ident[:])
            last.then_inc(s_pe.h, 1)
    s_pe.inc(NKC)

    @blk.vector
    def _(v):
        for kc in range(NKC):
            bank = pT[kc % 2]
            v.wait_ge(s_pe.h, pe_base + kc + 1)
            v.tensor_copy(bt[:, kc, :], bank[:]).then_inc(s_ev.h, 1)
    s_ev.inc(NKC)

    ctx.barrier("c_bt")

    pe_base = s_pe.v
    ev_base = s_ev.v

    @blk.tensor
    def _(t):
        for sc in range(NST):
            for fc in range(2):
                bank = pb[(sc % 2) * 2 + fc]
                gi = sc * 2 + fc
                if gi >= 4:
                    t.wait_ge(s_ev.h, ev_base + gi - 3)
                last = None
                for kc in range(NKC):
                    last = t.matmul(bank[:], bt[:, kc, sc * 128:(sc + 1) * 128],
                                    xn_fm[:, kc, fc * 512:(fc + 1) * 512],
                                    start=(kc == 0), stop=(kc == NKC - 1))
                last.then_inc(s_pe.h, 1)
    s_pe.inc(NST * 2)

    @blk.scalar
    def _(sc_e):
        for sc in range(NST):
            for fc in range(2):
                bank = pb[(sc % 2) * 2 + fc]
                gi = sc * 2 + fc
                sc_e.wait_ge(s_pe.h, pe_base + gi + 1)
                sc_e.activation(state_fm[:, sc, fc * 512:(fc + 1) * 512],
                                bank[:], AF.Tanh).then_inc(s_ev.h, 1)
    s_ev.inc(NST * 2)

    ctx.barrier("c_state")
    es_b.close()

    es_c = ExitStack()
    c_buf = ctx.sb("c_cbuf", [128, 8, 1024], BF16, es_c)
    ct = ctx.sb("c_ct", [128, NST, 2048], BF16, es_c)
    o_fm = ctx.sb("c_ofm", [128, NKC, T], BF16, es_c)

    for r in range(2):
        cast_base = s_cast.v

        @blk.gpsimd
        def _(g, r=r, cast_base=cast_base):
            g.wait_ge(s_cast.h, cast_base)
            g.dma_start(out=c_buf[:],
                        in_=hbm["L1_attn_C"][r * 1024:(r + 1) * 1024, :]
                        .rearrange("(oc p) s -> p oc s", p=128)
                        ).then_inc(s_cast.h, 16)
        s_cast.inc(16)

        pe_base = s_pe.v
        ev_base = s_ev.v

        @blk.tensor
        def _(t, r=r, cast_base=cast_base, pe_base=pe_base, ev_base=ev_base):
            t.wait_ge(s_cast.h, cast_base + 16)
            for sc in range(NST):
                bank = pT[sc % 2]
                if sc >= 2:
                    t.wait_ge(s_ev.h, ev_base + sc - 1)
                last = None
                for j in range(8):
                    last = t.transpose(bank[:, j * 128:(j + 1) * 128],
                                       c_buf[:, j, sc * 128:(sc + 1) * 128], ident[:])
                last.then_inc(s_pe.h, 1)
        s_pe.inc(NST)

        @blk.vector
        def _(v, r=r, pe_base=pe_base):
            for sc in range(NST):
                bank = pT[sc % 2]
                v.wait_ge(s_pe.h, pe_base + sc + 1)
                v.tensor_copy(ct[:, sc, r * 1024:(r + 1) * 1024],
                              bank[:]).then_inc(s_ev.h, 1)
        s_ev.inc(NST)
        ctx.barrier(f"c_ct_{r}")

    pe_base = s_pe.v
    ev_base = s_ev.v
    sc_sem = ctx.sem("sc")
    sc_base = sc_sem.v
    stt_base = s_ev.v
    osc = [ctx.t["sq_scratch"]]

    @blk.tensor
    def _(t):
        for oc in range(NKC):
            for fc in range(2):
                bank = pb[(oc % 2) * 2 + fc]
                gi = oc * 2 + fc
                if gi >= 4:
                    t.wait_ge(s_ev.h, ev_base + gi - 3)
                last = None
                for sc in range(NST):
                    last = t.matmul(bank[:], ct[:, sc, oc * 128:(oc + 1) * 128],
                                    state_fm[:, sc, fc * 512:(fc + 1) * 512],
                                    start=(sc == 0), stop=(sc == NST - 1))
                last.then_inc(s_pe.h, 1)
    s_pe.inc(NKC * 2)

    @blk.scalar
    def _(sc_e):
        for oc in range(NKC):
            for fc in range(2):
                bank = pb[(oc % 2) * 2 + fc]
                gi = oc * 2 + fc
                sc_e.wait_ge(s_pe.h, pe_base + gi + 1)
                if gi >= 2:
                    sc_e.wait_ge(s_ev.h, stt_base + gi - 1)
                sc_e.copy(osc[0][:, fc * 512:(fc + 1) * 512], bank[:]).then_inc(
                    sc_sem.h, 1)
    sc_sem.inc(NKC * 2)

    @blk.vector
    def _(v):
        for oc in range(NKC):
            for fc in range(2):
                gi = oc * 2 + fc
                v.wait_ge(sc_sem.h, sc_base + gi + 1)
                v.scalar_tensor_tensor(
                    o_fm[:, oc, fc * 512:(fc + 1) * 512],
                    xn_fm[:, oc, fc * 512:(fc + 1) * 512],
                    dpack[:, oc:oc + 1],
                    osc[0][:, fc * 512:(fc + 1) * 512],
                    op0=ALU.mult, op1=ALU.add).then_inc(s_ev.h, 1)
    s_ev.inc(NKC * 2)

    ctx.barrier("c_out")

    emit_out_epilogue(ctx, lambda fc, tt: o_fm[:, fc, tt * 128:(tt + 1) * 128], pT)

    es_c.close()
    es_ph.close()


def emit_bcast_rows(ctx, row_ap, width, dst_ap, wait_fn=None):
    blk = ctx.block
    s_b = ctx.sem("bnc")
    scr = ctx.bounce_dram
    t0 = s_b.inc(16)
    t1 = s_b.inc(16)

    @blk.scalar
    def _(s):
        if wait_fn is not None:
            wait_fn(s)
        s.dma_start(out=scr[0:1, 0:width], in_=row_ap).then_inc(s_b.h, 16)
        s.wait_ge(s_b.h, t0)
        s.dma_start(out=dst_ap,
                    in_=scr[0:1, 0:width].broadcast_to([128, width])
                    ).then_inc(s_b.h, 16)
    return t1


def emit_partition_sum(ctx, col_ap, pair_bf, ones_bf, psum2, row2, out_ap,
                       wait_fn=None):
    blk = ctx.block
    s_dv = ctx.sem("dv")
    s_pe = ctx.sem("pe")
    s_ev = ctx.sem("ev")
    d0 = s_dv.v
    d2 = s_dv.inc(2)

    @blk.vector
    def _(v):
        if wait_fn is not None:
            wait_fn(v)
        v.wait_ge(s_dv.h, d0)
        v.tensor_copy(pair_bf[:, 0:1], col_ap).then_inc(s_dv.h, 1)
        v.wait_ge(s_dv.h, d0 + 1)
        v.tensor_sub(pair_bf[:, 1:2], col_ap, pair_bf[:, 0:1]).then_inc(s_dv.h, 1)
    p0 = s_pe.inc()

    @blk.tensor
    def _(t):
        t.wait_ge(s_dv.h, d2)
        t.matmul(psum2, ones_bf, pair_bf[:], start=True, stop=True
                 ).then_inc(s_pe.h, 1)
    e0 = s_ev.inc()

    @blk.scalar
    def _(s):
        s.wait_ge(s_pe.h, p0)
        s.copy(row2, psum2).then_inc(s_ev.h, 1)
    d1 = s_dv.inc()

    @blk.vector
    def _(v):
        v.wait_ge(s_ev.h, e0)
        v.reduce_sum(out_ap, row2, axis=AX.X).then_inc(s_dv.h, 1)
    return d1



def emit_softmax_seg(ctx, bank_ap, rows_lo, nrows, nseg, seglen, scr, ebf_dst,
                     nm, den, bias_ap=None, bias_shared=False, valid_ap=None,
                     scale=0.125, pe_wait=None, slot_w=None,
                     slot_off=None):
    blk = ctx.block
    s_dv = ctx.sem("dv")
    s_sc = ctx.sem("sc")
    if slot_w is None:
        slot_w = seglen
    if slot_off is None:
        slot_off = lambda g: 0
    d0 = s_dv.v

    @blk.vector
    def _(v, d0=d0):
        if pe_wait is not None:
            pe_wait(v)
        v.wait_ge(s_dv.h, d0)
        if bias_ap is not None:
            if bias_shared and nseg > 1:
                b3 = bias_ap.rearrange("q (o k) -> q o k", o=1).broadcast_to(
                    [nrows, nseg, seglen])
                v.tensor_tensor(scr.rearrange("q (s k) -> q s k", s=nseg),
                                bank_ap.rearrange("q (s k) -> q s k", s=nseg),
                                b3, op=ALU.add).then_inc(s_dv.h, 1)
            else:
                v.tensor_tensor(scr, bank_ap, bias_ap, op=ALU.add).then_inc(s_dv.h, 1)
        else:
            v.tensor_copy(scr, bank_ap).then_inc(s_dv.h, 1)
        v.wait_ge(s_dv.h, d0 + 1)
        v.reduce_max(nm[:, 0:nseg], scr.rearrange("q (s k) -> q s k", s=nseg),
                     axis=AX.X, negate=True).then_inc(s_dv.h, 1)
        v.wait_ge(s_dv.h, d0 + 2)
        v.tensor_scalar(nm[:, 0:nseg], nm[:, 0:nseg], scale, None,
                        op0=ALU.mult).then_inc(s_dv.h, 1)
    s_dv.inc(3)
    sc0 = s_sc.v

    @blk.scalar
    def _(s, d0=d0, sc0=sc0):
        s.wait_ge(s_dv.h, d0 + 3)
        for g in range(nseg):
            c0 = g * slot_w + slot_off(g)
            s.activation(ebf_dst[:, c0:c0 + seglen],
                         scr[:, g * seglen:(g + 1) * seglen], AF.Exp,
                         bias=nm[:, g:g + 1], scale=scale,
                         accum_out=den[:, g:g + 1]).then_inc(s_sc.h, 1)
    s_sc.inc(nseg)
    d1 = s_dv.v

    @blk.vector
    def _(v, d1=d1, sc0=sc0):
        v.wait_ge(s_sc.h, sc0 + nseg)
        v.wait_ge(s_dv.h, d1)
        v.tensor_scalar(den[:, 0:nseg], den[:, 0:nseg], 1e-9, None,
                        op0=ALU.max).then_inc(s_dv.h, 1)
        v.wait_ge(s_dv.h, d1 + 1)
        v.reciprocal(den[:, 0:nseg], den[:, 0:nseg]).then_inc(s_dv.h, 1)
        if valid_ap is not None:
            v.wait_ge(s_dv.h, d1 + 2)
            v.tensor_tensor(ebf_dst, ebf_dst, valid_ap, op=ALU.mult
                            ).then_inc(s_dv.h, 1)
        v.wait_ge(s_dv.h, d1 + 2 + (1 if valid_ap is not None else 0))
        dn3 = den[:, 0:nseg].rearrange("q (s o) -> q s o", o=1).broadcast_to(
            [nrows, nseg, slot_w])
        v.tensor_tensor(ebf_dst.rearrange("q (s k) -> q s k", s=nseg),
                        ebf_dst.rearrange("q (s k) -> q s k", s=nseg),
                        dn3, op=ALU.mult).then_inc(s_dv.h, 1)
    s_dv.inc(3 + (1 if valid_ap is not None else 0))
    return s_dv.v


def emit_allgather_pair(ctx, src_fm, d_in, d_out, dst_fm, wait_dv, W=64):
    blk = ctx.block
    s_cast = ctx.sem("cast")
    s_cc = ctx.sem("cc")
    t0 = s_cast.inc(16)
    c0 = s_cc.inc()
    t1 = s_cast.inc(32)

    @blk.gpsimd
    def _(g, t0=t0, c0=c0, wait_dv=wait_dv, W=W):
        wait_dv(g)
        g.wait_ge(s_cast.h, t0 - 16)
        g.dma_start(out=d_in.rearrange("p (c l) -> p c l", c=NKC),
                    in_=src_fm).then_inc(s_cast.h, 16)
        g.wait_ge(s_cast.h, t0)
        g.collective_compute(
            "AllGather", ALU.bypass,
            replica_groups=[[i, i + 1] for i in range(0, NCORES, 2)],
            ins=[d_in.opt()], outs=[d_out.opt()]).then_inc(s_cc.h)
        g.wait_ge(s_cc.h, c0)
        for r in range(2):
            g.wait_ge(s_cast.h, t0 + 16 * r)
            rsrc = d_out[r * 128:(r + 1) * 128, :].rearrange(
                "p (c l) -> p c l", c=NKC)
            if W == 128:
                rsrc = rsrc[:, :, r * 64:(r + 1) * 64]
            g.dma_start(out=dst_fm[:, :, r * 64:(r + 1) * 64],
                        in_=rsrc).then_inc(s_cast.h, 16)
    return t1


def emit_latproj(ctx, w_ap, col0, n_oc, rhs_fm, nlat, wbuf, pb, out_cb):
    blk = ctx.block
    s_cast = ctx.sem("cast")
    s_pe = ctx.sem("pe")
    nog = n_oc // 2
    cast_base = s_cast.v
    pe_base0 = s_pe.v
    ev_done = []

    @blk.gpsimd
    def _(g, cast_base=cast_base, pe_base0=pe_base0):
        for og in range(nog):
            g.wait_ge(s_cast.h, cast_base + 16 * og)
            if og >= 2:
                g.wait_ge(s_pe.h, pe_base0 + 2 * (og - 1))
            g.dma_start(out=wbuf[og % 2][:],
                        in_=w_ap[:, col0 + og * 256:col0 + (og + 1) * 256]
                        .rearrange("(kc p) o -> p kc o", p=128)
                        ).then_inc(s_cast.h, 16)
    s_cast.inc(16 * nog)

    pe_mm = []
    for og in range(nog):
        for j in range(2):
            oc = og * 2 + j

            @blk.tensor
            def _(t, og=og, j=j, oc=oc, cast_base=cast_base):
                t.wait_ge(s_cast.h, cast_base + 16 * (og + 1))
                if oc >= 2:
                    t.wait_ge(ctx.sem("ev").h, ev_done[oc - 2])
                bank = pb[oc % 2]
                last = None
                for kc in range(NKC):
                    last = t.matmul(bank[:, 0:nlat],
                                    wbuf[og % 2][:, kc, j * 128:(j + 1) * 128],
                                    rhs_fm[:, kc, 0:nlat],
                                    start=(kc == 0), stop=(kc == NKC - 1))
                last.then_inc(s_pe.h, 1)
            pe_mm.append(s_pe.inc())
            ev_done.append(out_cb(oc, pb[oc % 2][:, 0:nlat], pe_mm[-1]))


def emit_phase_A(ctx, hbm):
    import os as _os
    A_STOP = int(_os.environ.get("A_STOP", "99"))
    _scopes = []

    def _stk():
        s = ExitStack()
        _scopes.append(s)
        return s

    def _close_all():
        for s in reversed(_scopes):
            s.close()

    blk = ctx.block
    cpack = ctx.t["s_cpack"]
    sq = ctx.t["sq_scratch"]
    tmb = ctx.t["tm_scratch"]
    abias = ctx.t["s_abias"]
    s_pe = ctx.sem("pe")
    s_ev = ctx.sem("ev")
    s_dv = ctx.sem("dv")
    s_sc = ctx.sem("sc")
    s_ld = ctx.sem("ld")
    s_cast = ctx.sem("cast")
    s_qsp = ctx.sem("spl")
    NB = 16

    es_ph = _stk()
    ol_tm = ctx.sb("a_ol", [128, NT, 2048], BF16, es_ph)
    gk_fm = ctx.sb("a_gk", [128, NKC, 128], BF16, es_ph)
    gv_tm = ctx.sb("a_gv", [128, 2048], BF16, es_ph)
    g_tm = ctx.sb("a_gtm", [128, NT, 32], BF16, es_ph)
    nmsc = ctx.sb("a_nm", [128, 32], F32, es_ph)
    densc = ctx.sb("a_den", [128, 32], F32, es_ph)

    es_msk = _stk()
    mloc = ctx.sb("s_a_mloc", [64, 64], BF16, es_msk)
    mlat = ctx.sb("s_a_mlat", [128, 128], BF16, es_msk)
    pswap = ctx.sb("s_a_pswap", [128, 128], BF16, es_msk)
    poolq = ctx.sb("s_a_poolq", [128, 128], BF16, es_msk)
    id2 = ctx.sb("s_a_id2", [64, 128], BF16, es_msk)

    cb0 = s_cast.v

    @blk.gpsimd
    def _(g, cb0=cb0):
        g.wait_ge(s_cast.h, cb0)
        g.dma_start(out=mloc[:], in_=hbm["a_mloc"]).then_inc(s_cast.h, 16)
        g.wait_ge(s_cast.h, cb0 + 16)
        g.dma_start(out=mlat[:], in_=hbm["a_mlat"]).then_inc(s_cast.h, 16)
        g.wait_ge(s_cast.h, cb0 + 32)
        g.dma_start(out=pswap[:], in_=hbm["a_pswap"]).then_inc(s_cast.h, 16)
        g.wait_ge(s_cast.h, cb0 + 48)
        g.dma_start(out=poolq[:], in_=hbm["a_poolq"]).then_inc(s_cast.h, 16)
        g.wait_ge(s_cast.h, cb0 + 64)
        g.dma_start(out=id2[:], in_=hbm["a_id2"]).then_inc(s_cast.h, 16)
    msk_t = s_cast.inc(16 * 5)

    es_kv = _stk()
    k_fm = ctx.sb("a_k", [128, NKC, T], BF16, es_kv)
    v_tm = ctx.sb("a_v", [128, NT, 2048], BF16, es_kv)
    es_q = _stk()
    q_ro = ctx.sb("a_qro", [128, NKC, T], BF16, es_q)

    es_xn = _stk()
    xn_fm = ctx.sb("a_xn", [128, NKC, T], BF16, es_xn)
    rope = ctx.sb("s_a_rope", [128, 2048], BF16, es_xn)
    qkwb = [ctx.sb(f"a_wq{i}", [128, NKC, 128], BF16, es_xn) for i in range(2)]
    qraw = ctx.sb("a_qraw", [128, 1024], BF16, es_xn)
    vfm = qraw
    rsc = ctx.t["sq_scratch"]
    pT = [ctx.ps("a_t0", [128, 1024], BF16, es_xn),
          ctx.ps("a_t1", [128, 1024], BF16, es_xn)]
    pb = [ctx.ps(f"a_b{i}", [128, 512], F32, es_xn) for i in range(4)]
    pswp = ctx.ps("a_swp", [128, 1024], F32, es_xn)

    rt0 = s_cast.inc(16)

    @blk.gpsimd
    def _(g, msk_t=msk_t):
        g.wait_ge(s_cast.h, msk_t)
        g.dma_start(out=rope[:], in_=hbm["a_rope"]).then_inc(s_cast.h, 16)

    emit_ln_to_fm(ctx, xn_fm, pT)
    ctx.barrier("a_xn")

    cast_base = s_cast.v
    NSTEP = 48
    mm_done = []
    vtr_done = []
    cons_done = []

    def _qkv_load(st, wait_mm):
        @blk.gpsimd
        def _(g, st=st, wait_mm=wait_mm):
            g.wait_ge(s_cast.h, cast_base + 16 * st)
            if wait_mm is not None:
                g.wait_ge(s_pe.h, wait_mm)
            g.dma_start(out=qkwb[st % 2][:],
                        in_=hbm["L0_attn_w_qkv"][:, st * 128:(st + 1) * 128]
                        .rearrange("(kc p) o -> p kc o", p=128)
                        ).then_inc(s_cast.h, 16)
        s_cast.inc(16)

    _qkv_load(0, None)
    _qkv_load(1, None)

    for st in range(NSTEP):
        oc = st
        pe0 = s_pe.v

        @blk.tensor
        def _(t, st=st, oc=oc, cast_base=cast_base, pe0=pe0):
            t.wait_ge(s_cast.h, cast_base + 16 * (st + 1))
            if oc >= 2:
                sem_n, val = cons_done[oc - 2]
                t.wait_ge((s_dv if sem_n == "dv" else s_sc).h, val)
            for fc in range(2):
                bank = pb[(oc % 2) * 2 + fc]
                last = None
                for kc in range(NKC):
                    last = t.matmul(bank[:], qkwb[st % 2][:, kc, :],
                                    xn_fm[:, kc, fc * 512:(fc + 1) * 512],
                                    start=(kc == 0), stop=(kc == NKC - 1))
                last.then_inc(s_pe.h, 1)
        mm2 = s_pe.inc(2)
        mm_done.append(mm2)
        if st + 2 < NSTEP:
            _qkv_load(st + 2, mm2)

        if oc < 32:
            isq = oc < 16
            cc = oc % 16
            prev_rope = [cd for cd in cons_done if cd[0] == "dv"]
            sc_t = s_sc.inc()

            @blk.scalar
            def _(s, oc=oc, cc=cc, mm2=mm2, prev_rope=prev_rope):
                s.wait_ge(s_pe.h, mm2)
                if prev_rope:
                    s.wait_ge(s_dv.h, prev_rope[-1][1])
                s.activation(qraw[:, 0:512], pb[(oc % 2) * 2][:], AF.Identity,
                             bias=abias[:, oc:oc + 1])
                s.activation(qraw[:, 512:1024], pb[(oc % 2) * 2 + 1][:],
                             AF.Identity, bias=abias[:, oc:oc + 1]
                             ).then_inc(s_sc.h, 1)

            @blk.tensor
            def _(t, sc_t=sc_t, oc=oc, prev_rope=prev_rope):
                t.wait_ge(s_sc.h, sc_t)
                if prev_rope:
                    t.wait_ge(s_dv.h, prev_rope[-1][1])
                t.matmul(pswp[:, 0:512], pswap[:], qraw[:, 0:512],
                         start=True, stop=True)
                t.matmul(pswp[:, 512:1024], pswap[:], qraw[:, 512:1024],
                         start=True, stop=True).then_inc(s_pe.h, 1)
            pe1 = s_pe.inc()
            d0 = s_dv.v

            @blk.vector
            def _(v, pe1=pe1, d0=d0, isq=isq, cc=cc, rt0=rt0):
                v.wait_ge(s_pe.h, pe1)
                v.wait_ge(s_dv.h, d0)
                if cc == 0 and isq:
                    v.wait_ge(s_cast.h, rt0)
                v.tensor_mul(rsc[:, 0:1024], qraw[:], rope[:, 0:1024]
                             ).then_inc(s_dv.h, 1)
                v.wait_ge(s_dv.h, d0 + 1)
                v.tensor_mul(rsc[:, 1024:2048], pswp[:], rope[:, 1024:2048]
                             ).then_inc(s_dv.h, 1)
                v.wait_ge(s_dv.h, d0 + 2)
                dst = (q_ro if isq else k_fm)[:, cc, :]
                v.tensor_add(dst, rsc[:, 0:1024], rsc[:, 1024:2048]
                             ).then_inc(s_dv.h, 1)
            cons = s_dv.inc(3)
            cons_done.append(("dv", cons))
        else:
            cc = oc - 32
            prev_rope = [cd for cd in cons_done if cd[0] == "dv"]
            sc_t = s_sc.inc()

            @blk.scalar
            def _(s, oc=oc, mm2=mm2, cc=cc, prev_rope=prev_rope):
                s.wait_ge(s_pe.h, mm2)
                if cc == 0 and prev_rope:
                    s.wait_ge(s_dv.h, prev_rope[-1][1])
                if cc > 0:
                    s.wait_ge(s_pe.h, vtr_done[-1])
                s.activation(vfm[:, 0:512], pb[(oc % 2) * 2][:], AF.Identity,
                             bias=abias[:, oc:oc + 1])
                s.activation(vfm[:, 512:1024], pb[(oc % 2) * 2 + 1][:],
                             AF.Identity, bias=abias[:, oc:oc + 1]
                             ).then_inc(s_sc.h, 1)
            pe1b = s_pe.v
            ev0 = s_ev.v

            @blk.tensor
            def _(t, sc_t=sc_t, ev0=ev0, oc=oc):
                t.wait_ge(s_sc.h, sc_t)
                if oc > 32:
                    t.wait_ge(s_ev.h, ev0)
                for tt in range(NT):
                    ban = pT[tt % 2]
                    if tt >= 2:
                        t.wait_ge(s_ev.h, ev0 + tt - 1)
                    t.transpose(ban[:, 0:128], vfm[:, tt * 128:(tt + 1) * 128],
                                ident_of(ctx)).then_inc(s_pe.h, 1)
            s_pe.inc(NT)
            vtr_done.append(s_pe.v)

            @blk.vector
            def _(v, pe1b=pe1b, cc=cc):
                for tt in range(NT):
                    ban = pT[tt % 2]
                    v.wait_ge(s_pe.h, pe1b + tt + 1)
                    v.tensor_copy(v_tm[:, tt, cc * 128:(cc + 1) * 128],
                                  ban[:, 0:128]).then_inc(s_ev.h, 1)
            s_ev.inc(NT)
            cons_done.append(("sc", sc_t))

    gw = ctx.sb("a_gw", [128, NKC, 32], BF16, es_xn)
    gfm = qraw
    gt0 = s_cast.inc(16)

    @blk.gpsimd
    def _(g, gt0=gt0):
        g.wait_ge(s_cast.h, gt0 - 16)
        g.dma_start(out=gw[:],
                    in_=hbm["L0_attn_w_gate"].rearrange("(kc p) o -> p kc o", p=128)
                    ).then_inc(s_cast.h, 16)
    ctx.barrier("a_qkv")
    ctx.tap("q_ro", q_ro[:])
    ctx.tap("k_fm", k_fm[:])
    ctx.tap("v_tm", v_tm[:])
    pe0 = s_pe.v

    @blk.tensor
    def _(t, gt0=gt0, pe0=pe0):
        t.wait_ge(s_cast.h, gt0)
        for fc in range(2):
            bank = pb[fc]
            last = None
            for kc in range(NKC):
                last = t.matmul(bank[0:32, :], gw[:, kc, :],
                                xn_fm[:, kc, fc * 512:(fc + 1) * 512],
                                start=(kc == 0), stop=(kc == NKC - 1))
            last.then_inc(s_pe.h, 1)
    s_pe.inc(2)
    sc_t = s_sc.inc()

    @blk.scalar
    def _(s, pe0=pe0):
        s.wait_ge(s_pe.h, pe0 + 2)
        s.activation(gfm[0:32, 0:512], pb[0][0:32, :], AF.Sigmoid,
                     bias=abias[0:32, 192:193])
        s.activation(gfm[0:32, 512:1024], pb[1][0:32, :], AF.Sigmoid,
                     bias=abias[0:32, 192:193]).then_inc(s_sc.h, 1)
    pe1 = s_pe.v
    ev0 = s_ev.v

    @blk.tensor
    def _(t, sc_t=sc_t, ev0=ev0):
        t.wait_ge(s_sc.h, sc_t)
        for tt in range(NT):
            ban = pT[tt % 2]
            if tt >= 2:
                t.wait_ge(s_ev.h, ev0 + tt - 1)
            t.transpose(ban[:, 0:32], gfm[0:32, tt * 128:(tt + 1) * 128],
                        ident_of(ctx)[0:32, 0:32]).then_inc(s_pe.h, 1)
    s_pe.inc(NT)

    @blk.vector
    def _(v, pe1=pe1):
        for tt in range(NT):
            ban = pT[tt % 2]
            v.wait_ge(s_pe.h, pe1 + tt + 1)
            v.tensor_copy(g_tm[:, tt, :], ban[:, 0:32]).then_inc(s_ev.h, 1)
    s_ev.inc(NT)
    ctx.barrier("a_a2")
    ctx.tap("g_tm", g_tm[:])
    es_xn.close()
    if A_STOP <= 1:
        _close_all()
        return

    es_loc = _stk()
    q_fm = q_ro
    qz = [ctx.sb(f"a_qz{i}", [128, 512], BF16, es_loc) for i in range(2)]
    ebf = ctx.sb("a_ebf", [64, 8, 128], BF16, es_loc)
    etb = ctx.sb("a_etb", [128, 8, 128], BF16, es_loc)
    pS = ctx.ps("a_ps", [128, 512], F32, es_loc)
    pEf = ctx.ps("a_pef", [128, 1024], F32, es_loc)
    pO = ctx.ps("a_po", [128, 512], F32, es_loc)

    @blk.vector
    def _(v):
        v.memset(qz[0][:], 0.0)
        v.memset(qz[1][:], 0.0).then_inc(s_dv.h, 1)
        v.memset(ebf[:], 0.0).then_inc(s_dv.h, 1)
    s_dv.inc(2)

    for h in range(32):
        hp, hh = h // 2, (h % 2) * 64
        for bg in range(2):
            dq = s_dv.inc()

            @blk.vector
            def _(v, h=h, hp=hp, hh=hh, bg=bg, dq=dq):
                v.wait_ge(s_dv.h, dq - 1)
                v.tensor_copy(qz[h % 2][hh:hh + 64, :],
                              q_fm[hh:hh + 64, hp, bg * 512:(bg + 1) * 512]
                              ).then_inc(s_dv.h, 1)

            @blk.tensor
            def _(t, hp=hp, h=h, bg=bg, dq=dq):
                t.wait_ge(s_dv.h, dq)
                if h == 0 and bg == 0:
                    t.wait_ge(s_cast.h, msk_t)
                last = None
                for b8 in range(8):
                    blk_i = bg * 8 + b8
                    last = t.matmul(
                        pS[0:64, b8 * 64:(b8 + 1) * 64],
                        qz[h % 2][:, b8 * 64:(b8 + 1) * 64],
                        k_fm[:, hp, blk_i * 64:(blk_i + 1) * 64],
                        start=True, stop=True)
                last.then_inc(s_pe.h, 1)
            mmt = s_pe.inc()
            sm_t = emit_softmax_seg(
                ctx, pS[0:64, :], 0, 64, 8, 64, sq[0:64, 0:512],
                ebf[0:64, :, :].rearrange("q s k -> q (s k)"),
                nmsc[0:64, 0:8], densc[0:64, 0:8],
                bias_ap=mloc[:], bias_shared=True,
                pe_wait=(lambda v, mmt=mmt: v.wait_ge(s_pe.h, mmt)),
                slot_w=128, slot_off=lambda g: (g % 2) * 64)

            @blk.tensor
            def _(t, sm_t=sm_t):
                t.wait_ge(s_dv.h, sm_t)
                last = None
                for b8 in range(8):
                    last = t.matmul(pEf[:, b8 * 128:(b8 + 1) * 128],
                                    ebf[0:64, b8, :], id2[:],
                                    start=True, stop=True)
                last.then_inc(s_pe.h, 1)
            pt = s_pe.inc()
            se = s_sc.inc()

            @blk.scalar
            def _(s, pt=pt):
                s.wait_ge(s_pe.h, pt)
                s.copy(etb[:], pEf[:].rearrange("p (s k) -> p s k", s=8)
                       ).then_inc(s_sc.h, 1)

            @blk.tensor
            def _(t, se=se, hp=hp, h=h, bg=bg):
                t.wait_ge(s_sc.h, se)
                last = None
                for b8 in range(8):
                    blk_i = bg * 8 + b8
                    last = t.matmul(
                        pO[:, b8 * 64:(b8 + 1) * 64],
                        etb[:, b8, :],
                        v_tm[:, blk_i // 2, h * 64:(h + 1) * 64],
                        start=True, stop=True)
                last.then_inc(s_pe.h, 1)
            av = s_pe.inc()

            @blk.vector
            def _(v, av=av, h=h, bg=bg):
                v.wait_ge(s_pe.h, av)
                last = None
                for b8 in range(8):
                    blk_i = bg * 8 + b8
                    rb = (blk_i % 2) * 64
                    last = v.tensor_copy(
                        ol_tm[rb:rb + 64, blk_i // 2, h * 64:(h + 1) * 64],
                        pO[rb:rb + 64, b8 * 64:(b8 + 1) * 64])
                last.then_inc(s_dv.h, 1)
            s_dv.inc()

    ctx.barrier("a_loc")
    ctx.tap("ol_tm", ol_tm[:])
    es_loc.close()
    if A_STOP <= 2:
        _close_all()
        return

    qt = s_qsp.inc(16)

    @blk.scalar
    def _(s):
        s.dma_start(out=ctx.qspill_dram.rearrange("p (c t) -> p c t", c=NKC),
                    in_=q_ro[:]).then_inc(s_qsp.h, 16)

    @blk.sync
    def _(s, qt=qt):
        s.wait_ge(s_qsp.h, qt)
    ctx.barrier("a_qspill")
    es_q.close()

    es_lat = _stk()
    lmf_full = ctx.sb("a_lmf", [128, NKC, 128], BF16, es_lat)
    lv_tm = ctx.sb("a_lvt", [128, 2048], BF16, es_lat)
    lmf2 = ctx.sb("a_lm2", [128, NKC, 128], BF16, es_lat)
    lmf3 = ctx.sb("a_lm3", [128, NKC, 128], BF16, es_lat)
    ro_full = ctx.sb("a_rof", [128, NKC, 128], BF16, es_lat)
    gv_fm = ctx.sb("a_gvf", [128, NKC, 128], BF16, es_lat)
    lsc = ctx.sb("a_lsc", [128, 128], BF16, es_lat)
    etb2 = ctx.sb("a_etb2b", [128, 128], BF16, es_lat)
    latw = [ctx.sb(f"a_lw{i}", [128, NKC, 256], BF16, es_lat) for i in range(2)]
    pS = ctx.ps("a_ps4", [128, 512], F32, es_lat)
    pS2 = ctx.ps("a_ps4b", [128, 512], F32, es_lat)
    pE = ctx.ps("a_pe4", [128, 1024], BF16, es_lat)
    pE2 = ctx.ps("a_pe4b", [128, 1024], BF16, es_lat)
    pEf = ctx.ps("a_pef4", [128, 512], F32, es_lat)
    pO = ctx.ps("a_po4", [128, 512], F32, es_lat)
    pb2 = [ctx.ps(f"a_pb4{i}", [128, 512], F32, es_lat) for i in range(2)]

    es_lat1 = _stk()
    lmf_own = ctx.sb("a_lmo", [128, NKC, 64], BF16, es_lat1)
    lq = ctx.sb("a_lq", [128, NKC, 128], BF16, es_lat1)
    lk = ctx.sb("a_lk", [128, NKC, 128], BF16, es_lat1)
    lv_fm = ctx.sb("a_lvf", [128, NKC, 128], BF16, es_lat1)
    lo_tm = ctx.sb("a_lot", [128, 2048], BF16, es_lat1)
    lo_fm = ctx.sb("a_lof", [128, NKC, 128], BF16, es_lat1)
    ebf = ctx.sb("a_ebf2", [128, 16, 128], BF16, es_lat1)
    etb = ctx.sb("a_etb2", [128, 16, 8], BF16, es_lat1)
    lqz = [ctx.sb(f"a_lqz{i}", [128, 128], BF16, es_lat1) for i in range(2)]

    @blk.vector
    def _(v):
        v.memset(ebf[:], 0.0)
        v.memset(lqz[0][:], 0.0)
        v.memset(lqz[1][:], 0.0).then_inc(s_dv.h, 1)
    s_dv.inc()

    for hp in range(NKC):
        pe0 = s_pe.v

        @blk.tensor
        def _(t, hp=hp, pe0=pe0):
            t.wait_ge(s_dv.h, s_dv.v)
            if hp == 0:
                t.wait_ge(s_cast.h, msk_t)
            last = None
            for fc in range(2):
                bank = pS if fc == 0 else pS2
                last = t.matmul(bank[0:8, :], poolq[:, hp * 8:(hp + 1) * 8],
                                k_fm[:, hp, fc * 512:(fc + 1) * 512],
                                start=True, stop=True)
            last.then_inc(s_pe.h, 1)
        mmt = s_pe.inc()
        for fc in range(2):
            bank = pS if fc == 0 else pS2
            emit_softmax_seg(
                ctx, bank[0:8, :], 0, 8, 8, 64, sq[0:8, 0:512],
                ebf[0:8, :, :].rearrange("q s k -> q (s k)")
                [:, fc * 1024:(fc + 1) * 1024],
                nmsc[0:8, fc * 8:fc * 8 + 8], densc[0:8, fc * 8:fc * 8 + 8],
                pe_wait=(lambda v, mmt=mmt: v.wait_ge(s_pe.h, mmt)),
                slot_w=128, slot_off=lambda g: (g % 2) * 64)
        sm_t = s_dv.v

        @blk.tensor
        def _(t, sm_t=sm_t):
            t.wait_ge(s_dv.h, sm_t)
            last = None
            for b in range(NB):
                last = t.matmul(pEf[:, b * 8:(b + 1) * 8],
                                ebf[0:8, b, :], ident_of(ctx)[0:8, 0:8],
                                start=True, stop=True)
            last.then_inc(s_pe.h, 1)
        pt = s_pe.inc()
        se = s_sc.inc()

        @blk.scalar
        def _(s, pt=pt):
            s.wait_ge(s_pe.h, pt)
            s.copy(etb[:], pEf[:, 0:128].rearrange("p (s k) -> p s k", s=NB)
                   ).then_inc(s_sc.h, 1)

        @blk.tensor
        def _(t, se=se, hp=hp):
            t.wait_ge(s_sc.h, se)
            last = None
            for b in range(NB):
                last = t.matmul(pO[:, b * 8:(b + 1) * 8],
                                v_tm[:, b // 2, hp * 128:(hp + 1) * 128],
                                etb[:, b, :], start=True, stop=True)
            last.then_inc(s_pe.h, 1)
        av = s_pe.inc()

        @blk.vector
        def _(v, av=av, hp=hp):
            v.wait_ge(s_pe.h, av)
            last = None
            for b in range(NB):
                last = v.tensor_copy(lmf_own[0:64, hp, b * 4:(b + 1) * 4],
                                     pO[0:64, b * 8:b * 8 + 4])
                last = v.tensor_copy(lmf_own[64:128, hp, b * 4:(b + 1) * 4],
                                     pO[64:128, b * 8 + 4:(b + 1) * 8])
            last.then_inc(s_dv.h, 1)
        s_dv.inc()

    ctx.barrier("a_pool")
    ctx.tap("lmf_own", lmf_own[:])
    if A_STOP <= 3:
        _close_all()
        return

    ag_t = emit_allgather_pair(ctx, lmf_own[:], ctx.ag1_in, ctx.ag1_out,
                               lmf_full, lambda g: None)

    @blk.sync
    def _(s, ag_t=ag_t):
        s.wait_ge(s_cast.h, ag_t)
    ctx.barrier("a_ag1")
    ctx.tap("lmf_full", lmf_full[:])
    if A_STOP <= 4:
        _close_all()
        return

    def bias_evac_cb(dst, bcol0, oc_off=0):
        def cb(oc, bank_ap, mm_t):
            se = s_sc.inc()

            @blk.scalar
            def _(s, oc=oc, bank_ap=bank_ap, mm_t=mm_t, se=se):
                s.wait_ge(s_pe.h, mm_t)
                s.activation(dst[:, oc - oc_off, :], bank_ap, AF.Identity,
                             bias=abias[:, bcol0 + oc:bcol0 + oc + 1]
                             ).then_inc(s_sc.h, 1)
            ev = s_ev.inc()

            @blk.vector
            def _(v, se=se):
                v.wait_ge(s_sc.h, se)
                v.engine_nop().then_inc(s_ev.h, 1)
            return ev
        return cb

    def resid_cb(dst, src_resid, bcol0):
        prev_d = [None]

        def cb(oc, bank_ap, mm_t):
            se = s_sc.inc()

            @blk.scalar
            def _(s, oc=oc, bank_ap=bank_ap, mm_t=mm_t):
                s.wait_ge(s_pe.h, mm_t)
                if prev_d[0] is not None:
                    s.wait_ge(s_dv.h, prev_d[0])
                s.activation(lsc[:, 0:128], bank_ap, AF.Identity,
                             bias=abias[:, bcol0 + oc:bcol0 + oc + 1]
                             ).then_inc(s_sc.h, 1)
            d0 = s_dv.inc()
            prev_d[0] = d0

            @blk.vector
            def _(v, oc=oc, se=se):
                v.wait_ge(s_sc.h, se)
                v.tensor_add(dst[:, oc, :], src_resid[:, oc, :], lsc[:, 0:128]
                             ).then_inc(s_dv.h, 1)
            ev = s_ev.inc()

            @blk.vector
            def _(v, d0=d0):
                v.wait_ge(s_dv.h, d0)
                v.engine_nop().then_inc(s_ev.h, 1)
            return ev
        return cb

    def lmqkv_cb(oc, bank_ap, mm_t):
        if oc < 16:
            return bias_evac_cb(lq, 48)(oc, bank_ap, mm_t)
        elif oc < 32:
            return bias_evac_cb(lk, 48, 16)(oc, bank_ap, mm_t)
        return bias_evac_cb(lv_fm, 48, 32)(oc, bank_ap, mm_t)

    emit_latproj(ctx, hbm["L0_attn_w_lm_qkv"], 0, 48, lmf_full, 128,
                 latw, pb2, lmqkv_cb)
    ctx.barrier("a_lmqkv")
    ctx.tap("lq", lq[:])
    ctx.tap("lk", lk[:])
    ctx.tap("lv_fm", lv_fm[:])

    pe0 = s_pe.v
    ev0 = s_ev.v

    @blk.tensor
    def _(t, pe0=pe0, ev0=ev0):
        for c in range(NKC):
            if c >= 2:
                t.wait_ge(s_ev.h, ev0 + c - 1)
            t.transpose([pE, pE2][c % 2][:, 0:128],
                        lv_fm[:, c, :], ident_of(ctx)).then_inc(s_pe.h, 1)
    s_pe.inc(NKC)

    @blk.scalar
    def _(s, pe0=pe0):
        for c in range(NKC):
            s.wait_ge(s_pe.h, pe0 + c + 1)
            s.copy(lv_tm[:, c * 128:(c + 1) * 128],
                   [pE, pE2][c % 2][:, 0:128]).then_inc(s_ev.h, 1)
    s_ev.inc(NKC)
    ctx.barrier("a_lvt")
    ctx.tap("lv_tm", lv_tm[:])

    for h in range(32):
        hp, hh = h // 2, (h % 2) * 64
        dq = s_dv.inc()

        @blk.vector
        def _(v, hp=hp, hh=hh, h=h, dq=dq):
            v.wait_ge(s_dv.h, dq - 1)
            v.tensor_copy(lqz[h % 2][hh:hh + 64, :], lq[hh:hh + 64, hp, :]
                          ).then_inc(s_dv.h, 1)

        @blk.tensor
        def _(t, hp=hp, h=h, dq=dq):
            t.wait_ge(s_dv.h, dq)
            t.matmul(pS[:, 0:128], lqz[h % 2][:], lk[:, hp, :],
                     start=True, stop=True).then_inc(s_pe.h, 1)
        mmt = s_pe.inc()
        dbg_lat = _os.environ.get("A_DBG_LAT", "0") == "1" and h in (0, 1, 17)
        if dbg_lat:
            ctx.barrier(f"dbgs{h}")
            dd = s_dv.inc()

            @blk.vector
            def _(v, dd=dd, mmt=mmt):
                v.wait_ge(s_pe.h, mmt)
                v.tensor_copy(sq[:, 1024:1152], pS[:, 0:128]).then_inc(s_dv.h, 1)
            ctx.barrier(f"dbgs2{h}")
            ctx.tap(f"lat_s{h}", sq[:, 1024:1152])
            ctx.tap(f"lat_qz{h}", lqz[h % 2][:])
            ctx.barrier(f"dbgs3{h}")
        sm_t = emit_softmax_seg(
            ctx, pS[:, 0:128], 0, 128, 1, 128, sq[:, 0:128], etb2[:],
            nmsc[:, 0:1], densc[:, 0:1], bias_ap=mlat[:],
            pe_wait=(lambda v, mmt=mmt: v.wait_ge(s_pe.h, mmt)))
        if dbg_lat:
            ctx.barrier(f"dbge{h}")
            ctx.tap(f"lat_e{h}", etb2[:])
            ctx.barrier(f"dbge2{h}")

        @blk.tensor
        def _(t, sm_t=sm_t):
            t.wait_ge(s_dv.h, sm_t)
            t.transpose(pE[:, 0:128], etb2[:], ident_of(ctx)
                        ).then_inc(s_pe.h, 1)
        pt = s_pe.inc()
        se = s_sc.inc()

        @blk.scalar
        def _(s, pt=pt):
            s.wait_ge(s_pe.h, pt)
            s.copy(etb2[:], pE[:, 0:128]).then_inc(s_sc.h, 1)

        @blk.tensor
        def _(t, se=se, h=h):
            t.wait_ge(s_sc.h, se)
            t.matmul(pO[:, 0:64], etb2[:], lv_tm[:, h * 64:(h + 1) * 64],
                     start=True, stop=True).then_inc(s_pe.h, 1)
        av = s_pe.inc()
        if dbg_lat:
            ctx.barrier(f"dbgo{h}")
            dd = s_dv.inc()

            @blk.vector
            def _(v, dd=dd, av=av):
                v.wait_ge(s_pe.h, av)
                v.tensor_copy(sq[:, 1024:1088], pO[:, 0:64]).then_inc(s_dv.h, 1)
            ctx.barrier(f"dbgo2{h}")
            ctx.tap(f"lat_eT{h}", etb2[:])
            ctx.tap(f"lat_o{h}", sq[:, 1024:1088])
            ctx.barrier(f"dbgo3{h}")

        @blk.vector
        def _(v, av=av, h=h):
            v.wait_ge(s_pe.h, av)
            v.tensor_copy(lo_tm[:, h * 64:(h + 1) * 64], pO[:, 0:64]
                          ).then_inc(s_dv.h, 1)
        s_dv.inc()
    ctx.barrier("a_lattn")
    ctx.tap("lo_tm", lo_tm[:])

    pe0 = s_pe.v
    ev0 = s_ev.v

    @blk.tensor
    def _(t, pe0=pe0, ev0=ev0):
        for c in range(NKC):
            if c >= 2:
                t.wait_ge(s_ev.h, ev0 + c - 1)
            t.transpose([pE, pE2][c % 2][:, 0:128],
                        lo_tm[:, c * 128:(c + 1) * 128], ident_of(ctx)
                        ).then_inc(s_pe.h, 1)
    s_pe.inc(NKC)

    @blk.scalar
    def _(s, pe0=pe0):
        for c in range(NKC):
            s.wait_ge(s_pe.h, pe0 + c + 1)
            s.copy(lo_fm[:, c, :], [pE, pE2][c % 2][:, 0:128]
                   ).then_inc(s_ev.h, 1)
    s_ev.inc(NKC)
    ctx.barrier("a_lofm")

    emit_latproj(ctx, hbm["L0_attn_w_lm_out"], 0, 16, lo_fm, 128,
                 latw, pb2, resid_cb(lmf2, lmf_full, 96))
    ctx.barrier("a_lmf2")
    ctx.tap("lmf2", lmf2[:])
    es_lat1.close()

    es_lat2 = _stk()
    l2q = ctx.sb("a_l2q", [128, NKC, 128], BF16, es_lat2)
    ro_all = ctx.sb("a_roo", [128, NKC, 128], BF16, es_lat2)
    ebr = ctx.sb("a_ebr", [4, 32, 128], BF16, es_lat2)
    l2qp = ctx.sb("a_l2qp", [128, 128], BF16, es_lat2)

    @blk.vector
    def _(v):
        v.wait_ge(s_dv.h, s_dv.v)
        v.memset(ebr[:], 0.0)
        v.memset(l2qp[:], 0.0).then_inc(s_dv.h, 1)
    s_dv.inc()

    emit_latproj(ctx, hbm["L0_attn_w_l2t_q"], 0, 16, lmf2, 128,
                 latw, pb2, bias_evac_cb(l2q, 112))
    ctx.barrier("a_l2q")
    ctx.tap("l2q", l2q[:])

    for obh in (0, 64):
        for hp in range(NKC):
            dq = s_dv.inc(2)

            @blk.vector
            def _(v, hp=hp, obh=obh, dq=dq):
                v.wait_ge(s_dv.h, dq - 2)
                v.tensor_copy(l2qp[0:64, 0:64],
                              l2q[0:64, hp, obh:obh + 64]).then_inc(s_dv.h, 1)
                v.wait_ge(s_dv.h, dq - 1)
                v.tensor_copy(l2qp[64:128, 64:128],
                              l2q[64:128, hp, obh:obh + 64]).then_inc(s_dv.h, 1)

            for h2 in range(2):
                for half8 in range(2):
                    dv_now = s_dv.v

                    @blk.tensor
                    def _(t, hp=hp, h2=h2, half8=half8, dv_now=dv_now):
                        t.wait_ge(s_dv.h, dv_now)
                        last = None
                        for g in range(8):
                            bl = half8 * 8 + g
                            last = t.matmul(
                                pS[0:4, g * 64:(g + 1) * 64],
                                l2qp[:, h2 * 64 + bl * 4:h2 * 64 + (bl + 1) * 4],
                                k_fm[:, hp, bl * 64:(bl + 1) * 64],
                                start=True, stop=True)
                        last.then_inc(s_pe.h, 1)
                    mmt = s_pe.inc()
                    s0 = (h2 * 16 + half8 * 8) * 128
                    emit_softmax_seg(
                        ctx, pS[0:4, 0:512], 0, 4, 8, 64, sq[0:4, 0:512],
                        ebr[0:4, :, :].rearrange("q s k -> q (s k)")
                        [:, s0:s0 + 1024],
                        nmsc[0:4, 0:8], densc[0:4, 0:8],
                        pe_wait=(lambda v, mmt=mmt: v.wait_ge(s_pe.h, mmt)),
                        slot_w=128, slot_off=lambda g: (g % 2) * 64)
            sm_t = s_dv.v

            @blk.tensor
            def _(t, sm_t=sm_t):
                t.wait_ge(s_dv.h, sm_t)
                last = None
                for sl in range(32):
                    last = t.matmul(pEf[:, sl * 4:(sl + 1) * 4],
                                    ebr[0:4, sl, :], ident_of(ctx)[0:4, 0:4],
                                    start=True, stop=True)
                last.then_inc(s_pe.h, 1)
            pt = s_pe.inc()
            se = s_sc.inc()

            @blk.scalar
            def _(s, pt=pt):
                s.wait_ge(s_pe.h, pt)
                s.copy(etb2[:], pEf[:, 0:128]).then_inc(s_sc.h, 1)

            @blk.tensor
            def _(t, se=se, hp=hp):
                t.wait_ge(s_sc.h, se)
                last = None
                for h2 in range(2):
                    for bl in range(NB):
                        sl = h2 * 16 + bl
                        last = t.matmul(
                            pO[:, h2 * 64 + bl * 4:h2 * 64 + (bl + 1) * 4],
                            v_tm[:, bl // 2, hp * 128:(hp + 1) * 128],
                            etb2[:, sl * 4:(sl + 1) * 4],
                            start=True, stop=True)
                last.then_inc(s_pe.h, 1)
            av = s_pe.inc()

            @blk.vector
            def _(v, av=av, hp=hp, obh=obh):
                v.wait_ge(s_pe.h, av)
                v.tensor_copy(ro_all[0:64, hp, obh:obh + 64], pO[0:64, 0:64])
                v.tensor_copy(ro_all[64:128, hp, obh:obh + 64],
                              pO[64:128, 64:128]).then_inc(s_dv.h, 1)
            s_dv.inc()
    ctx.barrier("a_ra")
    ctx.tap("ro_all", ro_all[:])
    if A_STOP <= 5:
        _close_all()
        return

    ag_t = emit_allgather_pair(ctx, ro_all[:], ctx.ag2_in, ctx.ag2_out,
                               ro_full, lambda g: None, W=128)

    @blk.sync
    def _(s, ag_t=ag_t):
        s.wait_ge(s_cast.h, ag_t)
    ctx.barrier("a_ag2")
    ctx.tap("ro_full", ro_full[:])
    es_lat2.close()

    emit_latproj(ctx, hbm["L0_attn_w_l2t_out"], 0, 16, ro_full, 128,
                 latw, pb2, resid_cb(lmf3, lmf2, 128))
    ctx.barrier("a_lmf3")
    ctx.tap("lmf3", lmf3[:])

    def lmkv_cb(oc, bank_ap, mm_t):
        if oc < 16:
            return bias_evac_cb(gk_fm, 144)(oc, bank_ap, mm_t)
        return bias_evac_cb(gv_fm, 144, 16)(oc, bank_ap, mm_t)

    emit_latproj(ctx, hbm["L0_attn_w_lm_kv"], 0, 32, lmf3, 128,
                 latw, pb2, lmkv_cb)
    ctx.barrier("a_lmkv")
    ctx.tap("gk_fm", gk_fm[:])
    ctx.tap("gv_fm", gv_fm[:])

    pe0 = s_pe.v
    ev0 = s_ev.v

    @blk.tensor
    def _(t, pe0=pe0, ev0=ev0):
        for c in range(NKC):
            if c >= 2:
                t.wait_ge(s_ev.h, ev0 + c - 1)
            t.transpose([pE, pE2][c % 2][:, 0:128],
                        gv_fm[:, c, :], ident_of(ctx)).then_inc(s_pe.h, 1)
    s_pe.inc(NKC)

    @blk.scalar
    def _(s, pe0=pe0):
        for c in range(NKC):
            s.wait_ge(s_pe.h, pe0 + c + 1)
            s.copy(gv_tm[:, c * 128:(c + 1) * 128],
                   [pE, pE2][c % 2][:, 0:128]).then_inc(s_ev.h, 1)
    s_ev.inc(NKC)
    ctx.barrier("a_gkv")
    if A_STOP <= 6:
        _close_all()
        return
    es_lat.close()
    es_kv.close()

    es_late = _stk()
    og_tm = ctx.sb("a_og", [128, NT, 2048], BF16, es_late)
    es_q9 = _stk()
    gmask = ctx.sb("s_a_gmask", [128, NT, 256], BF16, es_q9)
    qre = ctx.sb("a_qre", [128, NKC, T], BF16, es_q9)
    qgz = [ctx.sb(f"a_qgz{i}", [128, 128], BF16, es_q9) for i in range(2)]
    ebf3 = ctx.sb("a_ebf3", [128, 128], BF16, es_q9)
    etb3 = ctx.sb("a_etb3", [128, 128], BF16, es_q9)
    pS = ctx.ps("a_ps9", [128, 512], F32, es_q9)
    pE = ctx.ps("a_pe9", [128, 1024], BF16, es_q9)
    pO = ctx.ps("a_po9", [128, 512], F32, es_q9)
    qre_t = s_ld.inc(16)

    @blk.sync
    def _(s):
        s.dma_start(out=qre[:],
                    in_=ctx.qspill_dram.rearrange("p (c t) -> p c t", c=NKC)
                    ).then_inc(s_ld.h, 16)
    gm_t = s_cast.inc(16)

    @blk.gpsimd
    def _(g, gm_t=gm_t):
        g.wait_ge(s_cast.h, gm_t - 16)
        g.dma_start(out=gmask[:],
                    in_=hbm["a_gmask"].rearrange("(tt p) c -> p tt c", p=128)
                    ).then_inc(s_cast.h, 16)

    dz = s_dv.inc()

    @blk.vector
    def _(v, dz=dz):
        v.wait_ge(s_dv.h, dz - 1)
        v.memset(qgz[0][:], 0.0)
        v.memset(qgz[1][:], 0.0).then_inc(s_dv.h, 1)

    for tt in range(NT):
        for h in range(32):
            hp, hh = h // 2, (h % 2) * 64
            first = (tt == 0 and h == 0)
            dq = s_dv.inc()

            @blk.vector
            def _(v, hp=hp, hh=hh, h=h, tt=tt, dq=dq, first=first):
                v.wait_ge(s_dv.h, dq - 1)
                if first:
                    v.wait_ge(s_ld.h, qre_t)
                v.tensor_copy(qgz[h % 2][hh:hh + 64, :],
                              qre[hh:hh + 64, hp, tt * 128:(tt + 1) * 128]
                              ).then_inc(s_dv.h, 1)

            @blk.tensor
            def _(t, hp=hp, h=h, tt=tt, dq=dq, first=first):
                t.wait_ge(s_dv.h, dq)
                if first:
                    t.wait_ge(s_cast.h, gm_t)
                t.matmul(pS[:, 0:128], qgz[h % 2][:], gk_fm[:, hp, :],
                         start=True, stop=True).then_inc(s_pe.h, 1)
            mmt = s_pe.inc()
            sm_t = emit_softmax_seg(
                ctx, pS[:, 0:128], 0, 128, 1, 128, sq[:, 0:128], ebf3[:],
                nmsc[:, 0:1], densc[:, 0:1],
                bias_ap=gmask[:, tt, 0:128], valid_ap=gmask[:, tt, 128:256],
                pe_wait=(lambda v, mmt=mmt: v.wait_ge(s_pe.h, mmt)))

            @blk.tensor
            def _(t, sm_t=sm_t):
                t.wait_ge(s_dv.h, sm_t)
                t.transpose(pE[:, 0:128], ebf3[:], ident_of(ctx)
                            ).then_inc(s_pe.h, 1)
            pt = s_pe.inc()
            se = s_sc.inc()

            @blk.scalar
            def _(s, pt=pt):
                s.wait_ge(s_pe.h, pt)
                s.copy(etb3[:], pE[:, 0:128]).then_inc(s_sc.h, 1)

            @blk.tensor
            def _(t, se=se, h=h):
                t.wait_ge(s_sc.h, se)
                t.matmul(pO[:, 0:64], etb3[:], gv_tm[:, h * 64:(h + 1) * 64],
                         start=True, stop=True).then_inc(s_pe.h, 1)
            av = s_pe.inc()

            @blk.vector
            def _(v, av=av, h=h, tt=tt):
                v.wait_ge(s_pe.h, av)
                v.tensor_copy(og_tm[:, tt, h * 64:(h + 1) * 64], pO[:, 0:64]
                              ).then_inc(s_dv.h, 1)
            s_dv.inc()
    ctx.barrier("a_glob")
    ctx.tap("og_tm", og_tm[:])
    if A_STOP <= 7:
        _close_all()
        return
    es_q9.close()

    c_fm = ctx.sb("a_cfm", [128, NKC, T], BF16, es_late)
    o_fm = ctx.sb("a_ofm", [128, NKC, T], BF16, es_late)
    wob = [ctx.sb(f"a_wo{i}", [128, NKC, 512], BF16, es_late) for i in range(2)]
    pE = ctx.ps("a_pec", [128, 1024], BF16, es_late)
    pb3 = [ctx.ps(f"a_pb9{i}", [128, 512], F32, es_late) for i in range(4)]

    for tt in range(NT):
        d0 = s_dv.v

        @blk.vector
        def _(v, tt=tt, d0=d0):
            v.wait_ge(s_dv.h, d0)
            v.tensor_sub(sq[:], og_tm[:, tt, :], ol_tm[:, tt, :]).then_inc(s_dv.h, 1)
            v.wait_ge(s_dv.h, d0 + 1)
            v.tensor_tensor(
                sq[:].rearrange("p (h d) -> p h d", h=32),
                sq[:].rearrange("p (h d) -> p h d", h=32),
                g_tm[:, tt, :].rearrange("p (h o) -> p h o", o=1)
                .broadcast_to([128, 32, 64]), op=ALU.mult).then_inc(s_dv.h, 1)
            v.wait_ge(s_dv.h, d0 + 2)
            v.tensor_add(tmb[:], sq[:], ol_tm[:, tt, :]).then_inc(s_dv.h, 1)
        dv_c = s_dv.inc(3)
        pe0 = s_pe.v
        ev0 = s_ev.v

        @blk.tensor
        def _(t, tt=tt, dv_c=dv_c, pe0=pe0, ev0=ev0):
            for g in range(2):
                t.wait_ge(s_dv.h, dv_c)
                t.wait_ge(s_ev.h, ev0 + g)
                last = None
                for j in range(8):
                    fc = g * 8 + j
                    last = t.transpose(pE[:, j * 128:(j + 1) * 128],
                                       tmb[:, fc * 128:(fc + 1) * 128],
                                       ident_of(ctx))
                last.then_inc(s_pe.h, 1)
        s_pe.inc(2)

        @blk.scalar
        def _(s, tt=tt, pe0=pe0):
            for g in range(2):
                s.wait_ge(s_pe.h, pe0 + g + 1)
                s.copy(c_fm[:, g * 8:(g + 1) * 8, tt * 128:(tt + 1) * 128],
                       pE[:].rearrange("p (a b) -> p a b", a=8)).then_inc(s_ev.h, 1)
        s_ev.inc(2)
        ctx.barrier(f"a_cmb_{tt}")

    cast_base = s_cast.v
    pe_base0 = s_pe.v

    @blk.gpsimd
    def _(g, cast_base=cast_base, pe_base0=pe_base0):
        for og in range(4):
            g.wait_ge(s_cast.h, cast_base + 16 * og)
            if og >= 2:
                g.wait_ge(s_pe.h, pe_base0 + 8 * (og - 1))
            g.dma_start(out=wob[og % 2][:],
                        in_=hbm["L0_attn_w_out"][:, og * 512:(og + 1) * 512]
                        .rearrange("(kc p) o -> p kc o", p=128)
                        ).then_inc(s_cast.h, 16)
    s_cast.inc(64)

    sc_evs = []
    for og in range(4):
        for j in range(4):
            oc = og * 4 + j

            @blk.tensor
            def _(t, og=og, j=j, oc=oc, cast_base=cast_base):
                t.wait_ge(s_cast.h, cast_base + 16 * (og + 1))
                if oc >= 2:
                    t.wait_ge(s_sc.h, sc_evs[oc - 2])
                for fc in range(2):
                    bank = pb3[(oc % 2) * 2 + fc]
                    last = None
                    for kc in range(NKC):
                        last = t.matmul(bank[:],
                                        wob[og % 2][:, kc, j * 128:(j + 1) * 128],
                                        c_fm[:, kc, fc * 512:(fc + 1) * 512],
                                        start=(kc == 0), stop=(kc == NKC - 1))
                    last.then_inc(s_pe.h, 1)
            mm2 = s_pe.inc(2)
            se = s_sc.inc()

            @blk.scalar
            def _(s, oc=oc, mm2=mm2):
                s.wait_ge(s_pe.h, mm2)
                s.activation(o_fm[:, oc, 0:512], pb3[(oc % 2) * 2][:],
                             AF.Identity, bias=abias[:, 176 + oc:177 + oc])
                s.activation(o_fm[:, oc, 512:1024], pb3[(oc % 2) * 2 + 1][:],
                             AF.Identity, bias=abias[:, 176 + oc:177 + oc]
                             ).then_inc(s_sc.h, 1)
            sc_evs.append(se)
    ctx.barrier("a_wout")
    ctx.tap("o_fm", o_fm[:])

    pTe = [ctx.ps("a_et0", [128, 1024], BF16, es_late),
           ctx.ps("a_et1", [128, 1024], BF16, es_late)]
    emit_out_epilogue(ctx, lambda fc, tt: o_fm[:, fc, tt * 128:(tt + 1) * 128],
                      pTe)
    es_late.close()
    es_msk.close()
    es_ph.close()



def ident_of(ctx):
    return ctx.t["s_ident"][:]


def emit_butterflies(ctx, zap, n):
    blk = ctx.block
    s_dv = ctx.sem("dv")
    h = 1
    while h < n:
        for c in range(n):
            if c & h:
                continue
            d0 = s_dv.v
            s_dv.inc(2)

            @blk.vector
            def _(v, a=zap(c), b=zap(c + h), d0=d0):
                v.wait_ge(s_dv.h, d0)
                v.tensor_add(a, a, b).then_inc(s_dv.h, 1)
                v.wait_ge(s_dv.h, d0 + 1)
                v.scalar_tensor_tensor(b, b, -2.0, a, op0=ALU.mult,
                                       op1=ALU.add).then_inc(s_dv.h, 1)
        h *= 2


def emit_fwht_step1(ctx, zap, nslices, width, pb):
    blk = ctx.block
    s_pe = ctx.sem("pe")
    s_ev = ctx.sem("ev")
    h128 = ctx.t["s_h128"]
    pe_base = s_pe.v
    ev_base = s_ev.v

    @blk.tensor
    def _(t):
        for c in range(nslices):
            bank = pb[c % 4]
            if c >= 4:
                t.wait_ge(s_ev.h, ev_base + c - 3)
            t.matmul(bank[:, 0:width], h128[:], zap(c), start=True,
                     stop=True).then_inc(s_pe.h, 1)
    s_pe.inc(nslices)

    @blk.scalar
    def _(s):
        for c in range(nslices):
            bank = pb[c % 4]
            s.wait_ge(s_pe.h, pe_base + c + 1)
            s.copy(zap(c), bank[:, 0:width]).then_inc(s_ev.h, 1)
    s_ev.inc(nslices)


def emit_absmax_row(ctx, srcs, width, mcolf, mcol, mxd, pT, row_dst):
    blk = ctx.block
    s_dv = ctx.sem("dv")
    s_pe = ctx.sem("pe")
    s_ev = ctx.sem("ev")
    s_b = ctx.sem("bnc")
    ntile = width // 128
    d0 = s_dv.v

    @blk.vector
    def _(v, d0=d0):
        for i, ap in enumerate(srcs):
            v.wait_ge(s_dv.h, d0 + i)
            v.reduce_max(mcolf[:, i * width:(i + 1) * width], ap, axis=AX.X,
                         apply_absolute_value=True).then_inc(s_dv.h, 1)
        v.wait_ge(s_dv.h, d0 + len(srcs))
        if len(srcs) == 2:
            v.tensor_max(mcolf[:, 0:width], mcolf[:, 0:width],
                         mcolf[:, width:2 * width]).then_inc(s_dv.h, 1)
        else:
            v.engine_nop().then_inc(s_dv.h, 1)
        v.wait_ge(s_dv.h, d0 + len(srcs) + 1)
        v.tensor_copy(mcol, mcolf[:, 0:width]).then_inc(s_dv.h, 1)
    s_dv.inc(len(srcs) + 2)
    tree_done = s_dv.v

    pe0 = s_pe.v
    dv0 = s_dv.v

    @blk.tensor
    def _(t, tree_done=tree_done, pe0=pe0, dv0=dv0):
        for tt in range(ntile):
            t.wait_ge(s_dv.h, tree_done)
            if tt >= 2:
                t.wait_ge(s_dv.h, dv0 + tt - 1)
            t.transpose(pT[tt % 2][:, 0:128], mcol[:, tt * 128:(tt + 1) * 128],
                        ident_of(ctx)).then_inc(s_pe.h, 1)
    s_pe.inc(ntile)

    @blk.vector
    def _(v, pe0=pe0, dv0=dv0):
        for tt in range(ntile):
            v.wait_ge(s_pe.h, pe0 + tt + 1)
            v.wait_ge(s_dv.h, dv0 + tt)
            v.reduce_max(mxd["f"][:, tt:tt + 1], pT[tt % 2][:, 0:128],
                         axis=AX.X).then_inc(s_dv.h, 1)
    s_dv.inc(ntile)
    red_done = s_dv.v
    d1 = s_dv.inc()

    @blk.vector
    def _(v, red_done=red_done):
        v.wait_ge(s_dv.h, red_done)
        v.tensor_copy(mxd["b"][:, 0:ntile], mxd["f"][:, 0:ntile]).then_inc(s_dv.h, 1)
    pe1 = s_pe.inc()

    @blk.tensor
    def _(t, d1=d1):
        t.wait_ge(s_dv.h, d1)
        t.transpose(pT[0][0:ntile, 0:128], mxd["b"][:, 0:ntile],
                    ident_of(ctx)).then_inc(s_pe.h, 1)
    ev1 = s_ev.inc()

    @blk.scalar
    def _(s, pe1=pe1):
        s.wait_ge(s_pe.h, pe1)
        s.copy(mxd["rows"][0:ntile, :], pT[0][0:ntile, 0:128]).then_inc(s_ev.h, 1)
    scr = ctx.bounce_dram
    s_bq = ctx.sem("bnq")
    bt0 = s_bq.inc(16)
    bt = s_bq.inc(16)

    @blk.gpsimd
    def _(g, ev1=ev1, bt0=bt0):
        g.wait_ge(s_ev.h, ev1)
        g.dma_start(out=scr[0:1, 2048:2048 + width].rearrange(
                        "o (a b) -> (o a) b", b=128),
                    in_=mxd["rows"][0:ntile, :]).then_inc(s_bq.h, 16)
        g.wait_ge(s_bq.h, bt0)
        g.dma_start(out=row_dst, in_=scr[0:1, 2048:2048 + width]
                    ).then_inc(s_bq.h, 16)
    return bt


def emit_quant_chunks(ctx, zap, n, width, scale_rep, sq, wait_first):
    blk = ctx.block
    s_dv = ctx.sem("dv")
    cpack = ctx.t["s_cpack"]
    d0 = s_dv.v

    @blk.vector
    def _(v, d0=d0):
        for c in range(n):
            v.wait_ge(s_dv.h, d0 + 2 * c)
            if c == 0:
                wait_first(v)
            v.tensor_mul(sq[:, 0:width], zap(c), scale_rep).then_inc(s_dv.h, 1)
            v.wait_ge(s_dv.h, d0 + 2 * c + 1)
            v.tensor_scalar(zap(c), sq[:, 0:width],
                            cpack[:, C_MAGIC:C_MAGIC + 1],
                            cpack[:, C_MAGIC:C_MAGIC + 1],
                            op0=ALU.add, op1=ALU.subtract).then_inc(s_dv.h, 1)
    s_dv.inc(2 * n)


def emit_phase_B(ctx, hbm):
    blk = ctx.block
    cpack = ctx.t["s_cpack"]
    sq = ctx.t["sq_scratch"]
    tmb = ctx.t["tm_scratch"]
    NI8 = 64
    RT2 = float(np.sqrt(2048.0))
    RT8 = float(np.sqrt(8192.0))
    NW = float(8192 * 2048)

    s_pe = ctx.sem("pe")
    s_ev = ctx.sem("ev")
    s_dv = ctx.sem("dv")
    s_sc = ctx.sem("sc")
    s_sil = ctx.sem("sil")
    s_po = ctx.sem("po")
    s_ld = ctx.sem("ld")
    s_b = ctx.sem("bnc")
    s_spl = ctx.sem("spl")

    es_ph = ExitStack()
    bstat = ctx.sb("b_stat", [128, 8], F32, es_ph)
    brow = ctx.sb("b_brow", [1, 64], F32, es_ph)
    z03 = ctx.sb("b_z03", [128, 32, T], BF16, es_ph)
    rp = ctx.sb("b_rp", [128, 2048], F32, es_ph)
    t1b = ctx.sb("b_t1", [128, 2048], BF16, es_ph)
    zsp = ctx.zspill_dram

    es_p1 = ExitStack()
    slab1 = [ctx.sb(f"b_p1s{i}", [128, 8192], F32, es_p1) for i in range(2)]
    acc = ctx.sb("b_acc", [128, 8], F32, es_p1)

    dv0 = s_dv.inc()

    @blk.vector
    def _(v):
        v.memset(brow[:], 0.0)
        v.memset(acc[:, 0:3], 0.0).then_inc(s_dv.h, 1)

    names = ("L0_ffn_w_gate", "L0_ffn_w_up", "L0_ffn_w_down")
    plan = [(m, j) for m in range(3) for j in range(16)]
    ld_base = s_ld.v
    sc_base = s_sc.v
    dv_base = s_dv.v

    @blk.sync
    def _(s):
        for i, (m, j) in enumerate(plan):
            s.wait_ge(s_ld.h, ld_base + 16 * i)
            if i >= 2:
                s.wait_ge(s_sc.h, sc_base + i - 1)
            if m < 2:
                src = hbm[names[m]][j * 512:(j + 1) * 512, :].rearrange(
                    "(a p) h -> p a h", p=128)
                dst = slab1[i % 2][:].rearrange("p (a h) -> p a h", a=4)
            else:
                src = hbm[names[m]][j * 128:(j + 1) * 128, :]
                dst = slab1[i % 2][:]
            s.dma_start(out=dst, in_=src).then_inc(s_ld.h, 16)
    s_ld.inc(16 * len(plan))

    @blk.scalar
    def _(s):
        for i, (m, j) in enumerate(plan):
            s.wait_ge(s_ld.h, ld_base + 16 * (i + 1))
            if i >= 2:
                s.wait_ge(s_dv.h, dv_base + i - 1)
            s.activation(slab1[i % 2][:], slab1[i % 2][:], AF.Abs,
                         accum_out=acc[:, 4 + i % 2:5 + i % 2]).then_inc(s_sc.h, 1)
    s_sc.inc(len(plan))

    @blk.vector
    def _(v):
        v.wait_ge(s_dv.h, dv0)
        for i, (m, j) in enumerate(plan):
            v.wait_ge(s_sc.h, sc_base + i + 1)
            v.wait_ge(s_dv.h, dv_base + i)
            v.tensor_add(acc[:, m:m + 1], acc[:, m:m + 1],
                         acc[:, 4 + i % 2:5 + i % 2]).then_inc(s_dv.h, 1)
    s_dv.inc(len(plan))
    acc_done = s_dv.v

    pair_bf = ctx.sb("b_pair", [128, 2], BF16, es_p1)
    ones_bf = ctx.sb("b_ones", [128, 1], BF16, es_p1)
    psum2 = ctx.ps("b_ps2", [1, 2], F32, es_p1)
    d_on = s_dv.inc()

    @blk.vector
    def _(v):
        v.wait_ge(s_dv.h, acc_done)
        v.memset(ones_bf[:], 1.0).then_inc(s_dv.h, 1)
    for m in range(3):
        emit_partition_sum(
            ctx, acc[:, m:m + 1], pair_bf, ones_bf[:], psum2[:],
            brow[0:1, 32:34], brow[0:1, m:m + 1],
            wait_fn=(lambda v, t=d_on: v.wait_ge(s_dv.h, t)))

    d0 = s_dv.v

    @blk.vector
    def _(v, d0=d0):
        for m in range(3):
            v.wait_ge(s_dv.h, d0 + 2 * m)
            v.tensor_scalar(brow[0:1, 8 + m:9 + m], brow[0:1, m:m + 1],
                            1.0 / NW, 1e-5, op0=ALU.mult, op1=ALU.max
                            ).then_inc(s_dv.h, 1)
            v.wait_ge(s_dv.h, d0 + 2 * m + 1)
            v.reciprocal(brow[0:1, 12 + m:13 + m], brow[0:1, 8 + m:9 + m]
                         ).then_inc(s_dv.h, 1)
    row_done = s_dv.inc(6)
    bstat_t = emit_bcast_rows(ctx, brow[0:1, 8:16], 8, bstat[:, 0:8],
                              wait_fn=lambda e: e.wait_ge(s_dv.h, row_done))

    @blk.scalar
    def _(s):
        s.wait_ge(s_b.h, bstat_t)
    es_p1.close()
    ctx.barrier("b_p1")

    es_xq = ExitStack()
    xq1 = ctx.sb("b_xq1", [128, NKC, T], BF16, es_xq)
    rep = ctx.sb("b_rep", [128, 3 * T], F32, es_xq)
    brows = ctx.sb("b_rows", [1, 4096], F32, es_xq)
    bmxf = ctx.sb("b_mxf", [128, 8], F32, es_xq)
    bmxb = ctx.sb("b_mxb", [128, 8], BF16, es_xq)
    bmxr = ctx.sb("b_mxr", [8, 128], BF16, es_xq)
    mxd = {"f": bmxf, "b": bmxb, "rows": bmxr}

    es_q1p = ExitStack()
    pT = [ctx.ps("b_q1t0", [128, 1024], BF16, es_q1p),
          ctx.ps("b_q1t1", [128, 1024], BF16, es_q1p)]
    pb = [ctx.ps(f"b_q1b{i}", [128, 512], F32, es_q1p) for i in range(4)]

    emit_ln_to_fm(ctx, xq1, pT)
    emit_fwht_step1(ctx, lambda c: xq1[:, c // 2, (c % 2) * 512:(c % 2 + 1) * 512],
                    NKC * 2, 512, pb)
    ctx.barrier("b_fw1")
    emit_butterflies(ctx, lambda c: xq1[:, c, :], NKC)
    bt1 = emit_absmax_row(ctx, [xq1[:].rearrange("p c t -> p t c")], T,
                          sq[:], tmb[:, 0:T], mxd, pT, brows[0:1, 3072:4096])
    d0 = s_dv.v

    @blk.vector
    def _(v, d0=d0, bt1=bt1):
        v.wait_ge(ctx.sem("bnq").h, bt1)
        v.wait_ge(s_dv.h, d0)
        v.tensor_scalar(brows[0:1, 1024:2048], brows[0:1, 3072:4096],
                        1e-5 * RT2, None, op0=ALU.max).then_inc(s_dv.h, 1)
        v.wait_ge(s_dv.h, d0 + 1)
        v.reciprocal(brows[0:1, 0:1024], brows[0:1, 1024:2048]).then_inc(s_dv.h, 1)
        v.wait_ge(s_dv.h, d0 + 2)
        v.tensor_scalar(brows[0:1, 0:1024], brows[0:1, 0:1024], 127.0, None,
                        op0=ALU.mult).then_inc(s_dv.h, 1)
        v.wait_ge(s_dv.h, d0 + 3)
        v.tensor_scalar(brows[0:1, 1024:2048], brows[0:1, 3072:4096],
                        bstat[0:1, 0:1], 1.0 / (127.0 * RT2),
                        op0=ALU.mult, op1=ALU.mult).then_inc(s_dv.h, 1)
        v.wait_ge(s_dv.h, d0 + 4)
        v.tensor_scalar(brows[0:1, 2048:3072], brows[0:1, 3072:4096],
                        bstat[0:1, 1:2], 1.0 / (127.0 * RT2),
                        op0=ALU.mult, op1=ALU.mult).then_inc(s_dv.h, 1)
    rd = s_dv.inc(5)
    bt2 = emit_bcast_rows(ctx, brows[0:1, 0:3072], 3072, rep[:, 0:3072],
                          wait_fn=lambda e: e.wait_ge(s_dv.h, rd))
    emit_quant_chunks(ctx, lambda c: xq1[:, c, :], NKC, T, rep[:, 0:T],
                      sq, lambda v, bt2=bt2: v.wait_ge(s_b.h, bt2))
    ctx.barrier("b_q1")
    es_q1p.close()

    es_gu = ExitStack()
    ztmp = ctx.sb("b_ztmp", [128, 2, T], BF16, es_gu)
    wslab = [ctx.sb(f"b_ws{i}", [128, 2048], F32, es_gu) for i in range(2)]
    t2b = [ctx.sb(f"b_t2{i}", [128, 2048], BF16, es_gu) for i in range(2)]
    wT = [ctx.sb(f"b_wt{i}", [128, NKC, 128], BF16, es_gu) for i in range(2)]
    gsc = ctx.sb("b_gsc", [128, T], F32, es_gu)
    usc = ctx.sb("b_usc", [128, T], F32, es_gu)
    pT = [ctx.ps("b_gut0", [128, 1024], BF16, es_gu),
          ctx.ps("b_gut1", [128, 1024], BF16, es_gu)]
    pb = [ctx.ps(f"b_gub{i}", [128, 512], F32, es_gu) for i in range(4)]

    gnames = ("L0_ffn_w_gate", "L0_ffn_w_up")
    steps = [(oc, m) for oc in range(NI8) for m in range(2)]
    ld_base = s_ld.v
    sc_tern_base = s_sc.v
    tern_dv = []
    tern_po = []
    tr_pe = []
    mm_pe = []
    bank_free = []
    z_done = {}
    spill_t = {}

    for i, (oc, m) in enumerate(steps):
        ld_t = ld_base + 16 * (i + 1)

        @blk.sync
        def _(s, i=i, oc=oc, m=m):
            s.wait_ge(s_ld.h, ld_base + 16 * i)
            if i >= 2:
                s.wait_ge(s_sc.h, sc_tern_base + i - 1)
            s.dma_start(out=wslab[i % 2][:],
                        in_=hbm[gnames[m]][oc * 128:(oc + 1) * 128, :]
                        ).then_inc(s_ld.h, 16)
        s_ld.inc(16)

        sc_t = s_sc.inc()

        @blk.scalar
        def _(s, i=i, m=m, ld_t=ld_t):
            s.wait_ge(s_ld.h, ld_t)
            if i >= 1:
                s.wait_ge(s_po.h, tern_po[i - 1])
            s.activation(rp[:], wslab[i % 2][:], AF.Identity,
                         bias=cpack[:, C_MAGIC:C_MAGIC + 1],
                         scale=bstat[:, 4 + m:5 + m]).then_inc(s_sc.h, 1)
        po_t = s_po.inc()

        @blk.gpsimd
        def _(g, i=i, sc_t=sc_t):
            g.wait_ge(s_sc.h, sc_t)
            if i >= 1:
                g.wait_ge(s_dv.h, tern_dv[i - 1])
            g.tensor_scalar(t1b[:], rp[:], cpack[:, C_MAGIC:C_MAGIC + 1], 1.0,
                            op0=ALU.subtract, op1=ALU.min).then_inc(s_po.h, 1)
        tern_po.append(po_t)
        dv_t = s_dv.inc()

        @blk.vector
        def _(v, i=i, po_t=po_t):
            v.wait_ge(s_po.h, po_t)
            if i >= 2:
                v.wait_ge(s_pe.h, tr_pe[i - 2])
            v.tensor_scalar(t2b[i % 2][:], t1b[:], -1.0, None,
                            op0=ALU.max).then_inc(s_dv.h, 1)
        tern_dv.append(dv_t)

        pe_base_i = s_pe.v
        ev_base_i = s_ev.v

        @blk.tensor
        def _(t, i=i, dv_t=dv_t, ev_base_i=ev_base_i):
            for g in range(2):
                bank = pT[g]
                t.wait_ge(s_dv.h, dv_t)
                if i >= 1:
                    t.wait_ge(s_ev.h, ev_base_i)
                last = None
                for j in range(8):
                    kc = g * 8 + j
                    last = t.transpose(bank[:, j * 128:(j + 1) * 128],
                                       t2b[i % 2][:, kc * 128:(kc + 1) * 128],
                                       ident_of(ctx))
                last.then_inc(s_pe.h, 1)
        s_pe.inc(2)
        tr_pe.append(s_pe.v)

        @blk.scalar
        def _(s, i=i, pe_base_i=pe_base_i):
            for g in range(2):
                s.wait_ge(s_pe.h, pe_base_i + g + 1)
                if i >= 2:
                    s.wait_ge(s_pe.h, mm_pe[i - 2])
                s.copy(wT[i % 2][:, g * 8:(g + 1) * 8, :],
                       pT[g][:].rearrange("p (a b) -> p a b", a=8)
                       ).then_inc(s_ev.h, 1)
        ev_t = s_ev.inc(2)

        pe_mm_base = s_pe.v

        @blk.tensor
        def _(t, i=i, m=m, ev_t=ev_t):
            t.wait_ge(s_ev.h, ev_t)
            if i >= 2:
                t.wait_ge(s_dv.h, bank_free[i - 2])
            for fc in range(2):
                bank = pb[m * 2 + fc]
                last = None
                for kc in range(NKC):
                    last = t.matmul(bank[:], wT[i % 2][:, kc, :],
                                    xq1[:, kc, fc * 512:(fc + 1) * 512],
                                    start=(kc == 0), stop=(kc == NKC - 1))
                last.then_inc(s_pe.h, 1)
        s_pe.inc(2)
        mm_pe.append(s_pe.v)

        if m == 0:
            d0 = s_dv.v

            @blk.vector
            def _(v, i=i, oc=oc, d0=d0):
                v.wait_ge(s_dv.h, d0)
                if oc >= 1:
                    v.wait_ge(s_dv.h, z_done[oc - 1])
                for fc in range(2):
                    v.wait_ge(s_pe.h, mm_pe[i] - 1 + fc)
                    v.tensor_mul(gsc[:, fc * 512:(fc + 1) * 512], pb[fc][:],
                                 rep[:, T + fc * 512:T + (fc + 1) * 512]
                                 ).then_inc(s_dv.h, 1)
            s_dv.inc(2)
            bank_free.append(s_dv.v)
            g_dv = s_dv.v
            sil_t = s_sil.inc()

            @blk.scalar
            def _(s, g_dv=g_dv):
                s.wait_ge(s_dv.h, g_dv)
                s.activation(gsc[:], gsc[:], AF.Silu).then_inc(s_sil.h, 1)
        else:
            d0 = s_dv.v
            sil_t = s_sil.v

            @blk.vector
            def _(v, i=i, oc=oc, d0=d0, sil_t=sil_t):
                v.wait_ge(s_dv.h, d0)
                for fc in range(2):
                    v.wait_ge(s_pe.h, mm_pe[i] - 1 + fc)
                    v.tensor_mul(usc[:, fc * 512:(fc + 1) * 512], pb[2 + fc][:],
                                 rep[:, 2 * T + fc * 512:2 * T + (fc + 1) * 512]
                                 ).then_inc(s_dv.h, 1)
                v.wait_ge(s_sil.h, sil_t)
                v.wait_ge(s_dv.h, d0 + 2)
                if oc >= 34:
                    v.wait_ge(s_spl.h, spill_t[oc - 2])
                dst = z03[:, oc, :] if oc < 32 else ztmp[:, oc % 2, :]
                v.tensor_mul(dst, gsc[:], usc[:]).then_inc(s_dv.h, 1)
            s_dv.inc(3)
            bank_free.append(s_dv.v)
            z_done[oc] = s_dv.v
            if oc >= 32:
                zt = s_dv.v
                spl = s_spl.inc(16)

                @blk.scalar
                def _(s, oc=oc, zt=zt):
                    s.wait_ge(s_dv.h, zt)
                    if oc >= 33:
                        s.wait_ge(s_spl.h, spill_t[oc - 1])
                    s.dma_start(out=zsp[(oc - 32) * 128:(oc - 31) * 128, :],
                                in_=ztmp[:, oc % 2, :]).then_inc(s_spl.h, 16)
                spill_t[oc] = spl

    ctx.barrier("b_gu")
    es_gu.close()
    es_xq.close()

    for half in range(2):
        es_dn = ExitStack()
        zh = ctx.sb(f"b_zh{half}", [128, 32, 512], BF16, es_dn)
        brows2 = ctx.sb(f"b_rw2{half}", [1, 2048], F32, es_dn)
        rep2 = ctx.sb(f"b_rep2{half}", [128, 1024], F32, es_dn)
        bmxf = ctx.sb(f"b_mxf{half}", [128, 8], F32, es_dn)
        bmxb = ctx.sb(f"b_mxb{half}", [128, 8], BF16, es_dn)
        bmxr = ctx.sb(f"b_mxr{half}", [8, 128], BF16, es_dn)
        mxd = {"f": bmxf, "b": bmxb, "rows": bmxr}
        wslab = [ctx.sb(f"b_dw{half}{i}", [128, 2048], F32, es_dn) for i in range(2)]
        t2b = [ctx.sb(f"b_dt2{half}{i}", [128, 2048], BF16, es_dn) for i in range(2)]
        wTd = [ctx.sb(f"b_dwt{half}0", [128, NI8, 128], BF16, es_dn)]
        o_fm = ctx.sb(f"b_ofm{half}", [128, NKC, 512], BF16, es_dn)
        pT = [ctx.ps(f"b_dnt{half}0", [128, 1024], BF16, es_dn),
              ctx.ps(f"b_dnt{half}1", [128, 1024], BF16, es_dn)]
        pb = [ctx.ps(f"b_dnb{half}{i}", [128, 512], F32, es_dn) for i in range(4)]
        ts0 = half * 512

        def zap(c, ts0=ts0, zh=zh):
            return z03[:, c, ts0:ts0 + 512] if c < 32 else zh[:, c - 32, :]

        spl_all = s_spl.v
        ld_t = s_ld.inc(16)

        ld_pre = ld_t - 16

        @blk.sync
        def _(s, ts0=ts0, zh=zh, spl_all=spl_all, ld_pre=ld_pre):
            s.wait_ge(s_spl.h, spl_all)
            s.wait_ge(s_ld.h, ld_pre)
            s.dma_start(out=zh[:],
                        in_=zsp.rearrange("(c p) t -> p c t", p=128)[:, :, ts0:ts0 + 512]
                        ).then_inc(s_ld.h, 16)

        for dec in (blk.tensor, blk.vector, blk.scalar, blk.gpsimd):
            @dec
            def _(e, ld_t=ld_t):
                e.wait_ge(s_ld.h, ld_t)

        emit_fwht_step1(ctx, zap, NI8, 512, pb)
        ctx.barrier(f"b_fw2_{half}")
        emit_butterflies(ctx, zap, NI8)
        bt1 = emit_absmax_row(
            ctx,
            [z03[:, :, ts0:ts0 + 512].rearrange("p c t -> p t c"),
             zh[:].rearrange("p c t -> p t c")],
            512, sq[:], tmb[:, 0:512], mxd, pT, brows2[0:1, 1536:2048])
        d0 = s_dv.v

        @blk.vector
        def _(v, d0=d0, brows2=brows2, bt1=bt1):
            v.wait_ge(ctx.sem("bnq").h, bt1)
            v.wait_ge(s_dv.h, d0)
            v.tensor_scalar(brows2[0:1, 1024:1536], brows2[0:1, 1536:2048],
                            1e-5 * RT8, None, op0=ALU.max).then_inc(s_dv.h, 1)
            v.wait_ge(s_dv.h, d0 + 1)
            v.reciprocal(brows2[0:1, 0:512], brows2[0:1, 1024:1536]).then_inc(s_dv.h, 1)
            v.wait_ge(s_dv.h, d0 + 2)
            v.tensor_scalar(brows2[0:1, 0:512], brows2[0:1, 0:512], 127.0, None,
                            op0=ALU.mult).then_inc(s_dv.h, 1)
            v.wait_ge(s_dv.h, d0 + 3)
            v.tensor_scalar(brows2[0:1, 512:1024], brows2[0:1, 1536:2048],
                            bstat[0:1, 2:3], 1.0 / (127.0 * RT8),
                            op0=ALU.mult, op1=ALU.mult).then_inc(s_dv.h, 1)
        rd = s_dv.inc(4)
        bt2 = emit_bcast_rows(ctx, brows2[0:1, 0:1024], 1024, rep2[:, 0:1024],
                              wait_fn=lambda e, rd=rd: e.wait_ge(s_dv.h, rd))
        emit_quant_chunks(ctx, zap, NI8, 512, rep2[:, 0:512], sq,
                          lambda v, bt2=bt2: v.wait_ge(s_b.h, bt2))
        ctx.barrier(f"b_q2_{half}")

        ld_base2 = s_ld.v
        sc_tern_base2 = s_sc.v
        dplan = [(oc, q) for oc in range(NKC) for q in range(4)]
        tern_dv2 = []
        tern_po2 = []
        tr_pe2 = []
        wt_ev2 = {}
        mm_pe2 = {}
        out_dv2 = {}
        for idx, (oc, q) in enumerate(dplan):
            ld_t = ld_base2 + 16 * (idx + 1)

            @blk.sync
            def _(s, idx=idx, oc=oc, q=q):
                s.wait_ge(s_ld.h, ld_base2 + 16 * idx)
                if idx >= 2:
                    s.wait_ge(s_sc.h, sc_tern_base2 + idx - 1)
                s.dma_start(out=wslab[idx % 2][:],
                            in_=hbm["L0_ffn_w_down"][oc * 128:(oc + 1) * 128,
                                                     q * 2048:(q + 1) * 2048]
                            ).then_inc(s_ld.h, 16)
            s_ld.inc(16)

            sc_t = s_sc.inc()

            @blk.scalar
            def _(s, idx=idx, ld_t=ld_t):
                s.wait_ge(s_ld.h, ld_t)
                if idx >= 1:
                    s.wait_ge(s_po.h, tern_po2[idx - 1])
                s.activation(rp[:], wslab[idx % 2][:], AF.Identity,
                             bias=cpack[:, C_MAGIC:C_MAGIC + 1],
                             scale=bstat[:, 6:7]).then_inc(s_sc.h, 1)
            po_t = s_po.inc()

            @blk.gpsimd
            def _(g, idx=idx, sc_t=sc_t):
                g.wait_ge(s_sc.h, sc_t)
                if idx >= 1:
                    g.wait_ge(s_dv.h, tern_dv2[idx - 1])
                g.tensor_scalar(t1b[:], rp[:], cpack[:, C_MAGIC:C_MAGIC + 1], 1.0,
                                op0=ALU.subtract, op1=ALU.min).then_inc(s_po.h, 1)
            tern_po2.append(po_t)
            dv_t = s_dv.inc()

            @blk.vector
            def _(v, idx=idx, po_t=po_t):
                v.wait_ge(s_po.h, po_t)
                if idx >= 2:
                    v.wait_ge(s_pe.h, tr_pe2[idx - 2])
                v.tensor_scalar(t2b[idx % 2][:], t1b[:], -1.0, None,
                                op0=ALU.max).then_inc(s_dv.h, 1)
            tern_dv2.append(dv_t)

            pe_base_i = s_pe.v
            ev_base_i = s_ev.v

            @blk.tensor
            def _(t, idx=idx, dv_t=dv_t, ev_base_i=ev_base_i):
                for g in range(2):
                    bank = pT[g]
                    t.wait_ge(s_dv.h, dv_t)
                    if idx >= 1:
                        t.wait_ge(s_ev.h, ev_base_i)
                    last = None
                    for j in range(8):
                        kc = g * 8 + j
                        last = t.transpose(bank[:, j * 128:(j + 1) * 128],
                                           t2b[idx % 2][:, kc * 128:(kc + 1) * 128],
                                           ident_of(ctx))
                    last.then_inc(s_pe.h, 1)
            s_pe.inc(2)
            tr_pe2.append(s_pe.v)

            @blk.scalar
            def _(s, idx=idx, oc=oc, q=q, pe_base_i=pe_base_i):
                for g in range(2):
                    s.wait_ge(s_pe.h, pe_base_i + g + 1)
                    if oc >= 1 and q == 0 and g == 0:
                        s.wait_ge(s_pe.h, mm_pe2[oc - 1])
                    s.copy(wTd[0][:, q * 16 + g * 8:q * 16 + (g + 1) * 8, :],
                           pT[g][:].rearrange("p (a b) -> p a b", a=8)
                           ).then_inc(s_ev.h, 1)
            s_ev.inc(2)
            wt_ev2[(oc, q)] = s_ev.v

            if q == 3:
                @blk.tensor
                def _(t, oc=oc):
                    t.wait_ge(s_ev.h, wt_ev2[(oc, 3)])
                    if oc >= 2:
                        t.wait_ge(s_dv.h, out_dv2[oc - 2])
                    bank = pb[oc % 2]
                    last = None
                    for kc in range(NI8):
                        last = t.matmul(bank[:], wTd[0][:, kc, :], zap(kc),
                                        start=(kc == 0), stop=(kc == NI8 - 1))
                    last.then_inc(s_pe.h, 1)
                mmt = s_pe.inc()
                mm_pe2[oc] = mmt
                d0 = s_dv.v

                @blk.vector
                def _(v, oc=oc, mmt=mmt, d0=d0):
                    v.wait_ge(s_pe.h, mmt)
                    v.wait_ge(s_dv.h, d0)
                    v.tensor_mul(o_fm[:, oc, :], pb[oc % 2][:],
                                 rep2[:, 512:1024]).then_inc(s_dv.h, 1)
                s_dv.inc(1)
                out_dv2[oc] = s_dv.v

        ctx.barrier(f"b_dn_{half}")
        emit_out_epilogue(
            ctx,
            lambda fc, tt, half=half, o_fm=o_fm:
                o_fm[:, fc, (tt - half * 4) * 128:(tt - half * 4 + 1) * 128],
            pT, tiles=range(half * 4, half * 4 + 4))
        es_dn.close()

    es_ph.close()



def emit_phase_D(ctx, hbm):
    blk = ctx.block
    cpack = ctx.t["s_cpack"]
    sq = ctx.t["sq_scratch"]
    spack = ctx.t["s_d_spack"]
    NI = 48

    es_ph = ExitStack()
    xn_fm = ctx.sb("d_xnfm", [128, NKC, T], BF16, es_ph)
    fuse = ctx.sb("d_fuse", [128, NI, 512], BF16, es_ph)
    o_fm = ctx.sb("d_ofm", [128, NKC, 512], BF16, es_ph)

    s_pe = ctx.sem("pe")
    s_ev = ctx.sem("ev")
    s_dv = ctx.sem("dv")
    s_sc = ctx.sem("sc")
    s_cast = ctx.sem("cast")

    es_ln = ExitStack()
    pT = [ctx.ps("d_pt0", [128, 1024], BF16, es_ln),
          ctx.ps("d_pt1", [128, 1024], BF16, es_ln)]
    emit_ln_to_fm(ctx, xn_fm, pT)
    es_ln.close()

    for half in range(2):
        ts0 = half * 512
        es_gu = ExitStack()
        g8 = [ctx.sb(f"d_g8{half}{i}", [128, NKC, 256], FP8, es_gu) for i in range(2)]
        u8 = [ctx.sb(f"d_u8{half}{i}", [128, NKC, 256], FP8, es_gu) for i in range(2)]
        pb = [ctx.ps(f"d_pb{half}{i}", [128, 512], F32, es_gu) for i in range(8)]
        NOG = 24

        cast_base = s_cast.v
        pe_base = s_pe.v
        sc_base = s_sc.v
        dv_base = s_dv.v

        @blk.gpsimd
        def _(g, half=half, cast_base=cast_base, pe_base=pe_base):
            for og in range(NOG):
                g.wait_ge(s_cast.h, cast_base + 32 * og)
                if False:
                    pass
                if og >= 2:
                    g.wait_ge(s_pe.h, pe_base + 4 * (og - 1))
                g.dma_start(
                    out=g8[og % 2][:],
                    in_=hbm["L1_ffn_gate_w"][:, og * 256:(og + 1) * 256]
                    .rearrange("(kc p) o -> p kc o", p=128)).then_inc(s_cast.h, 16)
                g.wait_ge(s_cast.h, cast_base + 32 * og + 16)
                g.dma_start(
                    out=u8[og % 2][:],
                    in_=hbm["L1_ffn_up_w"][:, og * 256:(og + 1) * 256]
                    .rearrange("(kc p) o -> p kc o", p=128)).then_inc(s_cast.h, 16)
        s_cast.inc(32 * NOG)

        @blk.tensor
        def _(t, half=half, cast_base=cast_base, dv_base=dv_base):
            for og in range(NOG):
                t.wait_ge(s_cast.h, cast_base + 32 * (og + 1))
                if og >= 2:
                    t.wait_ge(s_dv.h, dv_base + 2 * (og - 1))
                for which, wb in ((0, g8[og % 2]), (1, u8[og % 2])):
                    for oc2 in range(2):
                        bank = pb[(og % 2) * 4 + which * 2 + oc2]
                        last = None
                        for kc in range(NKC):
                            last = t.matmul(
                                bank[:], wb[:, kc, oc2 * 128:(oc2 + 1) * 128],
                                xn_fm[:, kc, ts0:ts0 + 512],
                                start=(kc == 0), stop=(kc == NKC - 1))
                        last.then_inc(s_pe.h, 1)
        s_pe.inc(4 * NOG)

        @blk.scalar
        def _(sc_e, half=half, pe_base=pe_base, sc_base=sc_base, dv_base=dv_base):
            for og in range(NOG):
                for oc2 in range(2):
                    oc = og * 2 + oc2
                    gbank = pb[(og % 2) * 4 + oc2]
                    ubank = pb[(og % 2) * 4 + 2 + oc2]
                    sc_e.wait_ge(s_pe.h, pe_base + 4 * og + 4)
                    if og >= 1:
                        sc_e.wait_ge(s_dv.h, dv_base + 2 * (og - 1) + oc2 + 1)
                    sg = sq[:, oc2 * 1024:oc2 * 1024 + 512]
                    su = sq[:, oc2 * 1024 + 512:oc2 * 1024 + 1024]
                    sc_e.activation(sg, gbank[:], AF.Silu,
                                    scale=spack[:, oc:oc + 1])
                    sc_e.activation(su, ubank[:], AF.Copy,
                                    scale=spack[:, 48 + oc:48 + oc + 1]
                                    ).then_inc(s_sc.h, 1)
        s_sc.inc(2 * NOG)

        @blk.vector
        def _(v, half=half, sc_base=sc_base):
            for og in range(NOG):
                for oc2 in range(2):
                    oc = og * 2 + oc2
                    v.wait_ge(s_sc.h, sc_base + 2 * og + oc2 + 1)
                    sg = sq[:, oc2 * 1024:oc2 * 1024 + 512]
                    su = sq[:, oc2 * 1024 + 512:oc2 * 1024 + 1024]
                    v.tensor_mul(fuse[:, oc, :], sg, su).then_inc(s_dv.h, 1)
        s_dv.inc(2 * NOG)

        ctx.barrier(f"d_gu_{half}")
        es_gu.close()

        es_dn = ExitStack()
        d8 = [ctx.sb(f"d_d8{half}{i}", [128, NI, 256], FP8, es_dn) for i in range(2)]
        pb = [ctx.ps(f"d_pd{half}{i}", [128, 512], F32, es_dn) for i in range(4)]
        NOGD = 8
        cast_base = s_cast.v
        pe_base = s_pe.v
        sc_base = s_sc.v

        @blk.gpsimd
        def _(g, half=half, pe_base=pe_base, cast_base=cast_base):
            for og in range(NOGD):
                g.wait_ge(s_cast.h, cast_base + 16 * og)
                if og >= 2:
                    g.wait_ge(s_pe.h, pe_base + 2 * (og - 1))
                g.dma_start(
                    out=d8[og % 2][:],
                    in_=hbm["L1_ffn_down_w"][:, og * 256:(og + 1) * 256]
                    .rearrange("(kc p) o -> p kc o", p=128)).then_inc(s_cast.h, 16)
        s_cast.inc(16 * NOGD)

        @blk.tensor
        def _(t, half=half, cast_base=cast_base, sc_base=sc_base):
            for og in range(NOGD):
                t.wait_ge(s_cast.h, cast_base + 16 * (og + 1))
                for oc2 in range(2):
                    if og >= 2:
                        t.wait_ge(s_sc.h, sc_base + 2 * (og - 2) + oc2 + 1)
                    bank = pb[(og % 2) * 2 + oc2]
                    wb = d8[og % 2]
                    last = None
                    for kc in range(NI):
                        last = t.matmul(bank[:], wb[:, kc, oc2 * 128:(oc2 + 1) * 128],
                                        fuse[:, kc, :],
                                        start=(kc == 0), stop=(kc == NI - 1))
                    last.then_inc(s_pe.h, 1)
        s_pe.inc(2 * NOGD)

        @blk.scalar
        def _(sc_e, half=half, pe_base=pe_base):
            for og in range(NOGD):
                for oc2 in range(2):
                    oc = og * 2 + oc2
                    bank = pb[(og % 2) * 2 + oc2]
                    sc_e.wait_ge(s_pe.h, pe_base + 2 * og + oc2 + 1)
                    sc_e.activation(o_fm[:, oc, :], bank[:], AF.Copy,
                                    scale=spack[:, 96 + oc:96 + oc + 1]
                                    ).then_inc(s_sc.h, 1)
        s_sc.inc(2 * NOGD)

        ctx.barrier(f"d_dn_{half}")
        es_dn.close()

        es_ep = ExitStack()
        pT = [ctx.ps(f"d_et{half}0", [128, 1024], BF16, es_ep),
              ctx.ps(f"d_et{half}1", [128, 1024], BF16, es_ep)]
        emit_out_epilogue(
            ctx, lambda fc, tt: o_fm[:, fc, (tt - half * 4) * 128:(tt - half * 4 + 1) * 128],
            pT, tiles=range(half * 4, half * 4 + 4))
        es_ep.close()

    es_ph.close()



def build_program(phases, want_input=True, want_output=True):
    nc = bass.Bass("TRN2", target_bir_lowering=False)
    hbm = {}

    def din(name, shape):
        hbm[name] = nc.dram_tensor(name, list(shape), F32, kind="ExternalInput").ap()

    din("x", (T, D))
    din("cpack", (128, 8))
    hbm["ident"] = nc.dram_tensor("ident", [128, 128], BF16, kind="ExternalInput").ap()
    y = nc.dram_tensor("y", [T, D], F32, kind="ExternalOutput").ap()

    if "C" in phases:
        din("L1_attn_B", (1024, 2048))
        din("L1_attn_C", (2048, 1024))
        din("c_dpack", (128, 16))
    if "D" in phases:
        din("L1_ffn_gate_w", (2048, 6144))
        din("L1_ffn_up_w", (2048, 6144))
        din("L1_ffn_down_w", (6144, 2048))
        din("d_spack", (128, 112))
    if "A" in phases:
        din("L0_attn_w_qkv", (2048, 6144))
        din("L0_attn_w_lm_qkv", (2048, 6144))
        din("L0_attn_w_lm_out", (2048, 2048))
        din("L0_attn_w_l2t_q", (2048, 2048))
        din("L0_attn_w_l2t_out", (2048, 2048))
        din("L0_attn_w_lm_kv", (2048, 4096))
        din("L0_attn_w_out", (2048, 2048))
        din("L0_attn_w_gate", (2048, 32))
        din("a_bias", (128, 208))
        din("a_rope", (128, 2048))
        din("a_mloc", (64, 64))
        din("a_mlat", (128, 128))
        din("a_gmask", (1024, 256))
        din("a_pswap", (128, 128))
        din("a_poolq", (128, 128))
        din("a_id2", (64, 128))
    if "B" in phases:
        din("L0_ffn_w_gate", (8192, 2048))
        din("L0_ffn_w_up", (8192, 2048))
        din("L0_ffn_w_down", (2048, 8192))
        hbm["h128"] = nc.dram_tensor("h128", [128, 128], BF16,
                                     kind="ExternalInput").ap()

    es = ExitStack()
    with nc.Block() as block:
        ctx = Ctx(nc, block, es)
        ctx.sb("x_tile", [128, 2048], F32)
        ctx.sb("s_ident", [128, 128], BF16)
        ctx.sb("s_cpack", [128, 8], F32)
        ctx.sb("sdummy", [1, 4], F32)
        ctx.sb("sq_scratch", [128, 2048], F32)
        ctx.sb("stats", [128, 8], F32)
        ctx.sb("tm_scratch", [128, 2048], BF16)
        if "C" in phases:
            ctx.sb("s_c_dpack", [128, 16], F32)
        if "D" in phases:
            ctx.sb("s_d_spack", [128, 112], F32)
        if "A" in phases:
            ctx.sb("s_abias", [128, 208], F32)
            ctx.qspill_dram = nc.dram_tensor("a_qsp", [128, 16384], BF16,
                                             kind="Internal").ap()
            ctx.ag1_in = nc.dram_tensor("a_ag1i", [128, 1024], BF16,
                                        kind="Internal").ap()
            ctx.ag1_out = nc.dram_tensor("a_ag1o", [256, 1024], BF16,
                                         kind="Internal").ap()
            ctx.ag2_in = nc.dram_tensor("a_ag2i", [128, 2048], BF16,
                                        kind="Internal").ap()
            ctx.ag2_out = nc.dram_tensor("a_ag2o", [256, 2048], BF16,
                                         kind="Internal").ap()
        if "B" in phases:
            ctx.sb("s_h128", [128, 128], BF16)
            ctx.bounce_dram = nc.dram_tensor("b_bnc", [1, 4096], F32,
                                             kind="Internal").ap()
            ctx.zspill_dram = nc.dram_tensor("b_zsp", [4096, 1024], BF16,
                                             kind="Internal").ap()

        s_ld = ctx.sem("ld")
        s_st = ctx.sem("st")

        @block.sync
        def _(s):
            s.dma_start(out=ctx.t["s_ident"][:], in_=hbm["ident"]).then_inc(s_ld.h, 16)
            s.dma_start(out=ctx.t["s_cpack"][:], in_=hbm["cpack"]).then_inc(s_ld.h, 16)
            if "C" in phases:
                s.dma_start(out=ctx.t["s_c_dpack"][:], in_=hbm["c_dpack"]).then_inc(s_ld.h, 16)
            if "D" in phases:
                s.dma_start(out=ctx.t["s_d_spack"][:], in_=hbm["d_spack"]).then_inc(s_ld.h, 16)
            if "B" in phases:
                s.dma_start(out=ctx.t["s_h128"][:], in_=hbm["h128"]).then_inc(s_ld.h, 16)
            if "A" in phases:
                s.dma_start(out=ctx.t["s_abias"][:], in_=hbm["a_bias"]).then_inc(s_ld.h, 16)
        n_pre = (2 + (1 if "C" in phases else 0) + (1 if "D" in phases else 0)
                 + (1 if "B" in phases else 0) + (1 if "A" in phases else 0))
        s_ld.inc(16 * n_pre)
        pre_target = s_ld.v

        @block.scalar
        def _(e):
            e.memzero(ctx.t["sdummy"][:])

        for dec in (block.sync, block.tensor, block.vector, block.scalar, block.gpsimd):
            @dec
            def _(e):
                e.wait_ge(s_ld.h, pre_target)

        xh = nc.dram_tensor("xh", [T, D], F32, kind="Internal").ap() \
            if len(phases) > 1 else None
        for i, ph in enumerate(phases):
            ctx.x_src = hbm["x"] if i == 0 else xh
            ctx.x_dst = y if i == len(phases) - 1 else xh
            if ph == "A":
                emit_phase_A(ctx, hbm)
            elif ph == "B":
                emit_phase_B(ctx, hbm)
            elif ph == "C":
                emit_phase_C(ctx, hbm)
            elif ph == "D":
                emit_phase_D(ctx, hbm)

        @block.sync
        def _(s):
            s.wait_ge(s_st.h, s_st.v)

    es.close()
    return nc



def _consts_pack():
    c = np.zeros((128, 8), np.float32)
    c[:, C_MAGIC] = MAGIC
    c[:, C_NMAGIC] = -MAGIC
    c[:, C_EPS] = 1e-5
    c[:, C_ONE] = 1.0
    c[:, C_NEG1E30] = -1e30
    c[:, C_127] = 127.0
    c[:, C_EPS9] = 1e-9
    return c


def _ident_bf():
    return np.eye(128).astype(ml_dtypes.bfloat16)


def _a_packs(core):
    h = core % 2
    DH, M, S = 64, 4, 64
    pos = (h * 1024 + np.arange(1024)).astype(np.float64)
    f = np.arange(128)
    inv = 1.0 / (10000.0 ** (2 * ((f % 64) // 2) / 64.0))
    ang = pos[None, :] * inv[:, None]
    cos = np.cos(ang)
    sgn = np.where(f % 2 == 0, -1.0, 1.0)
    sin = np.sin(ang) * sgn[:, None]
    rope = np.concatenate([cos, sin], 1).astype(np.float32)
    psw = np.zeros((128, 128), np.float32)
    psw[np.arange(128), np.arange(128) ^ 1] = 1.0
    NEG = -1e30
    mloc = np.where(np.tril(np.ones((64, 64), bool)), 0.0, NEG).astype(np.float32)
    li = np.arange(128)
    mlat = np.where((li[:, None] // 4) >= (li[None, :] // 4), 0.0, NEG
                    ).astype(np.float32)
    gm = np.zeros((1024, 256), np.float32)
    for tt in range(8):
        tok = h * 1024 + tt * 128 + np.arange(128)
        tb = tok // 64
        valid = tb[:, None] > (li[None, :] // 4)
        gm[tt * 128:(tt + 1) * 128, 0:128] = np.where(valid, 0.0, NEG)
        gm[tt * 128:(tt + 1) * 128, 128:256] = valid.astype(np.float32)
    id2 = np.concatenate([np.eye(64), np.eye(64)], 1).astype(np.float32)
    return dict(a_rope=rope, a_pswap=psw, a_mloc=mloc, a_mlat=mlat, a_gmask=gm,
                a_id2=id2)


def _a_bias_pack(inputs):
    b = np.zeros((128, 208), np.float32)

    def put(vec, c0):
        v = np.asarray(vec, np.float32)
        n = v.size // 128
        b[:, c0:c0 + n] = v.reshape(n, 128).T
    put(inputs["L0_attn_b_qkv"], 0)
    put(inputs["L0_attn_b_lm_qkv"], 48)
    put(inputs["L0_attn_b_lm_out"], 96)
    put(inputs["L0_attn_b_l2t_q"], 112)
    put(inputs["L0_attn_b_l2t_out"], 128)
    put(inputs["L0_attn_b_lm_kv"], 144)
    put(inputs["L0_attn_b_out"], 176)
    b[0:32, 192] = np.asarray(inputs["L0_attn_b_gate"], np.float32)
    return b


def _a_poolq_pack(inputs):
    pq = np.asarray(inputs["L0_attn_pool_q"], np.float32).reshape(4, 32, 64)
    out = np.zeros((128, 128), np.float32)
    for hp in range(16):
        for m in range(4):
            out[0:64, hp * 8 + m] = pq[m, 2 * hp, :]
            out[64:128, hp * 8 + 4 + m] = pq[m, 2 * hp + 1, :]
    return out


def _h128_bf():
    i = np.arange(128)
    pc = np.array([bin(v).count("1") for v in range(128)])
    h = 1.0 - 2.0 * (pc[i[:, None] & i[None, :]] % 2)
    return h.astype(ml_dtypes.bfloat16)


def kernel(**inputs):
    return _kernel(inputs, groups=["ABCD"])


def _kernel(inputs, groups):
    x_full = np.ascontiguousarray(inputs["L0_attn_x"], dtype=np.float32)
    B, S, _ = x_full.shape
    cpack = _consts_pack()
    ident = _ident_bf()

    core_common = []
    for c in range(NCORES):
        s, h = c // 2, c % 2
        core_common.append(dict(
            cpack=cpack, ident=ident,
        ))

    x_cur = x_full.reshape(NCORES, T, D).copy()
    for gi, phases in enumerate(groups):
        nc = build_program(phases)
        in_maps = []
        for c in range(NCORES):
            m = dict(core_common[c])
            m["x"] = x_cur[c]
            if "C" in phases:
                m["L1_attn_B"] = np.ascontiguousarray(inputs["L1_attn_B"], np.float32)
                m["L1_attn_C"] = np.ascontiguousarray(inputs["L1_attn_C"], np.float32)
                m["c_dpack"] = np.ascontiguousarray(
                    inputs["L1_attn_D"].reshape(16, 128).T, np.float32)
            if "A" in phases:
                for nm_ in ("w_qkv", "w_lm_qkv", "w_lm_out", "w_l2t_q",
                            "w_l2t_out", "w_lm_kv", "w_out", "w_gate"):
                    m["L0_attn_" + nm_] = np.ascontiguousarray(
                        inputs["L0_attn_" + nm_], np.float32)
                m.update(_a_packs(c))
                m["a_bias"] = _a_bias_pack(inputs)
                m["a_poolq"] = _a_poolq_pack(inputs)
            if "B" in phases:
                m["L0_ffn_w_gate"] = np.ascontiguousarray(inputs["L0_ffn_w_gate"], np.float32)
                m["L0_ffn_w_up"] = np.ascontiguousarray(inputs["L0_ffn_w_up"], np.float32)
                m["L0_ffn_w_down"] = np.ascontiguousarray(inputs["L0_ffn_w_down"], np.float32)
                m["h128"] = _h128_bf()
            if "D" in phases:
                m["L1_ffn_gate_w"] = np.ascontiguousarray(inputs["L1_ffn_gate_w"], np.float32)
                m["L1_ffn_up_w"] = np.ascontiguousarray(inputs["L1_ffn_up_w"], np.float32)
                m["L1_ffn_down_w"] = np.ascontiguousarray(inputs["L1_ffn_down_w"], np.float32)
                sp = np.zeros((128, 112), np.float32)
                sp[:, 0:48] = inputs["L1_ffn_gate_scale"].reshape(48, 128).T
                sp[:, 48:96] = inputs["L1_ffn_up_scale"].reshape(48, 128).T
                sp[:, 96:112] = inputs["L1_ffn_down_scale"].reshape(16, 128).T
                m["d_spack"] = sp
            in_maps.append(m)
        res = run_bass_kernel_spmd(nc, in_maps, core_ids=list(range(NCORES)))
        x_cur = np.stack([np.asarray(r["y"]) for r in res.results], 0)

    return x_cur.reshape(B, S, D).astype(np.float32)


# version 05
# speedup vs baseline: 1.9883x; 1.9883x over previous
import numpy as np
import ml_dtypes
from contextlib import ExitStack

import concourse.bass as bass
import concourse.mybir as mybir
from concourse.bass_utils import run_bass_kernel_spmd

F32 = mybir.dt.float32
BF16 = mybir.dt.bfloat16
FP8 = mybir.dt.float8e4
AF = mybir.ActivationFunctionType
ALU = mybir.AluOpType
AX = mybir.AxisListType

D = 2048
T = 1024
NT = T // 128
NKC = D // 128
MAGIC = 12582912.0

C_MAGIC, C_NMAGIC, C_EPS, C_ONE, C_NEG1E30, C_127, C_EPS9, C_ZERO = range(8)

NCORES = 8


_TAPS = [False]


class Sem:
    def __init__(self, h):
        self.h = h
        self.v = 0

    def inc(self, n=1):
        self.v += n
        return self.v


class Ctx:
    def __init__(self, nc, block, es):
        self.nc = nc
        self.block = block
        self.es = es
        self._sems = {}
        self.t = {}

    def sem(self, name) -> Sem:
        if name not in self._sems:
            h = self.es.enter_context(self.nc.semaphore(name))
            self._sems[name] = Sem(h)
        return self._sems[name]

    def sb(self, name, shape, dt, scope=None):
        h = (scope or self.es).enter_context(self.nc.sbuf_tensor(name, shape, dt))
        self.t[name] = h
        return h

    def ps(self, name, shape, dt, scope=None):
        h = (scope or self.es).enter_context(self.nc.psum_tensor(name, shape, dt))
        self.t[name] = h
        return h

    def tap(self, name, ap):
        if not _TAPS[0]:
            return
        nc, blk = self.nc, self.block
        d = nc.dram_tensor("tap_" + name, list(ap.shape), ap.dtype,
                           kind="ExternalOutput").ap()
        s_t = self.sem("tapq")
        t0 = s_t.inc(16)

        @blk.sync
        def _(s, d=d, ap=ap, t0=t0):
            s.dma_start(out=d, in_=ap).then_inc(s_t.h, 16)
            s.wait_ge(s_t.h, t0)

    def barrier(self, tag):
        bar = self.sem("bar")
        target = bar.v + 5
        blk = self.block
        dummy = self.t["sdummy"]

        @blk.sync
        def _(e):
            e.nop().then_inc(bar.h, 1)
            e.wait_ge(bar.h, target)

        @blk.tensor
        def _(e):
            e.nop().then_inc(bar.h, 1)
            e.wait_ge(bar.h, target)

        @blk.vector
        def _(e):
            e.engine_nop().then_inc(bar.h, 1)
            e.wait_ge(bar.h, target)

        @blk.scalar
        def _(e):
            e.copy(dummy[0:1, 0:1], dummy[0:1, 1:2]).then_inc(bar.h, 1)
            e.wait_ge(bar.h, target)

        @blk.gpsimd
        def _(e):
            e.engine_nop().then_inc(bar.h, 1)
            e.wait_ge(bar.h, target)

        bar.v = target



def emit_ln_tile(ctx, src_ap_f, dst_ap, sq_scratch, stats, cpack, wait_before=None,
                 done_sem=None, src_is_bf16=False):
    blk = ctx.block
    v_sem = ctx.sem("lnv")
    s_sem = ctx.sem("lns")
    vt = v_sem.inc()

    @blk.vector
    def _(v):
        if wait_before is not None:
            wait_before(v)
        v.reduce_sum(stats[:, 0:1], src_ap_f, axis=AX.X).then_inc(v_sem.h, 1)

    st = s_sem.inc()

    @blk.scalar
    def _(sc):
        if wait_before is not None:
            wait_before(sc)
        sc.activation(sq_scratch[:], src_ap_f, AF.Square,
                      accum_out=stats[:, 1:2]).then_inc(s_sem.h, 1)

    vt2 = v_sem.inc(3)

    @blk.vector
    def _(v):
        v.wait_ge(s_sem.h, st)
        v.wait_ge(v_sem.h, vt)
        v.tensor_scalar_mul(stats[:, 2:3], stats[:, 0:1], 1.0 / 2048.0).then_inc(v_sem.h, 1)
        v.wait_ge(v_sem.h, vt + 1)
        v.tensor_scalar(stats[:, 4:5], stats[:, 2:3], stats[:, 2:3], None,
                        op0=ALU.mult).then_inc(v_sem.h, 1)
        v.wait_ge(v_sem.h, vt + 2)
        v.tensor_scalar(stats[:, 5:6], stats[:, 1:2], 1.0 / 2048.0, stats[:, 4:5],
                        op0=ALU.mult, op1=ALU.subtract).then_inc(v_sem.h, 1)

    st2 = s_sem.inc()

    @blk.scalar
    def _(sc):
        sc.wait_ge(v_sem.h, vt2)
        sc.activation(stats[:, 6:7], stats[:, 5:6], AF.Sqrt,
                      bias=cpack[:, C_EPS:C_EPS + 1]).then_inc(s_sem.h, 1)

    vt3 = v_sem.inc()

    @blk.vector
    def _(v):
        v.wait_ge(s_sem.h, st2)
        v.reciprocal(stats[:, 7:8], stats[:, 6:7]).then_inc(v_sem.h, 1)
        v.wait_ge(v_sem.h, vt3)
        ts = v.tensor_scalar(dst_ap, src_ap_f, stats[:, 2:3], stats[:, 7:8],
                             op0=ALU.subtract, op1=ALU.mult)
        if done_sem is not None:
            ts.then_inc(done_sem.h, 1)
    if done_sem is not None:
        done_sem.inc()


def emit_tm_to_fm(ctx, tm_tiles, fm_dst, pT, ident, pe_sem, ev_sem, tag,
                  wait_pe=None, tile_done=None):
    blk = ctx.block
    ngroups = NT // 4
    plan = [(fc, g) for fc in range(NKC) for g in range(ngroups)]
    pe_base = pe_sem.v
    ev_base = ev_sem.v

    @blk.tensor
    def _(t):
        for i, (fc, g) in enumerate(plan):
            bank = pT[i % 2]
            if i >= 2:
                t.wait_ge(ev_sem.h, ev_base + i - 1)
            if wait_pe is not None:
                wait_pe(t, fc, g)
            last = None
            for j in range(4):
                tt = g * 4 + j
                src = tm_tiles[tt]
                last = t.transpose(bank[:, j * 128:(j + 1) * 128],
                                   src[:, fc * 128:(fc + 1) * 128], ident[:])
            last.then_inc(pe_sem.h, 1)
    pe_sem.inc(len(plan))

    @blk.scalar
    def _(sc):
        for i, (fc, g) in enumerate(plan):
            bank = pT[i % 2]
            sc.wait_ge(pe_sem.h, pe_base + i + 1)
            sc.copy(fm_dst[:, fc, g * 512:(g + 1) * 512],
                    bank[:, 0:512]).then_inc(ev_sem.h, 1)
    ev_sem.inc(len(plan))


def emit_fm_to_tm_tile(ctx, fm_src_col, tm_dst, pT, ident, pe_sem, ev_sem,
                       n_chunks=NKC, evac_engine="scalar"):
    blk = ctx.block
    ngroups = (n_chunks + 7) // 8
    pe_base = pe_sem.v
    ev_base = ev_sem.v

    @blk.tensor
    def _(t):
        for g in range(ngroups):
            bank = pT[g % 2]
            if g >= 2:
                t.wait_ge(ev_sem.h, ev_base + g - 1)
            last = None
            for j in range(min(8, n_chunks - g * 8)):
                fc = g * 8 + j
                last = t.transpose(bank[:, j * 128:(j + 1) * 128], fm_src_col(fc),
                                   ident[:])
            last.then_inc(pe_sem.h, 1)
    pe_sem.inc(ngroups)

    dec = blk.scalar if evac_engine == "scalar" else blk.vector

    @dec
    def _(e):
        for g in range(ngroups):
            bank = pT[g % 2]
            e.wait_ge(pe_sem.h, pe_base + g + 1)
            n = min(8, n_chunks - g * 8) * 128
            cp = e.copy if evac_engine == "scalar" else e.tensor_copy
            cp(tm_dst[:, g * 1024:g * 1024 + n], bank[:, 0:n]).then_inc(ev_sem.h, 1)
    ev_sem.inc(ngroups)


def emit_ln_to_fm(ctx, xn_fm, pT):
    blk = ctx.block
    x_src = ctx.x_src
    xtile = ctx.t["x_tile"]
    ident = ctx.t["s_ident"]
    cpack = ctx.t["s_cpack"]
    sq = ctx.t["sq_scratch"]
    stats = ctx.t["stats"]
    tmb = ctx.t["tm_scratch"]
    s_pe = ctx.sem("pe")
    s_ev = ctx.sem("ev")
    s_dv = ctx.sem("dv")
    s_ld = ctx.sem("ld")
    s_st = ctx.sem("st")
    for tt in range(NT):
        st_target = s_st.v
        dv_before = s_dv.v
        ld_t = s_ld.inc(16)

        @blk.sync
        def _(s, tt=tt, st_target=st_target, dv_before=dv_before):
            s.wait_ge(s_st.h, st_target)
            s.wait_ge(s_dv.h, dv_before)
            s.dma_start(out=xtile[:], in_=x_src[tt * 128:(tt + 1) * 128, :]
                        ).then_inc(s_ld.h, 16)

        def wait_x(e, ld_t=ld_t):
            e.wait_ge(s_ld.h, ld_t)

        emit_ln_tile(ctx, xtile[:], tmb[:], sq, stats, cpack,
                     done_sem=s_dv, wait_before=wait_x)
        dv_target = s_dv.v
        pe_base = s_pe.v
        ev_base = s_ev.v

        @blk.tensor
        def _(t, tt=tt, dv_target=dv_target, pe_base=pe_base, ev_base=ev_base):
            for g in range(2):
                bank = pT[g % 2]
                t.wait_ge(s_dv.h, dv_target)
                if tt > 0 or g > 0:
                    t.wait_ge(s_ev.h, max(0, ev_base + g - 1))
                last = None
                for j in range(8):
                    fc = g * 8 + j
                    last = t.transpose(bank[:, j * 128:(j + 1) * 128],
                                       tmb[:, fc * 128:(fc + 1) * 128], ident[:])
                last.then_inc(s_pe.h, 1)
        s_pe.inc(2)

        @blk.scalar
        def _(sc, tt=tt, pe_base=pe_base, ev_base=ev_base):
            for g in range(2):
                bank = pT[g % 2]
                sc.wait_ge(s_pe.h, pe_base + g + 1)
                sc.copy(xn_fm[:, g * 8:(g + 1) * 8, tt * 128:(tt + 1) * 128],
                        bank[:].rearrange("p (a b) -> p a b", a=8)
                        ).then_inc(s_ev.h, 1)
        s_ev.inc(2)
        ctx.barrier(f"lnfm_{tt}")


def emit_out_epilogue(ctx, o_fm_col, pT, tiles=None):
    blk = ctx.block
    x_src = ctx.x_src
    x_dst = ctx.x_dst
    xtile = ctx.t["x_tile"]
    ident = ctx.t["s_ident"]
    cpack = ctx.t["s_cpack"]
    sq = ctx.t["sq_scratch"]
    stats = ctx.t["stats"]
    tmb = ctx.t["tm_scratch"]
    s_pe = ctx.sem("pe")
    s_ev = ctx.sem("ev")
    s_dv = ctx.sem("dv")
    s_ld = ctx.sem("ld")
    s_st = ctx.sem("st")
    for tt in (tiles if tiles is not None else range(NT)):
        st_before = s_st.v
        ld_t = s_ld.inc(16)

        @blk.sync
        def _(s, tt=tt, st_before=st_before):
            s.wait_ge(s_st.h, st_before)
            s.dma_start(out=xtile[:], in_=x_src[tt * 128:(tt + 1) * 128, :]
                        ).then_inc(s_ld.h, 16)

        emit_fm_to_tm_tile(ctx, lambda fc, tt=tt: o_fm_col(fc, tt),
                           tmb[:], pT, ident, s_pe, s_ev)
        ctx.barrier(f"oe_t_{tt}")
        emit_ln_tile(ctx, tmb[:], sq[:], sq, stats, cpack, done_sem=s_dv)
        dv_t = s_dv.v
        dv_t2 = s_dv.inc()

        @blk.vector
        def _(v, tt=tt, dv_t=dv_t, ld_t=ld_t):
            v.wait_ge(s_dv.h, dv_t)
            v.wait_ge(s_ld.h, ld_t)
            v.tensor_add(xtile[:], xtile[:], sq[:]).then_inc(s_dv.h, 1)
        st_t = s_st.inc(16)

        @blk.scalar
        def _(s, tt=tt, dv_t2=dv_t2):
            s.wait_ge(s_dv.h, dv_t2)
            s.dma_start(out=x_dst[tt * 128:(tt + 1) * 128, :], in_=xtile[:]
                        ).then_inc(s_st.h, 16)
        ctx.barrier(f"oe_r_{tt}")



def emit_phase_C(ctx, hbm):
    nc, blk = ctx.nc, ctx.block
    es_ph = ExitStack()
    ident = ctx.t["s_ident"]
    cpack = ctx.t["s_cpack"]
    sq = ctx.t["sq_scratch"]
    stats = ctx.t["stats"]
    tmb = ctx.t["tm_scratch"]
    NST = 8

    xn_fm = ctx.sb("c_xnfm", [128, NKC, T], BF16, es_ph)
    state_fm = ctx.sb("c_state", [128, NST, T], BF16, es_ph)
    dpack = ctx.t["s_c_dpack"]

    pT = [ctx.ps("c_pt0", [128, 1024], BF16, es_ph),
          ctx.ps("c_pt1", [128, 1024], BF16, es_ph)]
    pb = [ctx.ps(f"c_pb{i}", [128, 512], F32, es_ph) for i in range(4)]

    s_pe = ctx.sem("pe")
    s_ev = ctx.sem("ev")
    s_dv = ctx.sem("dv")
    s_ld = ctx.sem("ld")
    s_cast = ctx.sem("cast")

    emit_ln_to_fm(ctx, xn_fm, pT)

    es_b = ExitStack()
    b_buf = ctx.sb("c_bbuf", [128, NST, 2048], BF16, es_b)
    bt = ctx.sb("c_bt", [128, NKC, 1024], BF16, es_b)

    cast_base = s_cast.v

    @blk.gpsimd
    def _(g, cast_base=cast_base):
        g.wait_ge(s_cast.h, cast_base)
        g.dma_start(out=b_buf[:],
                    in_=hbm["L1_attn_B"].rearrange("(sc p) d -> p sc d", p=128)
                    ).then_inc(s_cast.h, 16)
    s_cast.inc(16)

    pe_base = s_pe.v
    ev_base = s_ev.v

    @blk.tensor
    def _(t):
        t.wait_ge(s_cast.h, cast_base + 16)
        for kc in range(NKC):
            bank = pT[kc % 2]
            if kc >= 2:
                t.wait_ge(s_ev.h, ev_base + kc - 1)
            last = None
            for sc in range(NST):
                last = t.transpose(bank[:, sc * 128:(sc + 1) * 128],
                                   b_buf[:, sc, kc * 128:(kc + 1) * 128], ident[:])
            last.then_inc(s_pe.h, 1)
    s_pe.inc(NKC)

    @blk.vector
    def _(v):
        for kc in range(NKC):
            bank = pT[kc % 2]
            v.wait_ge(s_pe.h, pe_base + kc + 1)
            v.tensor_copy(bt[:, kc, :], bank[:]).then_inc(s_ev.h, 1)
    s_ev.inc(NKC)

    ctx.barrier("c_bt")

    pe_base = s_pe.v
    ev_base = s_ev.v

    @blk.tensor
    def _(t):
        for sc in range(NST):
            for fc in range(2):
                bank = pb[(sc % 2) * 2 + fc]
                gi = sc * 2 + fc
                if gi >= 4:
                    t.wait_ge(s_ev.h, ev_base + gi - 3)
                last = None
                for kc in range(NKC):
                    last = t.matmul(bank[:], bt[:, kc, sc * 128:(sc + 1) * 128],
                                    xn_fm[:, kc, fc * 512:(fc + 1) * 512],
                                    start=(kc == 0), stop=(kc == NKC - 1))
                last.then_inc(s_pe.h, 1)
    s_pe.inc(NST * 2)

    @blk.scalar
    def _(sc_e):
        for sc in range(NST):
            for fc in range(2):
                bank = pb[(sc % 2) * 2 + fc]
                gi = sc * 2 + fc
                sc_e.wait_ge(s_pe.h, pe_base + gi + 1)
                sc_e.activation(state_fm[:, sc, fc * 512:(fc + 1) * 512],
                                bank[:], AF.Tanh).then_inc(s_ev.h, 1)
    s_ev.inc(NST * 2)

    ctx.barrier("c_state")
    es_b.close()

    es_c = ExitStack()
    c_buf = ctx.sb("c_cbuf", [128, 8, 1024], BF16, es_c)
    ct = ctx.sb("c_ct", [128, NST, 2048], BF16, es_c)
    o_fm = ctx.sb("c_ofm", [128, NKC, T], BF16, es_c)

    for r in range(2):
        cast_base = s_cast.v

        @blk.gpsimd
        def _(g, r=r, cast_base=cast_base):
            g.wait_ge(s_cast.h, cast_base)
            g.dma_start(out=c_buf[:],
                        in_=hbm["L1_attn_C"][r * 1024:(r + 1) * 1024, :]
                        .rearrange("(oc p) s -> p oc s", p=128)
                        ).then_inc(s_cast.h, 16)
        s_cast.inc(16)

        pe_base = s_pe.v
        ev_base = s_ev.v

        @blk.tensor
        def _(t, r=r, cast_base=cast_base, pe_base=pe_base, ev_base=ev_base):
            t.wait_ge(s_cast.h, cast_base + 16)
            for sc in range(NST):
                bank = pT[sc % 2]
                if sc >= 2:
                    t.wait_ge(s_ev.h, ev_base + sc - 1)
                last = None
                for j in range(8):
                    last = t.transpose(bank[:, j * 128:(j + 1) * 128],
                                       c_buf[:, j, sc * 128:(sc + 1) * 128], ident[:])
                last.then_inc(s_pe.h, 1)
        s_pe.inc(NST)

        @blk.vector
        def _(v, r=r, pe_base=pe_base):
            for sc in range(NST):
                bank = pT[sc % 2]
                v.wait_ge(s_pe.h, pe_base + sc + 1)
                v.tensor_copy(ct[:, sc, r * 1024:(r + 1) * 1024],
                              bank[:]).then_inc(s_ev.h, 1)
        s_ev.inc(NST)
        ctx.barrier(f"c_ct_{r}")

    pe_base = s_pe.v
    ev_base = s_ev.v
    sc_sem = ctx.sem("sc")
    sc_base = sc_sem.v
    stt_base = s_ev.v
    osc = [ctx.t["sq_scratch"]]

    @blk.tensor
    def _(t):
        for oc in range(NKC):
            for fc in range(2):
                bank = pb[(oc % 2) * 2 + fc]
                gi = oc * 2 + fc
                if gi >= 4:
                    t.wait_ge(s_ev.h, ev_base + gi - 3)
                last = None
                for sc in range(NST):
                    last = t.matmul(bank[:], ct[:, sc, oc * 128:(oc + 1) * 128],
                                    state_fm[:, sc, fc * 512:(fc + 1) * 512],
                                    start=(sc == 0), stop=(sc == NST - 1))
                last.then_inc(s_pe.h, 1)
    s_pe.inc(NKC * 2)

    @blk.scalar
    def _(sc_e):
        for oc in range(NKC):
            for fc in range(2):
                bank = pb[(oc % 2) * 2 + fc]
                gi = oc * 2 + fc
                sc_e.wait_ge(s_pe.h, pe_base + gi + 1)
                if gi >= 2:
                    sc_e.wait_ge(s_ev.h, stt_base + gi - 1)
                sc_e.copy(osc[0][:, fc * 512:(fc + 1) * 512], bank[:]).then_inc(
                    sc_sem.h, 1)
    sc_sem.inc(NKC * 2)

    @blk.vector
    def _(v):
        for oc in range(NKC):
            for fc in range(2):
                gi = oc * 2 + fc
                v.wait_ge(sc_sem.h, sc_base + gi + 1)
                v.scalar_tensor_tensor(
                    o_fm[:, oc, fc * 512:(fc + 1) * 512],
                    xn_fm[:, oc, fc * 512:(fc + 1) * 512],
                    dpack[:, oc:oc + 1],
                    osc[0][:, fc * 512:(fc + 1) * 512],
                    op0=ALU.mult, op1=ALU.add).then_inc(s_ev.h, 1)
    s_ev.inc(NKC * 2)

    ctx.barrier("c_out")

    emit_out_epilogue(ctx, lambda fc, tt: o_fm[:, fc, tt * 128:(tt + 1) * 128], pT)

    es_c.close()
    es_ph.close()


def emit_bcast_rows(ctx, row_ap, width, dst_ap, wait_fn=None):
    blk = ctx.block
    s_b = ctx.sem("bnc")
    scr = ctx.bounce_dram
    t0 = s_b.inc(16)
    t1 = s_b.inc(16)

    @blk.scalar
    def _(s):
        if wait_fn is not None:
            wait_fn(s)
        s.dma_start(out=scr[0:1, 0:width], in_=row_ap).then_inc(s_b.h, 16)
        s.wait_ge(s_b.h, t0)
        s.dma_start(out=dst_ap,
                    in_=scr[0:1, 0:width].broadcast_to([128, width])
                    ).then_inc(s_b.h, 16)
    return t1


def emit_partition_sum(ctx, col_ap, pair_bf, ones_bf, psum2, row2, out_ap,
                       wait_fn=None):
    blk = ctx.block
    s_dv = ctx.sem("dv")
    s_pe = ctx.sem("pe")
    s_ev = ctx.sem("ev")
    d0 = s_dv.v
    d2 = s_dv.inc(2)

    @blk.vector
    def _(v):
        if wait_fn is not None:
            wait_fn(v)
        v.wait_ge(s_dv.h, d0)
        v.tensor_copy(pair_bf[:, 0:1], col_ap).then_inc(s_dv.h, 1)
        v.wait_ge(s_dv.h, d0 + 1)
        v.tensor_sub(pair_bf[:, 1:2], col_ap, pair_bf[:, 0:1]).then_inc(s_dv.h, 1)
    p0 = s_pe.inc()

    @blk.tensor
    def _(t):
        t.wait_ge(s_dv.h, d2)
        t.matmul(psum2, ones_bf, pair_bf[:], start=True, stop=True
                 ).then_inc(s_pe.h, 1)
    e0 = s_ev.inc()

    @blk.scalar
    def _(s):
        s.wait_ge(s_pe.h, p0)
        s.copy(row2, psum2).then_inc(s_ev.h, 1)
    d1 = s_dv.inc()

    @blk.vector
    def _(v):
        v.wait_ge(s_ev.h, e0)
        v.reduce_sum(out_ap, row2, axis=AX.X).then_inc(s_dv.h, 1)
    return d1



def emit_softmax_seg(ctx, bank_ap, rows_lo, nrows, nseg, seglen, scr, ebf_dst,
                     nm, den, bias_ap=None, bias_shared=False, valid_ap=None,
                     scale=0.125, pe_wait=None, slot_w=None,
                     slot_off=None):
    blk = ctx.block
    s_dv = ctx.sem("dv")
    s_sc = ctx.sem("sc")
    if slot_w is None:
        slot_w = seglen
    if slot_off is None:
        slot_off = lambda g: 0
    d0 = s_dv.v

    @blk.vector
    def _(v, d0=d0):
        if pe_wait is not None:
            pe_wait(v)
        v.wait_ge(s_dv.h, d0)
        if bias_ap is not None:
            if bias_shared and nseg > 1:
                b3 = bias_ap.rearrange("q (o k) -> q o k", o=1).broadcast_to(
                    [nrows, nseg, seglen])
                v.tensor_tensor(scr.rearrange("q (s k) -> q s k", s=nseg),
                                bank_ap.rearrange("q (s k) -> q s k", s=nseg),
                                b3, op=ALU.add).then_inc(s_dv.h, 1)
            else:
                v.tensor_tensor(scr, bank_ap, bias_ap, op=ALU.add).then_inc(s_dv.h, 1)
        else:
            v.tensor_copy(scr, bank_ap).then_inc(s_dv.h, 1)
        v.wait_ge(s_dv.h, d0 + 1)
        v.reduce_max(nm[:, 0:nseg], scr.rearrange("q (s k) -> q s k", s=nseg),
                     axis=AX.X, negate=True).then_inc(s_dv.h, 1)
        v.wait_ge(s_dv.h, d0 + 2)
        v.tensor_scalar(nm[:, 0:nseg], nm[:, 0:nseg], scale, None,
                        op0=ALU.mult).then_inc(s_dv.h, 1)
    s_dv.inc(3)
    sc0 = s_sc.v

    @blk.scalar
    def _(s, d0=d0, sc0=sc0):
        s.wait_ge(s_dv.h, d0 + 3)
        for g in range(nseg):
            c0 = g * slot_w + slot_off(g)
            s.activation(ebf_dst[:, c0:c0 + seglen],
                         scr[:, g * seglen:(g + 1) * seglen], AF.Exp,
                         bias=nm[:, g:g + 1], scale=scale,
                         accum_out=den[:, g:g + 1]).then_inc(s_sc.h, 1)
    s_sc.inc(nseg)
    d1 = s_dv.v

    @blk.vector
    def _(v, d1=d1, sc0=sc0):
        v.wait_ge(s_sc.h, sc0 + nseg)
        v.wait_ge(s_dv.h, d1)
        v.tensor_scalar(den[:, 0:nseg], den[:, 0:nseg], 1e-9, None,
                        op0=ALU.max).then_inc(s_dv.h, 1)
        v.wait_ge(s_dv.h, d1 + 1)
        v.reciprocal(den[:, 0:nseg], den[:, 0:nseg]).then_inc(s_dv.h, 1)
        if valid_ap is not None:
            v.wait_ge(s_dv.h, d1 + 2)
            v.tensor_tensor(ebf_dst, ebf_dst, valid_ap, op=ALU.mult
                            ).then_inc(s_dv.h, 1)
        v.wait_ge(s_dv.h, d1 + 2 + (1 if valid_ap is not None else 0))
        dn3 = den[:, 0:nseg].rearrange("q (s o) -> q s o", o=1).broadcast_to(
            [nrows, nseg, slot_w])
        v.tensor_tensor(ebf_dst.rearrange("q (s k) -> q s k", s=nseg),
                        ebf_dst.rearrange("q (s k) -> q s k", s=nseg),
                        dn3, op=ALU.mult).then_inc(s_dv.h, 1)
    s_dv.inc(3 + (1 if valid_ap is not None else 0))
    return s_dv.v


def emit_allgather_pair(ctx, src_fm, d_in, d_out, dst_fm, wait_dv, W=64):
    blk = ctx.block
    s_cast = ctx.sem("cast")
    s_cc = ctx.sem("cc")
    t0 = s_cast.inc(16)
    c0 = s_cc.inc()
    t1 = s_cast.inc(32)

    @blk.gpsimd
    def _(g, t0=t0, c0=c0, wait_dv=wait_dv, W=W):
        wait_dv(g)
        g.wait_ge(s_cast.h, t0 - 16)
        g.dma_start(out=d_in.rearrange("p (c l) -> p c l", c=NKC),
                    in_=src_fm).then_inc(s_cast.h, 16)
        g.wait_ge(s_cast.h, t0)
        g.collective_compute(
            "AllGather", ALU.bypass,
            replica_groups=[[i, i + 1] for i in range(0, NCORES, 2)],
            ins=[d_in.opt()], outs=[d_out.opt()]).then_inc(s_cc.h)
        g.wait_ge(s_cc.h, c0)
        for r in range(2):
            g.wait_ge(s_cast.h, t0 + 16 * r)
            rsrc = d_out[r * 128:(r + 1) * 128, :].rearrange(
                "p (c l) -> p c l", c=NKC)
            if W == 128:
                rsrc = rsrc[:, :, r * 64:(r + 1) * 64]
            g.dma_start(out=dst_fm[:, :, r * 64:(r + 1) * 64],
                        in_=rsrc).then_inc(s_cast.h, 16)
    return t1


def emit_latproj(ctx, w_ap, col0, n_oc, rhs_fm, nlat, wbuf, pb, out_cb):
    blk = ctx.block
    s_cast = ctx.sem("cast")
    s_pe = ctx.sem("pe")
    nog = n_oc // 2
    cast_base = s_cast.v
    pe_base0 = s_pe.v
    ev_done = []

    @blk.gpsimd
    def _(g, cast_base=cast_base, pe_base0=pe_base0):
        for og in range(nog):
            g.wait_ge(s_cast.h, cast_base + 16 * og)
            if og >= 2:
                g.wait_ge(s_pe.h, pe_base0 + 2 * (og - 1))
            g.dma_start(out=wbuf[og % 2][:],
                        in_=w_ap[:, col0 + og * 256:col0 + (og + 1) * 256]
                        .rearrange("(kc p) o -> p kc o", p=128)
                        ).then_inc(s_cast.h, 16)
    s_cast.inc(16 * nog)

    pe_mm = []
    for og in range(nog):
        for j in range(2):
            oc = og * 2 + j

            @blk.tensor
            def _(t, og=og, j=j, oc=oc, cast_base=cast_base):
                t.wait_ge(s_cast.h, cast_base + 16 * (og + 1))
                if oc >= 2:
                    t.wait_ge(ctx.sem("ev").h, ev_done[oc - 2])
                bank = pb[oc % 2]
                last = None
                for kc in range(NKC):
                    last = t.matmul(bank[:, 0:nlat],
                                    wbuf[og % 2][:, kc, j * 128:(j + 1) * 128],
                                    rhs_fm[:, kc, 0:nlat],
                                    start=(kc == 0), stop=(kc == NKC - 1))
                last.then_inc(s_pe.h, 1)
            pe_mm.append(s_pe.inc())
            ev_done.append(out_cb(oc, pb[oc % 2][:, 0:nlat], pe_mm[-1]))


def emit_phase_A(ctx, hbm):
    import os as _os
    A_STOP = int(_os.environ.get("A_STOP", "99"))
    _scopes = []

    def _stk():
        s = ExitStack()
        _scopes.append(s)
        return s

    def _close_all():
        if _sc_cur[0] is not None:
            _sc_cur[0].close()
            _sc_cur[0] = None
        for s in reversed(_scopes):
            s.close()

    blk = ctx.block
    cpack = ctx.t["s_cpack"]
    sq = ctx.t["sq_scratch"]
    tmb = ctx.t["tm_scratch"]
    abias = ctx.t["s_abias"]
    s_pe = ctx.sem("pe")
    s_ev = ctx.sem("ev")
    s_dv = ctx.sem("dv")
    s_sc = ctx.sem("sc")
    s_ld = ctx.sem("ld")
    s_cast = ctx.sem("cast")
    s_qsp = ctx.sem("spl")
    NB = 16

    _ns = ctx.nc.named_scope
    _sc_cur = [None]

    def _mark(name):
        if _sc_cur[0] is not None:
            _sc_cur[0].close()
            _sc_cur[0] = None
        if name is not None:
            _sc_cur[0] = ExitStack()
            _sc_cur[0].enter_context(_ns(name))

    es_ph = _stk()
    ol_tm = ctx.sb("a_ol", [128, NT, 2048], BF16, es_ph)
    gk_fm = ctx.sb("a_gk", [128, NKC, 128], BF16, es_ph)
    gv_tm = ctx.sb("a_gv", [128, 2048], BF16, es_ph)
    g_tm = ctx.sb("a_gtm", [128, NT, 32], BF16, es_ph)
    nmsc = ctx.sb("a_nm", [128, 32], F32, es_ph)
    densc = ctx.sb("a_den", [128, 32], F32, es_ph)

    es_msk = _stk()
    mloc = ctx.sb("s_a_mloc", [64, 64], BF16, es_msk)
    mlat = ctx.sb("s_a_mlat", [128, 128], BF16, es_msk)
    pswap = ctx.sb("s_a_pswap", [128, 128], BF16, es_msk)
    poolq = ctx.sb("s_a_poolq", [128, 128], BF16, es_msk)
    id2 = ctx.sb("s_a_id2", [64, 128], BF16, es_msk)

    cb0 = s_cast.v

    @blk.gpsimd
    def _(g, cb0=cb0):
        g.wait_ge(s_cast.h, cb0)
        g.dma_start(out=mloc[:], in_=hbm["a_mloc"]).then_inc(s_cast.h, 16)
        g.wait_ge(s_cast.h, cb0 + 16)
        g.dma_start(out=mlat[:], in_=hbm["a_mlat"]).then_inc(s_cast.h, 16)
        g.wait_ge(s_cast.h, cb0 + 32)
        g.dma_start(out=pswap[:], in_=hbm["a_pswap"]).then_inc(s_cast.h, 16)
        g.wait_ge(s_cast.h, cb0 + 48)
        g.dma_start(out=poolq[:], in_=hbm["a_poolq"]).then_inc(s_cast.h, 16)
        g.wait_ge(s_cast.h, cb0 + 64)
        g.dma_start(out=id2[:], in_=hbm["a_id2"]).then_inc(s_cast.h, 16)
    msk_t = s_cast.inc(16 * 5)

    _mark("A_a2")
    es_kv = _stk()
    k_fm = ctx.sb("a_k", [128, NKC, T], BF16, es_kv)
    v_tm = ctx.sb("a_v", [128, NT, 2048], BF16, es_kv)
    es_q = _stk()
    q_ro = ctx.sb("a_qro", [128, NKC, T], BF16, es_q)

    es_xn = _stk()
    xn_fm = ctx.sb("a_xn", [128, NKC, T], BF16, es_xn)
    rope = ctx.sb("s_a_rope", [128, 2048], BF16, es_xn)
    qkwb = [ctx.sb(f"a_wq{i}", [128, NKC, 128], BF16, es_xn) for i in range(2)]
    qraw = ctx.sb("a_qraw", [128, 1024], BF16, es_xn)
    vfm = qraw
    rsc = ctx.t["sq_scratch"]
    pT = [ctx.ps("a_t0", [128, 1024], BF16, es_xn),
          ctx.ps("a_t1", [128, 1024], BF16, es_xn)]
    pb = [ctx.ps(f"a_b{i}", [128, 512], F32, es_xn) for i in range(4)]
    pswp = ctx.ps("a_swp", [128, 1024], F32, es_xn)

    rt0 = s_cast.inc(16)

    @blk.gpsimd
    def _(g, msk_t=msk_t):
        g.wait_ge(s_cast.h, msk_t)
        g.dma_start(out=rope[:], in_=hbm["a_rope"]).then_inc(s_cast.h, 16)

    emit_ln_to_fm(ctx, xn_fm, pT)
    ctx.barrier("a_xn")

    cast_base = s_cast.v
    NSTEP = 48
    mm_done = []
    vtr_done = []
    cons_done = []

    def _qkv_load(st, wait_mm):
        @blk.gpsimd
        def _(g, st=st, wait_mm=wait_mm):
            g.wait_ge(s_cast.h, cast_base + 16 * st)
            if wait_mm is not None:
                g.wait_ge(s_pe.h, wait_mm)
            g.dma_start(out=qkwb[st % 2][:],
                        in_=hbm["L0_attn_w_qkv"][:, st * 128:(st + 1) * 128]
                        .rearrange("(kc p) o -> p kc o", p=128)
                        ).then_inc(s_cast.h, 16)
        s_cast.inc(16)

    _qkv_load(0, None)
    _qkv_load(1, None)

    for st in range(NSTEP):
        oc = st
        pe0 = s_pe.v

        @blk.tensor
        def _(t, st=st, oc=oc, cast_base=cast_base, pe0=pe0):
            t.wait_ge(s_cast.h, cast_base + 16 * (st + 1))
            if oc >= 2:
                sem_n, val = cons_done[oc - 2]
                t.wait_ge((s_dv if sem_n == "dv" else s_sc).h, val)
            for fc in range(2):
                bank = pb[(oc % 2) * 2 + fc]
                last = None
                for kc in range(NKC):
                    last = t.matmul(bank[:], qkwb[st % 2][:, kc, :],
                                    xn_fm[:, kc, fc * 512:(fc + 1) * 512],
                                    start=(kc == 0), stop=(kc == NKC - 1))
                last.then_inc(s_pe.h, 1)
        mm2 = s_pe.inc(2)
        mm_done.append(mm2)
        if st + 2 < NSTEP:
            _qkv_load(st + 2, mm2)

        if oc < 32:
            isq = oc < 16
            cc = oc % 16
            prev_rope = [cd for cd in cons_done if cd[0] == "dv"]
            sc_t = s_sc.inc()

            @blk.scalar
            def _(s, oc=oc, cc=cc, mm2=mm2, prev_rope=prev_rope):
                s.wait_ge(s_pe.h, mm2)
                if prev_rope:
                    s.wait_ge(s_dv.h, prev_rope[-1][1])
                s.activation(qraw[:, 0:512], pb[(oc % 2) * 2][:], AF.Identity,
                             bias=abias[:, oc:oc + 1])
                s.activation(qraw[:, 512:1024], pb[(oc % 2) * 2 + 1][:],
                             AF.Identity, bias=abias[:, oc:oc + 1]
                             ).then_inc(s_sc.h, 1)

            @blk.tensor
            def _(t, sc_t=sc_t, oc=oc, prev_rope=prev_rope):
                t.wait_ge(s_sc.h, sc_t)
                if prev_rope:
                    t.wait_ge(s_dv.h, prev_rope[-1][1])
                t.matmul(pswp[:, 0:512], pswap[:], qraw[:, 0:512],
                         start=True, stop=True)
                t.matmul(pswp[:, 512:1024], pswap[:], qraw[:, 512:1024],
                         start=True, stop=True).then_inc(s_pe.h, 1)
            pe1 = s_pe.inc()
            d0 = s_dv.v

            @blk.vector
            def _(v, pe1=pe1, d0=d0, isq=isq, cc=cc, rt0=rt0):
                v.wait_ge(s_pe.h, pe1)
                v.wait_ge(s_dv.h, d0)
                if cc == 0 and isq:
                    v.wait_ge(s_cast.h, rt0)
                v.tensor_mul(rsc[:, 0:1024], qraw[:], rope[:, 0:1024]
                             ).then_inc(s_dv.h, 1)
                v.wait_ge(s_dv.h, d0 + 1)
                v.tensor_mul(rsc[:, 1024:2048], pswp[:], rope[:, 1024:2048]
                             ).then_inc(s_dv.h, 1)
                v.wait_ge(s_dv.h, d0 + 2)
                dst = (q_ro if isq else k_fm)[:, cc, :]
                v.tensor_add(dst, rsc[:, 0:1024], rsc[:, 1024:2048]
                             ).then_inc(s_dv.h, 1)
            cons = s_dv.inc(3)
            cons_done.append(("dv", cons))
        else:
            cc = oc - 32
            prev_rope = [cd for cd in cons_done if cd[0] == "dv"]
            sc_t = s_sc.inc()

            @blk.scalar
            def _(s, oc=oc, mm2=mm2, cc=cc, prev_rope=prev_rope):
                s.wait_ge(s_pe.h, mm2)
                if cc == 0 and prev_rope:
                    s.wait_ge(s_dv.h, prev_rope[-1][1])
                if cc > 0:
                    s.wait_ge(s_pe.h, vtr_done[-1])
                s.activation(vfm[:, 0:512], pb[(oc % 2) * 2][:], AF.Identity,
                             bias=abias[:, oc:oc + 1])
                s.activation(vfm[:, 512:1024], pb[(oc % 2) * 2 + 1][:],
                             AF.Identity, bias=abias[:, oc:oc + 1]
                             ).then_inc(s_sc.h, 1)
            pe1b = s_pe.v
            ev0 = s_ev.v

            @blk.tensor
            def _(t, sc_t=sc_t, ev0=ev0, oc=oc):
                t.wait_ge(s_sc.h, sc_t)
                if oc > 32:
                    t.wait_ge(s_ev.h, ev0)
                for tt in range(NT):
                    ban = pT[tt % 2]
                    if tt >= 2:
                        t.wait_ge(s_ev.h, ev0 + tt - 1)
                    t.transpose(ban[:, 0:128], vfm[:, tt * 128:(tt + 1) * 128],
                                ident_of(ctx)).then_inc(s_pe.h, 1)
            s_pe.inc(NT)
            vtr_done.append(s_pe.v)

            @blk.vector
            def _(v, pe1b=pe1b, cc=cc):
                for tt in range(NT):
                    ban = pT[tt % 2]
                    v.wait_ge(s_pe.h, pe1b + tt + 1)
                    v.tensor_copy(v_tm[:, tt, cc * 128:(cc + 1) * 128],
                                  ban[:, 0:128]).then_inc(s_ev.h, 1)
            s_ev.inc(NT)
            cons_done.append(("sc", sc_t))

    gw = ctx.sb("a_gw", [128, NKC, 32], BF16, es_xn)
    gfm = qraw
    gt0 = s_cast.inc(16)

    @blk.gpsimd
    def _(g, gt0=gt0):
        g.wait_ge(s_cast.h, gt0 - 16)
        g.dma_start(out=gw[:],
                    in_=hbm["L0_attn_w_gate"].rearrange("(kc p) o -> p kc o", p=128)
                    ).then_inc(s_cast.h, 16)
    ctx.barrier("a_qkv")
    ctx.tap("q_ro", q_ro[:])
    ctx.tap("k_fm", k_fm[:])
    ctx.tap("v_tm", v_tm[:])
    pe0 = s_pe.v

    @blk.tensor
    def _(t, gt0=gt0, pe0=pe0):
        t.wait_ge(s_cast.h, gt0)
        for fc in range(2):
            bank = pb[fc]
            last = None
            for kc in range(NKC):
                last = t.matmul(bank[0:32, :], gw[:, kc, :],
                                xn_fm[:, kc, fc * 512:(fc + 1) * 512],
                                start=(kc == 0), stop=(kc == NKC - 1))
            last.then_inc(s_pe.h, 1)
    s_pe.inc(2)
    sc_t = s_sc.inc()

    @blk.scalar
    def _(s, pe0=pe0):
        s.wait_ge(s_pe.h, pe0 + 2)
        s.activation(gfm[0:32, 0:512], pb[0][0:32, :], AF.Sigmoid,
                     bias=abias[0:32, 192:193])
        s.activation(gfm[0:32, 512:1024], pb[1][0:32, :], AF.Sigmoid,
                     bias=abias[0:32, 192:193]).then_inc(s_sc.h, 1)
    pe1 = s_pe.v
    ev0 = s_ev.v

    @blk.tensor
    def _(t, sc_t=sc_t, ev0=ev0):
        t.wait_ge(s_sc.h, sc_t)
        for tt in range(NT):
            ban = pT[tt % 2]
            if tt >= 2:
                t.wait_ge(s_ev.h, ev0 + tt - 1)
            t.transpose(ban[:, 0:32], gfm[0:32, tt * 128:(tt + 1) * 128],
                        ident_of(ctx)[0:32, 0:32]).then_inc(s_pe.h, 1)
    s_pe.inc(NT)

    @blk.vector
    def _(v, pe1=pe1):
        for tt in range(NT):
            ban = pT[tt % 2]
            v.wait_ge(s_pe.h, pe1 + tt + 1)
            v.tensor_copy(g_tm[:, tt, :], ban[:, 0:32]).then_inc(s_ev.h, 1)
    s_ev.inc(NT)
    ctx.barrier("a_a2")
    ctx.tap("g_tm", g_tm[:])
    es_xn.close()
    if A_STOP <= 1:
        _close_all()
        return

    _mark("A_loc")
    es_loc = _stk()
    q_fm = q_ro
    qz = [ctx.sb(f"a_qz{i}", [128, 512], BF16, es_loc) for i in range(2)]
    ebf = ctx.sb("a_ebf", [64, 8, 128], BF16, es_loc)
    etb = ctx.sb("a_etb", [128, 8, 128], BF16, es_loc)
    pS = ctx.ps("a_ps", [128, 512], F32, es_loc)
    pEf = ctx.ps("a_pef", [128, 1024], F32, es_loc)
    pO = ctx.ps("a_po", [128, 512], F32, es_loc)

    @blk.vector
    def _(v):
        v.memset(qz[0][:], 0.0)
        v.memset(qz[1][:], 0.0).then_inc(s_dv.h, 1)
        v.memset(ebf[:], 0.0).then_inc(s_dv.h, 1)
    s_dv.inc(2)

    for h in range(32):
        hp, hh = h // 2, (h % 2) * 64
        for bg in range(2):
            dq = s_dv.inc()

            @blk.vector
            def _(v, h=h, hp=hp, hh=hh, bg=bg, dq=dq):
                v.wait_ge(s_dv.h, dq - 1)
                v.tensor_copy(qz[h % 2][hh:hh + 64, :],
                              q_fm[hh:hh + 64, hp, bg * 512:(bg + 1) * 512]
                              ).then_inc(s_dv.h, 1)

            @blk.tensor
            def _(t, hp=hp, h=h, bg=bg, dq=dq):
                t.wait_ge(s_dv.h, dq)
                if h == 0 and bg == 0:
                    t.wait_ge(s_cast.h, msk_t)
                last = None
                for b8 in range(8):
                    blk_i = bg * 8 + b8
                    last = t.matmul(
                        pS[0:64, b8 * 64:(b8 + 1) * 64],
                        qz[h % 2][:, b8 * 64:(b8 + 1) * 64],
                        k_fm[:, hp, blk_i * 64:(blk_i + 1) * 64],
                        start=True, stop=True)
                last.then_inc(s_pe.h, 1)
            mmt = s_pe.inc()
            sm_t = emit_softmax_seg(
                ctx, pS[0:64, :], 0, 64, 8, 64, sq[0:64, 0:512],
                ebf[0:64, :, :].rearrange("q s k -> q (s k)"),
                nmsc[0:64, 0:8], densc[0:64, 0:8],
                bias_ap=mloc[:], bias_shared=True,
                pe_wait=(lambda v, mmt=mmt: v.wait_ge(s_pe.h, mmt)),
                slot_w=128, slot_off=lambda g: (g % 2) * 64)

            @blk.tensor
            def _(t, sm_t=sm_t):
                t.wait_ge(s_dv.h, sm_t)
                last = None
                for b8 in range(8):
                    last = t.matmul(pEf[:, b8 * 128:(b8 + 1) * 128],
                                    ebf[0:64, b8, :], id2[:],
                                    start=True, stop=True)
                last.then_inc(s_pe.h, 1)
            pt = s_pe.inc()
            se = s_sc.inc()

            @blk.scalar
            def _(s, pt=pt):
                s.wait_ge(s_pe.h, pt)
                s.copy(etb[:], pEf[:].rearrange("p (s k) -> p s k", s=8)
                       ).then_inc(s_sc.h, 1)

            @blk.tensor
            def _(t, se=se, hp=hp, h=h, bg=bg):
                t.wait_ge(s_sc.h, se)
                last = None
                for b8 in range(8):
                    blk_i = bg * 8 + b8
                    last = t.matmul(
                        pO[:, b8 * 64:(b8 + 1) * 64],
                        etb[:, b8, :],
                        v_tm[:, blk_i // 2, h * 64:(h + 1) * 64],
                        start=True, stop=True)
                last.then_inc(s_pe.h, 1)
            av = s_pe.inc()

            @blk.vector
            def _(v, av=av, h=h, bg=bg):
                v.wait_ge(s_pe.h, av)
                last = None
                for b8 in range(8):
                    blk_i = bg * 8 + b8
                    rb = (blk_i % 2) * 64
                    last = v.tensor_copy(
                        ol_tm[rb:rb + 64, blk_i // 2, h * 64:(h + 1) * 64],
                        pO[rb:rb + 64, b8 * 64:(b8 + 1) * 64])
                last.then_inc(s_dv.h, 1)
            s_dv.inc()

    ctx.barrier("a_loc")
    ctx.tap("ol_tm", ol_tm[:])
    es_loc.close()
    if A_STOP <= 2:
        _close_all()
        return

    _mark("A_spill")
    qt = s_qsp.inc(16)

    @blk.scalar
    def _(s):
        s.dma_start(out=ctx.qspill_dram.rearrange("p (c t) -> p c t", c=NKC),
                    in_=q_ro[:]).then_inc(s_qsp.h, 16)

    @blk.sync
    def _(s, qt=qt):
        s.wait_ge(s_qsp.h, qt)
    ctx.barrier("a_qspill")
    es_q.close()

    _mark("A_pool")
    es_lat = _stk()
    lmf_full = ctx.sb("a_lmf", [128, NKC, 128], BF16, es_lat)
    lv_tm = ctx.sb("a_lvt", [128, 2048], BF16, es_lat)
    lmf2 = ctx.sb("a_lm2", [128, NKC, 128], BF16, es_lat)
    lmf3 = ctx.sb("a_lm3", [128, NKC, 128], BF16, es_lat)
    ro_full = ctx.sb("a_rof", [128, NKC, 128], BF16, es_lat)
    gv_fm = ctx.sb("a_gvf", [128, NKC, 128], BF16, es_lat)
    lsc = ctx.sb("a_lsc", [128, 128], BF16, es_lat)
    etb2 = ctx.sb("a_etb2b", [128, 128], BF16, es_lat)
    latw = [ctx.sb(f"a_lw{i}", [128, NKC, 256], BF16, es_lat) for i in range(2)]
    pS = ctx.ps("a_ps4", [128, 512], F32, es_lat)
    pS2 = ctx.ps("a_ps4b", [128, 512], F32, es_lat)
    pE = ctx.ps("a_pe4", [128, 1024], BF16, es_lat)
    pE2 = ctx.ps("a_pe4b", [128, 1024], BF16, es_lat)
    pEf = ctx.ps("a_pef4", [128, 512], F32, es_lat)
    pO = ctx.ps("a_po4", [128, 512], F32, es_lat)
    pb2 = [ctx.ps(f"a_pb4{i}", [128, 512], F32, es_lat) for i in range(2)]

    es_lat1 = _stk()
    lmf_own = ctx.sb("a_lmo", [128, NKC, 64], BF16, es_lat1)
    lq = ctx.sb("a_lq", [128, NKC, 128], BF16, es_lat1)
    lk = ctx.sb("a_lk", [128, NKC, 128], BF16, es_lat1)
    lv_fm = ctx.sb("a_lvf", [128, NKC, 128], BF16, es_lat1)
    lo_tm = ctx.sb("a_lot", [128, 2048], BF16, es_lat1)
    lo_fm = ctx.sb("a_lof", [128, NKC, 128], BF16, es_lat1)
    ebf = ctx.sb("a_ebf2", [128, 16, 128], BF16, es_lat1)
    etb = ctx.sb("a_etb2", [128, 16, 8], BF16, es_lat1)
    lqz = [ctx.sb(f"a_lqz{i}", [128, 128], BF16, es_lat1) for i in range(2)]

    @blk.vector
    def _(v):
        v.memset(ebf[:], 0.0)
        v.memset(lqz[0][:], 0.0)
        v.memset(lqz[1][:], 0.0).then_inc(s_dv.h, 1)
    s_dv.inc()

    for hp in range(NKC):
        pe0 = s_pe.v

        @blk.tensor
        def _(t, hp=hp, pe0=pe0):
            t.wait_ge(s_dv.h, s_dv.v)
            if hp == 0:
                t.wait_ge(s_cast.h, msk_t)
            last = None
            for fc in range(2):
                bank = pS if fc == 0 else pS2
                last = t.matmul(bank[0:8, :], poolq[:, hp * 8:(hp + 1) * 8],
                                k_fm[:, hp, fc * 512:(fc + 1) * 512],
                                start=True, stop=True)
            last.then_inc(s_pe.h, 1)
        mmt = s_pe.inc()
        for fc in range(2):
            bank = pS if fc == 0 else pS2
            emit_softmax_seg(
                ctx, bank[0:8, :], 0, 8, 8, 64, sq[0:8, 0:512],
                ebf[0:8, :, :].rearrange("q s k -> q (s k)")
                [:, fc * 1024:(fc + 1) * 1024],
                nmsc[0:8, fc * 8:fc * 8 + 8], densc[0:8, fc * 8:fc * 8 + 8],
                pe_wait=(lambda v, mmt=mmt: v.wait_ge(s_pe.h, mmt)),
                slot_w=128, slot_off=lambda g: (g % 2) * 64)
        sm_t = s_dv.v

        @blk.tensor
        def _(t, sm_t=sm_t):
            t.wait_ge(s_dv.h, sm_t)
            last = None
            for b in range(NB):
                last = t.matmul(pEf[:, b * 8:(b + 1) * 8],
                                ebf[0:8, b, :], ident_of(ctx)[0:8, 0:8],
                                start=True, stop=True)
            last.then_inc(s_pe.h, 1)
        pt = s_pe.inc()
        se = s_sc.inc()

        @blk.scalar
        def _(s, pt=pt):
            s.wait_ge(s_pe.h, pt)
            s.copy(etb[:], pEf[:, 0:128].rearrange("p (s k) -> p s k", s=NB)
                   ).then_inc(s_sc.h, 1)

        @blk.tensor
        def _(t, se=se, hp=hp):
            t.wait_ge(s_sc.h, se)
            last = None
            for b in range(NB):
                last = t.matmul(pO[:, b * 8:(b + 1) * 8],
                                v_tm[:, b // 2, hp * 128:(hp + 1) * 128],
                                etb[:, b, :], start=True, stop=True)
            last.then_inc(s_pe.h, 1)
        av = s_pe.inc()

        @blk.vector
        def _(v, av=av, hp=hp):
            v.wait_ge(s_pe.h, av)
            last = None
            for b in range(NB):
                last = v.tensor_copy(lmf_own[0:64, hp, b * 4:(b + 1) * 4],
                                     pO[0:64, b * 8:b * 8 + 4])
                last = v.tensor_copy(lmf_own[64:128, hp, b * 4:(b + 1) * 4],
                                     pO[64:128, b * 8 + 4:(b + 1) * 8])
            last.then_inc(s_dv.h, 1)
        s_dv.inc()

    ctx.barrier("a_pool")
    ctx.tap("lmf_own", lmf_own[:])
    if A_STOP <= 3:
        _close_all()
        return

    _mark("A_ag1")
    ag_t = emit_allgather_pair(ctx, lmf_own[:], ctx.ag1_in, ctx.ag1_out,
                               lmf_full, lambda g: None)

    @blk.sync
    def _(s, ag_t=ag_t):
        s.wait_ge(s_cast.h, ag_t)
    ctx.barrier("a_ag1")
    ctx.tap("lmf_full", lmf_full[:])
    if A_STOP <= 4:
        _close_all()
        return

    _mark("A_lat")
    def bias_evac_cb(dst, bcol0, oc_off=0):
        def cb(oc, bank_ap, mm_t):
            se = s_sc.inc()

            @blk.scalar
            def _(s, oc=oc, bank_ap=bank_ap, mm_t=mm_t, se=se):
                s.wait_ge(s_pe.h, mm_t)
                s.activation(dst[:, oc - oc_off, :], bank_ap, AF.Identity,
                             bias=abias[:, bcol0 + oc:bcol0 + oc + 1]
                             ).then_inc(s_sc.h, 1)
            ev = s_ev.inc()

            @blk.vector
            def _(v, se=se):
                v.wait_ge(s_sc.h, se)
                v.engine_nop().then_inc(s_ev.h, 1)
            return ev
        return cb

    def resid_cb(dst, src_resid, bcol0):
        prev_d = [None]

        def cb(oc, bank_ap, mm_t):
            se = s_sc.inc()

            @blk.scalar
            def _(s, oc=oc, bank_ap=bank_ap, mm_t=mm_t):
                s.wait_ge(s_pe.h, mm_t)
                if prev_d[0] is not None:
                    s.wait_ge(s_dv.h, prev_d[0])
                s.activation(lsc[:, 0:128], bank_ap, AF.Identity,
                             bias=abias[:, bcol0 + oc:bcol0 + oc + 1]
                             ).then_inc(s_sc.h, 1)
            d0 = s_dv.inc()
            prev_d[0] = d0

            @blk.vector
            def _(v, oc=oc, se=se):
                v.wait_ge(s_sc.h, se)
                v.tensor_add(dst[:, oc, :], src_resid[:, oc, :], lsc[:, 0:128]
                             ).then_inc(s_dv.h, 1)
            ev = s_ev.inc()

            @blk.vector
            def _(v, d0=d0):
                v.wait_ge(s_dv.h, d0)
                v.engine_nop().then_inc(s_ev.h, 1)
            return ev
        return cb

    def lmqkv_cb(oc, bank_ap, mm_t):
        if oc < 16:
            return bias_evac_cb(lq, 48)(oc, bank_ap, mm_t)
        elif oc < 32:
            return bias_evac_cb(lk, 48, 16)(oc, bank_ap, mm_t)
        return bias_evac_cb(lv_fm, 48, 32)(oc, bank_ap, mm_t)

    emit_latproj(ctx, hbm["L0_attn_w_lm_qkv"], 0, 48, lmf_full, 128,
                 latw, pb2, lmqkv_cb)
    ctx.barrier("a_lmqkv")
    ctx.tap("lq", lq[:])
    ctx.tap("lk", lk[:])
    ctx.tap("lv_fm", lv_fm[:])

    pe0 = s_pe.v
    ev0 = s_ev.v

    @blk.tensor
    def _(t, pe0=pe0, ev0=ev0):
        for c in range(NKC):
            if c >= 2:
                t.wait_ge(s_ev.h, ev0 + c - 1)
            t.transpose([pE, pE2][c % 2][:, 0:128],
                        lv_fm[:, c, :], ident_of(ctx)).then_inc(s_pe.h, 1)
    s_pe.inc(NKC)

    @blk.scalar
    def _(s, pe0=pe0):
        for c in range(NKC):
            s.wait_ge(s_pe.h, pe0 + c + 1)
            s.copy(lv_tm[:, c * 128:(c + 1) * 128],
                   [pE, pE2][c % 2][:, 0:128]).then_inc(s_ev.h, 1)
    s_ev.inc(NKC)
    ctx.barrier("a_lvt")
    ctx.tap("lv_tm", lv_tm[:])

    for h in range(32):
        hp, hh = h // 2, (h % 2) * 64
        dq = s_dv.inc()

        @blk.vector
        def _(v, hp=hp, hh=hh, h=h, dq=dq):
            v.wait_ge(s_dv.h, dq - 1)
            v.tensor_copy(lqz[h % 2][hh:hh + 64, :], lq[hh:hh + 64, hp, :]
                          ).then_inc(s_dv.h, 1)

        @blk.tensor
        def _(t, hp=hp, h=h, dq=dq):
            t.wait_ge(s_dv.h, dq)
            t.matmul(pS[:, 0:128], lqz[h % 2][:], lk[:, hp, :],
                     start=True, stop=True).then_inc(s_pe.h, 1)
        mmt = s_pe.inc()
        dbg_lat = _os.environ.get("A_DBG_LAT", "0") == "1" and h in (0, 1, 17)
        if dbg_lat:
            ctx.barrier(f"dbgs{h}")
            dd = s_dv.inc()

            @blk.vector
            def _(v, dd=dd, mmt=mmt):
                v.wait_ge(s_pe.h, mmt)
                v.tensor_copy(sq[:, 1024:1152], pS[:, 0:128]).then_inc(s_dv.h, 1)
            ctx.barrier(f"dbgs2{h}")
            ctx.tap(f"lat_s{h}", sq[:, 1024:1152])
            ctx.tap(f"lat_qz{h}", lqz[h % 2][:])
            ctx.barrier(f"dbgs3{h}")
        sm_t = emit_softmax_seg(
            ctx, pS[:, 0:128], 0, 128, 1, 128, sq[:, 0:128], etb2[:],
            nmsc[:, 0:1], densc[:, 0:1], bias_ap=mlat[:],
            pe_wait=(lambda v, mmt=mmt: v.wait_ge(s_pe.h, mmt)))
        if dbg_lat:
            ctx.barrier(f"dbge{h}")
            ctx.tap(f"lat_e{h}", etb2[:])
            ctx.barrier(f"dbge2{h}")

        @blk.tensor
        def _(t, sm_t=sm_t):
            t.wait_ge(s_dv.h, sm_t)
            t.transpose(pE[:, 0:128], etb2[:], ident_of(ctx)
                        ).then_inc(s_pe.h, 1)
        pt = s_pe.inc()
        se = s_sc.inc()

        @blk.scalar
        def _(s, pt=pt):
            s.wait_ge(s_pe.h, pt)
            s.copy(etb2[:], pE[:, 0:128]).then_inc(s_sc.h, 1)

        @blk.tensor
        def _(t, se=se, h=h):
            t.wait_ge(s_sc.h, se)
            t.matmul(pO[:, 0:64], etb2[:], lv_tm[:, h * 64:(h + 1) * 64],
                     start=True, stop=True).then_inc(s_pe.h, 1)
        av = s_pe.inc()
        if dbg_lat:
            ctx.barrier(f"dbgo{h}")
            dd = s_dv.inc()

            @blk.vector
            def _(v, dd=dd, av=av):
                v.wait_ge(s_pe.h, av)
                v.tensor_copy(sq[:, 1024:1088], pO[:, 0:64]).then_inc(s_dv.h, 1)
            ctx.barrier(f"dbgo2{h}")
            ctx.tap(f"lat_eT{h}", etb2[:])
            ctx.tap(f"lat_o{h}", sq[:, 1024:1088])
            ctx.barrier(f"dbgo3{h}")

        @blk.vector
        def _(v, av=av, h=h):
            v.wait_ge(s_pe.h, av)
            v.tensor_copy(lo_tm[:, h * 64:(h + 1) * 64], pO[:, 0:64]
                          ).then_inc(s_dv.h, 1)
        s_dv.inc()
    ctx.barrier("a_lattn")
    ctx.tap("lo_tm", lo_tm[:])

    pe0 = s_pe.v
    ev0 = s_ev.v

    @blk.tensor
    def _(t, pe0=pe0, ev0=ev0):
        for c in range(NKC):
            if c >= 2:
                t.wait_ge(s_ev.h, ev0 + c - 1)
            t.transpose([pE, pE2][c % 2][:, 0:128],
                        lo_tm[:, c * 128:(c + 1) * 128], ident_of(ctx)
                        ).then_inc(s_pe.h, 1)
    s_pe.inc(NKC)

    @blk.scalar
    def _(s, pe0=pe0):
        for c in range(NKC):
            s.wait_ge(s_pe.h, pe0 + c + 1)
            s.copy(lo_fm[:, c, :], [pE, pE2][c % 2][:, 0:128]
                   ).then_inc(s_ev.h, 1)
    s_ev.inc(NKC)
    ctx.barrier("a_lofm")

    emit_latproj(ctx, hbm["L0_attn_w_lm_out"], 0, 16, lo_fm, 128,
                 latw, pb2, resid_cb(lmf2, lmf_full, 96))
    ctx.barrier("a_lmf2")
    ctx.tap("lmf2", lmf2[:])
    es_lat1.close()

    _mark("A_ra")
    es_lat2 = _stk()
    l2q = ctx.sb("a_l2q", [128, NKC, 128], BF16, es_lat2)
    ro_all = ctx.sb("a_roo", [128, NKC, 128], BF16, es_lat2)
    ebr = ctx.sb("a_ebr", [4, 32, 128], BF16, es_lat2)
    l2qp = ctx.sb("a_l2qp", [128, 128], BF16, es_lat2)

    @blk.vector
    def _(v):
        v.wait_ge(s_dv.h, s_dv.v)
        v.memset(ebr[:], 0.0)
        v.memset(l2qp[:], 0.0).then_inc(s_dv.h, 1)
    s_dv.inc()

    emit_latproj(ctx, hbm["L0_attn_w_l2t_q"], 0, 16, lmf2, 128,
                 latw, pb2, bias_evac_cb(l2q, 112))
    ctx.barrier("a_l2q")
    ctx.tap("l2q", l2q[:])

    for obh in (0, 64):
        for hp in range(NKC):
            dq = s_dv.inc(2)

            @blk.vector
            def _(v, hp=hp, obh=obh, dq=dq):
                v.wait_ge(s_dv.h, dq - 2)
                v.tensor_copy(l2qp[0:64, 0:64],
                              l2q[0:64, hp, obh:obh + 64]).then_inc(s_dv.h, 1)
                v.wait_ge(s_dv.h, dq - 1)
                v.tensor_copy(l2qp[64:128, 64:128],
                              l2q[64:128, hp, obh:obh + 64]).then_inc(s_dv.h, 1)

            for h2 in range(2):
                for half8 in range(2):
                    dv_now = s_dv.v

                    @blk.tensor
                    def _(t, hp=hp, h2=h2, half8=half8, dv_now=dv_now):
                        t.wait_ge(s_dv.h, dv_now)
                        last = None
                        for g in range(8):
                            bl = half8 * 8 + g
                            last = t.matmul(
                                pS[0:4, g * 64:(g + 1) * 64],
                                l2qp[:, h2 * 64 + bl * 4:h2 * 64 + (bl + 1) * 4],
                                k_fm[:, hp, bl * 64:(bl + 1) * 64],
                                start=True, stop=True)
                        last.then_inc(s_pe.h, 1)
                    mmt = s_pe.inc()
                    s0 = (h2 * 16 + half8 * 8) * 128
                    emit_softmax_seg(
                        ctx, pS[0:4, 0:512], 0, 4, 8, 64, sq[0:4, 0:512],
                        ebr[0:4, :, :].rearrange("q s k -> q (s k)")
                        [:, s0:s0 + 1024],
                        nmsc[0:4, 0:8], densc[0:4, 0:8],
                        pe_wait=(lambda v, mmt=mmt: v.wait_ge(s_pe.h, mmt)),
                        slot_w=128, slot_off=lambda g: (g % 2) * 64)
            sm_t = s_dv.v

            @blk.tensor
            def _(t, sm_t=sm_t):
                t.wait_ge(s_dv.h, sm_t)
                last = None
                for sl in range(32):
                    last = t.matmul(pEf[:, sl * 4:(sl + 1) * 4],
                                    ebr[0:4, sl, :], ident_of(ctx)[0:4, 0:4],
                                    start=True, stop=True)
                last.then_inc(s_pe.h, 1)
            pt = s_pe.inc()
            se = s_sc.inc()

            @blk.scalar
            def _(s, pt=pt):
                s.wait_ge(s_pe.h, pt)
                s.copy(etb2[:], pEf[:, 0:128]).then_inc(s_sc.h, 1)

            @blk.tensor
            def _(t, se=se, hp=hp):
                t.wait_ge(s_sc.h, se)
                last = None
                for h2 in range(2):
                    for bl in range(NB):
                        sl = h2 * 16 + bl
                        last = t.matmul(
                            pO[:, h2 * 64 + bl * 4:h2 * 64 + (bl + 1) * 4],
                            v_tm[:, bl // 2, hp * 128:(hp + 1) * 128],
                            etb2[:, sl * 4:(sl + 1) * 4],
                            start=True, stop=True)
                last.then_inc(s_pe.h, 1)
            av = s_pe.inc()

            @blk.vector
            def _(v, av=av, hp=hp, obh=obh):
                v.wait_ge(s_pe.h, av)
                v.tensor_copy(ro_all[0:64, hp, obh:obh + 64], pO[0:64, 0:64])
                v.tensor_copy(ro_all[64:128, hp, obh:obh + 64],
                              pO[64:128, 64:128]).then_inc(s_dv.h, 1)
            s_dv.inc()
    ctx.barrier("a_ra")
    ctx.tap("ro_all", ro_all[:])
    if A_STOP <= 5:
        _close_all()
        return

    _mark("A_l2o")
    ag_t = emit_allgather_pair(ctx, ro_all[:], ctx.ag2_in, ctx.ag2_out,
                               ro_full, lambda g: None, W=128)

    @blk.sync
    def _(s, ag_t=ag_t):
        s.wait_ge(s_cast.h, ag_t)
    ctx.barrier("a_ag2")
    ctx.tap("ro_full", ro_full[:])
    es_lat2.close()

    emit_latproj(ctx, hbm["L0_attn_w_l2t_out"], 0, 16, ro_full, 128,
                 latw, pb2, resid_cb(lmf3, lmf2, 128))
    ctx.barrier("a_lmf3")
    ctx.tap("lmf3", lmf3[:])

    _mark("A_gkv")
    def lmkv_cb(oc, bank_ap, mm_t):
        if oc < 16:
            return bias_evac_cb(gk_fm, 144)(oc, bank_ap, mm_t)
        return bias_evac_cb(gv_fm, 144, 16)(oc, bank_ap, mm_t)

    emit_latproj(ctx, hbm["L0_attn_w_lm_kv"], 0, 32, lmf3, 128,
                 latw, pb2, lmkv_cb)
    ctx.barrier("a_lmkv")
    ctx.tap("gk_fm", gk_fm[:])
    ctx.tap("gv_fm", gv_fm[:])

    pe0 = s_pe.v
    ev0 = s_ev.v

    @blk.tensor
    def _(t, pe0=pe0, ev0=ev0):
        for c in range(NKC):
            if c >= 2:
                t.wait_ge(s_ev.h, ev0 + c - 1)
            t.transpose([pE, pE2][c % 2][:, 0:128],
                        gv_fm[:, c, :], ident_of(ctx)).then_inc(s_pe.h, 1)
    s_pe.inc(NKC)

    @blk.scalar
    def _(s, pe0=pe0):
        for c in range(NKC):
            s.wait_ge(s_pe.h, pe0 + c + 1)
            s.copy(gv_tm[:, c * 128:(c + 1) * 128],
                   [pE, pE2][c % 2][:, 0:128]).then_inc(s_ev.h, 1)
    s_ev.inc(NKC)
    ctx.barrier("a_gkv")
    if A_STOP <= 6:
        _close_all()
        return
    es_lat.close()
    es_kv.close()

    _mark("A_glob")
    es_late = _stk()
    og_tm = ctx.sb("a_og", [128, NT, 2048], BF16, es_late)
    es_q9 = _stk()
    gmask = ctx.sb("s_a_gmask", [128, NT, 256], BF16, es_q9)
    qre = ctx.sb("a_qre", [128, NKC, T], BF16, es_q9)
    qgz = [ctx.sb(f"a_qgz{i}", [128, 128], BF16, es_q9) for i in range(2)]
    ebf3 = ctx.sb("a_ebf3", [128, 128], BF16, es_q9)
    etb3 = ctx.sb("a_etb3", [128, 128], BF16, es_q9)
    pS = ctx.ps("a_ps9", [128, 512], F32, es_q9)
    pE = ctx.ps("a_pe9", [128, 1024], BF16, es_q9)
    pO = ctx.ps("a_po9", [128, 512], F32, es_q9)
    qre_t = s_ld.inc(16)

    @blk.sync
    def _(s):
        s.dma_start(out=qre[:],
                    in_=ctx.qspill_dram.rearrange("p (c t) -> p c t", c=NKC)
                    ).then_inc(s_ld.h, 16)
    gm_t = s_cast.inc(16)

    @blk.gpsimd
    def _(g, gm_t=gm_t):
        g.wait_ge(s_cast.h, gm_t - 16)
        g.dma_start(out=gmask[:],
                    in_=hbm["a_gmask"].rearrange("(tt p) c -> p tt c", p=128)
                    ).then_inc(s_cast.h, 16)

    dz = s_dv.inc()

    @blk.vector
    def _(v, dz=dz):
        v.wait_ge(s_dv.h, dz - 1)
        v.memset(qgz[0][:], 0.0)
        v.memset(qgz[1][:], 0.0).then_inc(s_dv.h, 1)

    for tt in range(NT):
        for h in range(32):
            hp, hh = h // 2, (h % 2) * 64
            first = (tt == 0 and h == 0)
            dq = s_dv.inc()

            @blk.vector
            def _(v, hp=hp, hh=hh, h=h, tt=tt, dq=dq, first=first):
                v.wait_ge(s_dv.h, dq - 1)
                if first:
                    v.wait_ge(s_ld.h, qre_t)
                v.tensor_copy(qgz[h % 2][hh:hh + 64, :],
                              qre[hh:hh + 64, hp, tt * 128:(tt + 1) * 128]
                              ).then_inc(s_dv.h, 1)

            @blk.tensor
            def _(t, hp=hp, h=h, tt=tt, dq=dq, first=first):
                t.wait_ge(s_dv.h, dq)
                if first:
                    t.wait_ge(s_cast.h, gm_t)
                t.matmul(pS[:, 0:128], qgz[h % 2][:], gk_fm[:, hp, :],
                         start=True, stop=True).then_inc(s_pe.h, 1)
            mmt = s_pe.inc()
            sm_t = emit_softmax_seg(
                ctx, pS[:, 0:128], 0, 128, 1, 128, sq[:, 0:128], ebf3[:],
                nmsc[:, 0:1], densc[:, 0:1],
                bias_ap=gmask[:, tt, 0:128], valid_ap=gmask[:, tt, 128:256],
                pe_wait=(lambda v, mmt=mmt: v.wait_ge(s_pe.h, mmt)))

            @blk.tensor
            def _(t, sm_t=sm_t):
                t.wait_ge(s_dv.h, sm_t)
                t.transpose(pE[:, 0:128], ebf3[:], ident_of(ctx)
                            ).then_inc(s_pe.h, 1)
            pt = s_pe.inc()
            se = s_sc.inc()

            @blk.scalar
            def _(s, pt=pt):
                s.wait_ge(s_pe.h, pt)
                s.copy(etb3[:], pE[:, 0:128]).then_inc(s_sc.h, 1)

            @blk.tensor
            def _(t, se=se, h=h):
                t.wait_ge(s_sc.h, se)
                t.matmul(pO[:, 0:64], etb3[:], gv_tm[:, h * 64:(h + 1) * 64],
                         start=True, stop=True).then_inc(s_pe.h, 1)
            av = s_pe.inc()

            @blk.vector
            def _(v, av=av, h=h, tt=tt):
                v.wait_ge(s_pe.h, av)
                v.tensor_copy(og_tm[:, tt, h * 64:(h + 1) * 64], pO[:, 0:64]
                              ).then_inc(s_dv.h, 1)
            s_dv.inc()
    ctx.barrier("a_glob")
    ctx.tap("og_tm", og_tm[:])
    if A_STOP <= 7:
        _close_all()
        return
    es_q9.close()

    _mark("A_a10")
    c_fm = ctx.sb("a_cfm", [128, NKC, T], BF16, es_late)
    o_fm = ctx.sb("a_ofm", [128, NKC, T], BF16, es_late)
    wob = [ctx.sb(f"a_wo{i}", [128, NKC, 512], BF16, es_late) for i in range(2)]
    pE = ctx.ps("a_pec", [128, 1024], BF16, es_late)
    pb3 = [ctx.ps(f"a_pb9{i}", [128, 512], F32, es_late) for i in range(4)]

    for tt in range(NT):
        d0 = s_dv.v

        @blk.vector
        def _(v, tt=tt, d0=d0):
            v.wait_ge(s_dv.h, d0)
            v.tensor_sub(sq[:], og_tm[:, tt, :], ol_tm[:, tt, :]).then_inc(s_dv.h, 1)
            v.wait_ge(s_dv.h, d0 + 1)
            v.tensor_tensor(
                sq[:].rearrange("p (h d) -> p h d", h=32),
                sq[:].rearrange("p (h d) -> p h d", h=32),
                g_tm[:, tt, :].rearrange("p (h o) -> p h o", o=1)
                .broadcast_to([128, 32, 64]), op=ALU.mult).then_inc(s_dv.h, 1)
            v.wait_ge(s_dv.h, d0 + 2)
            v.tensor_add(tmb[:], sq[:], ol_tm[:, tt, :]).then_inc(s_dv.h, 1)
        dv_c = s_dv.inc(3)
        pe0 = s_pe.v
        ev0 = s_ev.v

        @blk.tensor
        def _(t, tt=tt, dv_c=dv_c, pe0=pe0, ev0=ev0):
            for g in range(2):
                t.wait_ge(s_dv.h, dv_c)
                t.wait_ge(s_ev.h, ev0 + g)
                last = None
                for j in range(8):
                    fc = g * 8 + j
                    last = t.transpose(pE[:, j * 128:(j + 1) * 128],
                                       tmb[:, fc * 128:(fc + 1) * 128],
                                       ident_of(ctx))
                last.then_inc(s_pe.h, 1)
        s_pe.inc(2)

        @blk.scalar
        def _(s, tt=tt, pe0=pe0):
            for g in range(2):
                s.wait_ge(s_pe.h, pe0 + g + 1)
                s.copy(c_fm[:, g * 8:(g + 1) * 8, tt * 128:(tt + 1) * 128],
                       pE[:].rearrange("p (a b) -> p a b", a=8)).then_inc(s_ev.h, 1)
        s_ev.inc(2)
        ctx.barrier(f"a_cmb_{tt}")

    cast_base = s_cast.v
    pe_base0 = s_pe.v

    @blk.gpsimd
    def _(g, cast_base=cast_base, pe_base0=pe_base0):
        for og in range(4):
            g.wait_ge(s_cast.h, cast_base + 16 * og)
            if og >= 2:
                g.wait_ge(s_pe.h, pe_base0 + 8 * (og - 1))
            g.dma_start(out=wob[og % 2][:],
                        in_=hbm["L0_attn_w_out"][:, og * 512:(og + 1) * 512]
                        .rearrange("(kc p) o -> p kc o", p=128)
                        ).then_inc(s_cast.h, 16)
    s_cast.inc(64)

    sc_evs = []
    for og in range(4):
        for j in range(4):
            oc = og * 4 + j

            @blk.tensor
            def _(t, og=og, j=j, oc=oc, cast_base=cast_base):
                t.wait_ge(s_cast.h, cast_base + 16 * (og + 1))
                if oc >= 2:
                    t.wait_ge(s_sc.h, sc_evs[oc - 2])
                for fc in range(2):
                    bank = pb3[(oc % 2) * 2 + fc]
                    last = None
                    for kc in range(NKC):
                        last = t.matmul(bank[:],
                                        wob[og % 2][:, kc, j * 128:(j + 1) * 128],
                                        c_fm[:, kc, fc * 512:(fc + 1) * 512],
                                        start=(kc == 0), stop=(kc == NKC - 1))
                    last.then_inc(s_pe.h, 1)
            mm2 = s_pe.inc(2)
            se = s_sc.inc()

            @blk.scalar
            def _(s, oc=oc, mm2=mm2):
                s.wait_ge(s_pe.h, mm2)
                s.activation(o_fm[:, oc, 0:512], pb3[(oc % 2) * 2][:],
                             AF.Identity, bias=abias[:, 176 + oc:177 + oc])
                s.activation(o_fm[:, oc, 512:1024], pb3[(oc % 2) * 2 + 1][:],
                             AF.Identity, bias=abias[:, 176 + oc:177 + oc]
                             ).then_inc(s_sc.h, 1)
            sc_evs.append(se)
    ctx.barrier("a_wout")
    ctx.tap("o_fm", o_fm[:])

    pTe = [ctx.ps("a_et0", [128, 1024], BF16, es_late),
           ctx.ps("a_et1", [128, 1024], BF16, es_late)]
    _mark("A_epi")
    emit_out_epilogue(ctx, lambda fc, tt: o_fm[:, fc, tt * 128:(tt + 1) * 128],
                      pTe)
    _mark(None)
    es_late.close()
    es_msk.close()
    es_ph.close()



def ident_of(ctx):
    return ctx.t["s_ident"][:]


def emit_butterflies(ctx, zap, n):
    blk = ctx.block
    s_dv = ctx.sem("dv")
    h = 1
    while h < n:
        for c in range(n):
            if c & h:
                continue
            d0 = s_dv.v
            s_dv.inc(2)

            @blk.vector
            def _(v, a=zap(c), b=zap(c + h), d0=d0):
                v.wait_ge(s_dv.h, d0)
                v.tensor_add(a, a, b).then_inc(s_dv.h, 1)
                v.wait_ge(s_dv.h, d0 + 1)
                v.scalar_tensor_tensor(b, b, -2.0, a, op0=ALU.mult,
                                       op1=ALU.add).then_inc(s_dv.h, 1)
        h *= 2


def emit_fwht_step1(ctx, zap, nslices, width, pb):
    blk = ctx.block
    s_pe = ctx.sem("pe")
    s_ev = ctx.sem("ev")
    h128 = ctx.t["s_h128"]
    pe_base = s_pe.v
    ev_base = s_ev.v

    @blk.tensor
    def _(t):
        for c in range(nslices):
            bank = pb[c % 4]
            if c >= 4:
                t.wait_ge(s_ev.h, ev_base + c - 3)
            t.matmul(bank[:, 0:width], h128[:], zap(c), start=True,
                     stop=True).then_inc(s_pe.h, 1)
    s_pe.inc(nslices)

    @blk.scalar
    def _(s):
        for c in range(nslices):
            bank = pb[c % 4]
            s.wait_ge(s_pe.h, pe_base + c + 1)
            s.copy(zap(c), bank[:, 0:width]).then_inc(s_ev.h, 1)
    s_ev.inc(nslices)


def emit_absmax_row(ctx, srcs, width, mcolf, mcol, mxd, pT, row_dst):
    blk = ctx.block
    s_dv = ctx.sem("dv")
    s_pe = ctx.sem("pe")
    s_ev = ctx.sem("ev")
    s_b = ctx.sem("bnc")
    ntile = width // 128
    d0 = s_dv.v

    @blk.vector
    def _(v, d0=d0):
        for i, ap in enumerate(srcs):
            v.wait_ge(s_dv.h, d0 + i)
            v.reduce_max(mcolf[:, i * width:(i + 1) * width], ap, axis=AX.X,
                         apply_absolute_value=True).then_inc(s_dv.h, 1)
        v.wait_ge(s_dv.h, d0 + len(srcs))
        if len(srcs) == 2:
            v.tensor_max(mcolf[:, 0:width], mcolf[:, 0:width],
                         mcolf[:, width:2 * width]).then_inc(s_dv.h, 1)
        else:
            v.engine_nop().then_inc(s_dv.h, 1)
        v.wait_ge(s_dv.h, d0 + len(srcs) + 1)
        v.tensor_copy(mcol, mcolf[:, 0:width]).then_inc(s_dv.h, 1)
    s_dv.inc(len(srcs) + 2)
    tree_done = s_dv.v

    pe0 = s_pe.v
    dv0 = s_dv.v

    @blk.tensor
    def _(t, tree_done=tree_done, pe0=pe0, dv0=dv0):
        for tt in range(ntile):
            t.wait_ge(s_dv.h, tree_done)
            if tt >= 2:
                t.wait_ge(s_dv.h, dv0 + tt - 1)
            t.transpose(pT[tt % 2][:, 0:128], mcol[:, tt * 128:(tt + 1) * 128],
                        ident_of(ctx)).then_inc(s_pe.h, 1)
    s_pe.inc(ntile)

    @blk.vector
    def _(v, pe0=pe0, dv0=dv0):
        for tt in range(ntile):
            v.wait_ge(s_pe.h, pe0 + tt + 1)
            v.wait_ge(s_dv.h, dv0 + tt)
            v.reduce_max(mxd["f"][:, tt:tt + 1], pT[tt % 2][:, 0:128],
                         axis=AX.X).then_inc(s_dv.h, 1)
    s_dv.inc(ntile)
    red_done = s_dv.v
    d1 = s_dv.inc()

    @blk.vector
    def _(v, red_done=red_done):
        v.wait_ge(s_dv.h, red_done)
        v.tensor_copy(mxd["b"][:, 0:ntile], mxd["f"][:, 0:ntile]).then_inc(s_dv.h, 1)
    pe1 = s_pe.inc()

    @blk.tensor
    def _(t, d1=d1):
        t.wait_ge(s_dv.h, d1)
        t.transpose(pT[0][0:ntile, 0:128], mxd["b"][:, 0:ntile],
                    ident_of(ctx)).then_inc(s_pe.h, 1)
    ev1 = s_ev.inc()

    @blk.scalar
    def _(s, pe1=pe1):
        s.wait_ge(s_pe.h, pe1)
        s.copy(mxd["rows"][0:ntile, :], pT[0][0:ntile, 0:128]).then_inc(s_ev.h, 1)
    scr = ctx.bounce_dram
    s_bq = ctx.sem("bnq")
    bt0 = s_bq.inc(16)
    bt = s_bq.inc(16)

    @blk.gpsimd
    def _(g, ev1=ev1, bt0=bt0):
        g.wait_ge(s_ev.h, ev1)
        g.dma_start(out=scr[0:1, 2048:2048 + width].rearrange(
                        "o (a b) -> (o a) b", b=128),
                    in_=mxd["rows"][0:ntile, :]).then_inc(s_bq.h, 16)
        g.wait_ge(s_bq.h, bt0)
        g.dma_start(out=row_dst, in_=scr[0:1, 2048:2048 + width]
                    ).then_inc(s_bq.h, 16)
    return bt


def emit_quant_chunks(ctx, zap, n, width, scale_rep, sq, wait_first):
    blk = ctx.block
    s_dv = ctx.sem("dv")
    cpack = ctx.t["s_cpack"]
    d0 = s_dv.v

    @blk.vector
    def _(v, d0=d0):
        for c in range(n):
            v.wait_ge(s_dv.h, d0 + 2 * c)
            if c == 0:
                wait_first(v)
            v.tensor_mul(sq[:, 0:width], zap(c), scale_rep).then_inc(s_dv.h, 1)
            v.wait_ge(s_dv.h, d0 + 2 * c + 1)
            v.tensor_scalar(zap(c), sq[:, 0:width],
                            cpack[:, C_MAGIC:C_MAGIC + 1],
                            cpack[:, C_MAGIC:C_MAGIC + 1],
                            op0=ALU.add, op1=ALU.subtract).then_inc(s_dv.h, 1)
    s_dv.inc(2 * n)


def emit_phase_B(ctx, hbm):
    blk = ctx.block
    cpack = ctx.t["s_cpack"]
    sq = ctx.t["sq_scratch"]
    tmb = ctx.t["tm_scratch"]
    NI8 = 64
    RT2 = float(np.sqrt(2048.0))
    RT8 = float(np.sqrt(8192.0))
    NW = float(8192 * 2048)

    s_pe = ctx.sem("pe")
    s_ev = ctx.sem("ev")
    s_dv = ctx.sem("dv")
    s_sc = ctx.sem("sc")
    s_sil = ctx.sem("sil")
    s_po = ctx.sem("po")
    s_ld = ctx.sem("ld")
    s_b = ctx.sem("bnc")
    s_spl = ctx.sem("spl")

    es_ph = ExitStack()
    bstat = ctx.sb("b_stat", [128, 8], F32, es_ph)
    brow = ctx.sb("b_brow", [1, 64], F32, es_ph)
    z03 = ctx.sb("b_z03", [128, 32, T], BF16, es_ph)
    rp = ctx.sb("b_rp", [128, 2048], F32, es_ph)
    t1b = ctx.sb("b_t1", [128, 2048], BF16, es_ph)
    zsp = ctx.zspill_dram

    _ns = ctx.nc.named_scope
    es_scope_p1 = ExitStack(); es_scope_p1.enter_context(_ns("B_p1"))
    es_p1 = ExitStack()
    slab1 = [ctx.sb(f"b_p1s{i}", [128, 8192], F32, es_p1) for i in range(2)]
    acc = ctx.sb("b_acc", [128, 8], F32, es_p1)

    dv0 = s_dv.inc()

    @blk.vector
    def _(v):
        v.memset(brow[:], 0.0)
        v.memset(acc[:, 0:3], 0.0).then_inc(s_dv.h, 1)

    names = ("b_wsh_g", "b_wsh_u", "b_wsh_d")
    plan = [(m, j) for m in range(3) for j in range(2)]
    ld_base = s_ld.v
    sc_base = s_sc.v
    dv_base = s_dv.v

    @blk.sync
    def _(s):
        for i, (m, j) in enumerate(plan):
            s.wait_ge(s_ld.h, ld_base + 16 * i)
            if i >= 2:
                s.wait_ge(s_sc.h, sc_base + i - 1)
            if m < 2:
                src = hbm[names[m]][j * 512:(j + 1) * 512, :].rearrange(
                    "(a p) h -> p a h", p=128)
                dst = slab1[i % 2][:].rearrange("p (a h) -> p a h", a=4)
            else:
                src = hbm[names[m]][j * 128:(j + 1) * 128, :]
                dst = slab1[i % 2][:]
            s.dma_start(out=dst, in_=src).then_inc(s_ld.h, 16)
    s_ld.inc(16 * len(plan))

    @blk.scalar
    def _(s):
        for i, (m, j) in enumerate(plan):
            s.wait_ge(s_ld.h, ld_base + 16 * (i + 1))
            if i >= 2:
                s.wait_ge(s_dv.h, dv_base + i - 1)
            s.activation(slab1[i % 2][:], slab1[i % 2][:], AF.Abs,
                         accum_out=acc[:, 4 + i % 2:5 + i % 2]).then_inc(s_sc.h, 1)
    s_sc.inc(len(plan))

    @blk.vector
    def _(v):
        v.wait_ge(s_dv.h, dv0)
        for i, (m, j) in enumerate(plan):
            v.wait_ge(s_sc.h, sc_base + i + 1)
            v.wait_ge(s_dv.h, dv_base + i)
            v.tensor_add(acc[:, m:m + 1], acc[:, m:m + 1],
                         acc[:, 4 + i % 2:5 + i % 2]).then_inc(s_dv.h, 1)
    s_dv.inc(len(plan))
    acc_done = s_dv.v

    pair_bf = ctx.sb("b_pair", [128, 2], BF16, es_p1)
    ones_bf = ctx.sb("b_ones", [128, 1], BF16, es_p1)
    psum2 = ctx.ps("b_ps2", [1, 2], F32, es_p1)
    d_on = s_dv.inc()

    @blk.vector
    def _(v):
        v.wait_ge(s_dv.h, acc_done)
        v.memset(ones_bf[:], 1.0).then_inc(s_dv.h, 1)
    for m in range(3):
        emit_partition_sum(
            ctx, acc[:, m:m + 1], pair_bf, ones_bf[:], psum2[:],
            brow[0:1, 32:34], brow[0:1, m:m + 1],
            wait_fn=(lambda v, t=d_on: v.wait_ge(s_dv.h, t)))

    d0 = s_dv.v
    s_cast = ctx.sem("cast")
    s_cc = ctx.sem("cc")
    ex0 = s_cast.inc(16)
    cc0 = s_cc.inc()
    ex1 = s_cast.inc(16)

    @blk.gpsimd
    def _(g, d0=d0, ex0=ex0, cc0=cc0):
        g.wait_ge(s_dv.h, d0)
        g.wait_ge(s_cast.h, ex0 - 16)
        g.dma_start(out=ctx.rs_in, in_=brow[0:1, 0:8]).then_inc(s_cast.h, 16)
        g.wait_ge(s_cast.h, ex0)
        g.collective_compute("AllReduce", ALU.add,
                             replica_groups=[list(range(NCORES))],
                             ins=[ctx.rs_in.opt()], outs=[ctx.rs_out.opt()]
                             ).then_inc(s_cc.h)
        g.wait_ge(s_cc.h, cc0)
        g.dma_start(out=brow[0:1, 0:8], in_=ctx.rs_out).then_inc(s_cast.h, 16)

    @blk.vector
    def _(v, d0=d0, ex1=ex1):
        v.wait_ge(s_cast.h, ex1)
        for m in range(3):
            v.wait_ge(s_dv.h, d0 + 2 * m)
            v.tensor_scalar(brow[0:1, 8 + m:9 + m], brow[0:1, m:m + 1],
                            1.0 / NW, 1e-5, op0=ALU.mult, op1=ALU.max
                            ).then_inc(s_dv.h, 1)
            v.wait_ge(s_dv.h, d0 + 2 * m + 1)
            v.reciprocal(brow[0:1, 12 + m:13 + m], brow[0:1, 8 + m:9 + m]
                         ).then_inc(s_dv.h, 1)
    row_done = s_dv.inc(6)
    bstat_t = emit_bcast_rows(ctx, brow[0:1, 8:16], 8, bstat[:, 0:8],
                              wait_fn=lambda e: e.wait_ge(s_dv.h, row_done))

    @blk.scalar
    def _(s):
        s.wait_ge(s_b.h, bstat_t)
    es_p1.close()
    ctx.barrier("b_p1")
    es_scope_p1.close()

    es_scope_w2 = ExitStack(); es_scope_w2.enter_context(_ns("B_w2"))
    es_w2 = ExitStack()
    wslab = [ctx.sb(f"b_w2s{i}", [128, 2048], F32, es_w2) for i in range(2)]
    w2t2 = [ctx.sb(f"b_w2t2{i}", [128, 2048], BF16, es_w2) for i in range(2)]
    w2wT = [ctx.sb(f"b_w2wt{i}", [128, NKC, 128], BF16, es_w2) for i in range(2)]
    pT = [ctx.ps("b_w2p0", [128, 1024], BF16, es_w2),
          ctx.ps("b_w2p1", [128, 1024], BF16, es_w2)]

    wplan = ([("b_wsh_g", i * 128, 0, 0, i * 2048) for i in range(8)]
             + [("b_wsh_u", i * 128, 0, 1, 16384 + i * 2048) for i in range(8)]
             + [("b_wsh_d", o * 128, q * 2048, 2, 32768 + o * 8192 + q * 2048)
                for o in range(2) for q in range(4)])
    ld_b = s_ld.v
    sc_b = s_sc.v
    dv_b = s_dv.v
    cast_b = s_cast.v
    pe_b = s_pe.v
    ev_b = s_ev.v
    for i, (nm, r0, c0, m, dcol) in enumerate(wplan):

        @blk.sync
        def _(s, i=i, nm=nm, r0=r0, c0=c0):
            s.wait_ge(s_ld.h, ld_b + 16 * i)
            if i >= 2:
                s.wait_ge(s_sc.h, sc_b + 3 * (i - 2) + 1)
            s.dma_start(out=wslab[i % 2][:],
                        in_=hbm[nm][r0:r0 + 128, c0:c0 + 2048]
                        ).then_inc(s_ld.h, 16)
        s_ld.inc(16)

        @blk.scalar
        def _(s, i=i, m=m):
            s.wait_ge(s_ld.h, ld_b + 16 * (i + 1))
            if i >= 1:
                s.wait_ge(s_dv.h, dv_b + 2 * i - 1)
            s.activation(rp[:], wslab[i % 2][:], AF.Identity,
                         bias=cpack[:, C_MAGIC:C_MAGIC + 1],
                         scale=bstat[:, 4 + m:5 + m]).then_inc(s_sc.h, 1)
        rp_t = s_sc.inc()

        @blk.vector
        def _(v, i=i, rp_t=rp_t):
            v.wait_ge(s_sc.h, rp_t)
            v.wait_ge(s_dv.h, dv_b + 2 * i)
            v.tensor_scalar(t1b[:], rp[:], cpack[:, C_MAGIC:C_MAGIC + 1], 1.0,
                            op0=ALU.subtract, op1=ALU.min).then_inc(s_dv.h, 1)
            if i >= 2:
                v.wait_ge(s_pe.h, pe_b + 2 * (i - 1))
            v.wait_ge(s_dv.h, dv_b + 2 * i + 1)
            v.tensor_scalar(w2t2[i % 2][:], t1b[:], -1.0, None,
                            op0=ALU.max).then_inc(s_dv.h, 1)
        s_dv.inc(2)

        @blk.tensor
        def _(t, i=i):
            for g in range(2):
                t.wait_ge(s_dv.h, dv_b + 2 * (i + 1))
                if i >= 1:
                    t.wait_ge(s_ev.h, ev_b + 2 * (i - 1) + g + 1)
                last = None
                for j in range(8):
                    kc = g * 8 + j
                    last = t.transpose(pT[g][:, j * 128:(j + 1) * 128],
                                       w2t2[i % 2][:, kc * 128:(kc + 1) * 128],
                                       ident_of(ctx))
                last.then_inc(s_pe.h, 1)
        s_pe.inc(2)

        @blk.scalar
        def _(s, i=i):
            for g in range(2):
                s.wait_ge(s_pe.h, pe_b + 2 * i + g + 1)
                if i >= 2:
                    s.wait_ge(s_cast.h, cast_b + 16 * (i - 1))
                s.copy(w2wT[i % 2][:, g * 8:(g + 1) * 8, :],
                       pT[g][:].rearrange("p (a b) -> p a b", a=8)
                       ).then_inc(s_sc.h, 1)
        s_sc.inc(2)
        ev_cp = s_ev.inc(2)

        @blk.vector
        def _(v, i=i):
            for g in range(2):
                v.wait_ge(s_sc.h, sc_b + 3 * i + 2 + g)
                v.engine_nop().then_inc(s_ev.h, 1)

        @blk.gpsimd
        def _(g_, i=i, dcol=dcol, ev_cp=ev_cp):
            g_.wait_ge(s_ev.h, ev_cp)
            g_.wait_ge(s_cast.h, cast_b + 16 * i)
            g_.dma_start(out=ctx.wsh_dram[:, dcol:dcol + 2048],
                         in_=w2wT[i % 2][:].rearrange("p a b -> p (a b)")
                         ).then_inc(s_cast.h, 16)
        s_cast.inc(16)

    cc_gu = s_cc.inc()

    @blk.gpsimd
    def _(g, cast_b=cast_b):
        g.wait_ge(s_cast.h, cast_b + 16 * len(wplan))
        g.collective_compute("AllGather", ALU.bypass,
                             replica_groups=[list(range(NCORES))],
                             ins=[ctx.wsh_dram.opt()], outs=[ctx.wall_dram.opt()]
                             ).then_inc(s_cc.h)
    es_w2.close()
    es_scope_w2.close()
    es_scope_q1 = ExitStack(); es_scope_q1.enter_context(_ns("B_q1"))

    es_xq = ExitStack()
    xq1 = ctx.sb("b_xq1", [128, NKC, T], BF16, es_xq)
    rep = ctx.sb("b_rep", [128, 3 * T], F32, es_xq)
    brows = ctx.sb("b_rows", [1, 4096], F32, es_xq)
    bmxf = ctx.sb("b_mxf", [128, 8], F32, es_xq)
    bmxb = ctx.sb("b_mxb", [128, 8], BF16, es_xq)
    bmxr = ctx.sb("b_mxr", [8, 128], BF16, es_xq)
    mxd = {"f": bmxf, "b": bmxb, "rows": bmxr}

    es_q1p = ExitStack()
    pT = [ctx.ps("b_q1t0", [128, 1024], BF16, es_q1p),
          ctx.ps("b_q1t1", [128, 1024], BF16, es_q1p)]
    pb = [ctx.ps(f"b_q1b{i}", [128, 512], F32, es_q1p) for i in range(4)]

    emit_ln_to_fm(ctx, xq1, pT)
    emit_fwht_step1(ctx, lambda c: xq1[:, c // 2, (c % 2) * 512:(c % 2 + 1) * 512],
                    NKC * 2, 512, pb)
    ctx.barrier("b_fw1")
    emit_butterflies(ctx, lambda c: xq1[:, c, :], NKC)
    bt1 = emit_absmax_row(ctx, [xq1[:].rearrange("p c t -> p t c")], T,
                          sq[:], tmb[:, 0:T], mxd, pT, brows[0:1, 3072:4096])
    d0 = s_dv.v

    @blk.vector
    def _(v, d0=d0, bt1=bt1):
        v.wait_ge(ctx.sem("bnq").h, bt1)
        v.wait_ge(s_dv.h, d0)
        v.tensor_scalar(brows[0:1, 1024:2048], brows[0:1, 3072:4096],
                        1e-5 * RT2, None, op0=ALU.max).then_inc(s_dv.h, 1)
        v.wait_ge(s_dv.h, d0 + 1)
        v.reciprocal(brows[0:1, 0:1024], brows[0:1, 1024:2048]).then_inc(s_dv.h, 1)
        v.wait_ge(s_dv.h, d0 + 2)
        v.tensor_scalar(brows[0:1, 0:1024], brows[0:1, 0:1024], 127.0, None,
                        op0=ALU.mult).then_inc(s_dv.h, 1)
        v.wait_ge(s_dv.h, d0 + 3)
        v.tensor_scalar(brows[0:1, 1024:2048], brows[0:1, 3072:4096],
                        bstat[0:1, 0:1], 1.0 / (127.0 * RT2),
                        op0=ALU.mult, op1=ALU.mult).then_inc(s_dv.h, 1)
        v.wait_ge(s_dv.h, d0 + 4)
        v.tensor_scalar(brows[0:1, 2048:3072], brows[0:1, 3072:4096],
                        bstat[0:1, 1:2], 1.0 / (127.0 * RT2),
                        op0=ALU.mult, op1=ALU.mult).then_inc(s_dv.h, 1)
    rd = s_dv.inc(5)
    bt2 = emit_bcast_rows(ctx, brows[0:1, 0:3072], 3072, rep[:, 0:3072],
                          wait_fn=lambda e: e.wait_ge(s_dv.h, rd))
    emit_quant_chunks(ctx, lambda c: xq1[:, c, :], NKC, T, rep[:, 0:T],
                      sq, lambda v, bt2=bt2: v.wait_ge(s_b.h, bt2))
    ctx.barrier("b_q1")
    es_q1p.close()
    es_scope_q1.close()
    es_scope_gu = ExitStack(); es_scope_gu.enter_context(_ns("B_gu"))

    es_gu = ExitStack()
    ztmp = ctx.sb("b_ztmp", [128, 2, T], BF16, es_gu)
    wTs = [ctx.sb(f"b_wts{i}", [128, NKC, 128], BF16, es_gu) for i in range(2)]
    gsc = ctx.sb("b_gsc", [128, T], F32, es_gu)
    usc = ctx.sb("b_usc", [128, T], F32, es_gu)
    pb = [ctx.ps(f"b_gub{i}", [128, 512], F32, es_gu) for i in range(4)]

    gnames = ("L0_ffn_w_gate", "L0_ffn_w_up")
    steps = [(oc, m) for oc in range(NI8) for m in range(2)]
    ld_base = s_ld.v
    sc_tern_base = s_sc.v
    tern_dv = []
    tern_po = []
    tr_pe = []
    mm_pe = []
    bank_free = []
    z_done = {}
    spill_t = {}

    for i, (oc, m) in enumerate(steps):
        ld_t = ld_base + 16 * (i + 1)
        rk, sl = oc // 8, oc % 8
        dcol = m * 16384 + sl * 2048

        @blk.sync
        def _(s, i=i, rk=rk, dcol=dcol):
            s.wait_ge(s_ld.h, ld_base + 16 * i)
            if i == 0:
                s.wait_ge(s_cc.h, cc_gu)
            if i >= 2:
                s.wait_ge(s_pe.h, mm_pe[i - 2])
            s.dma_start(out=wTs[i % 2][:],
                        in_=ctx.wall_dram[rk * 128:(rk + 1) * 128,
                                          dcol:dcol + 2048]
                        .rearrange("p (kc j) -> p kc j", j=128)
                        ).then_inc(s_ld.h, 16)
        s_ld.inc(16)

        @blk.tensor
        def _(t, i=i, m=m, ld_t=ld_t):
            t.wait_ge(s_ld.h, ld_t)
            if i >= 2:
                t.wait_ge(s_dv.h, bank_free[i - 2])
            for fc in range(2):
                bank = pb[m * 2 + fc]
                last = None
                for kc in range(NKC):
                    last = t.matmul(bank[:], wTs[i % 2][:, kc, :],
                                    xq1[:, kc, fc * 512:(fc + 1) * 512],
                                    start=(kc == 0), stop=(kc == NKC - 1))
                last.then_inc(s_pe.h, 1)
        s_pe.inc(2)
        mm_pe.append(s_pe.v)

        if m == 0:
            d0 = s_dv.v

            @blk.vector
            def _(v, i=i, oc=oc, d0=d0):
                v.wait_ge(s_dv.h, d0)
                if oc >= 1:
                    v.wait_ge(s_dv.h, z_done[oc - 1])
                for fc in range(2):
                    v.wait_ge(s_pe.h, mm_pe[i] - 1 + fc)
                    v.tensor_mul(gsc[:, fc * 512:(fc + 1) * 512], pb[fc][:],
                                 rep[:, T + fc * 512:T + (fc + 1) * 512]
                                 ).then_inc(s_dv.h, 1)
            s_dv.inc(2)
            bank_free.append(s_dv.v)
            g_dv = s_dv.v
            sil_t = s_sil.inc()

            @blk.scalar
            def _(s, g_dv=g_dv):
                s.wait_ge(s_dv.h, g_dv)
                s.activation(gsc[:], gsc[:], AF.Silu).then_inc(s_sil.h, 1)
        else:
            d0 = s_dv.v
            sil_t = s_sil.v

            @blk.vector
            def _(v, i=i, oc=oc, d0=d0, sil_t=sil_t):
                v.wait_ge(s_dv.h, d0)
                for fc in range(2):
                    v.wait_ge(s_pe.h, mm_pe[i] - 1 + fc)
                    v.tensor_mul(usc[:, fc * 512:(fc + 1) * 512], pb[2 + fc][:],
                                 rep[:, 2 * T + fc * 512:2 * T + (fc + 1) * 512]
                                 ).then_inc(s_dv.h, 1)
                v.wait_ge(s_sil.h, sil_t)
                v.wait_ge(s_dv.h, d0 + 2)
                if oc >= 34:
                    v.wait_ge(s_spl.h, spill_t[oc - 2])
                dst = z03[:, oc, :] if oc < 32 else ztmp[:, oc % 2, :]
                v.tensor_mul(dst, gsc[:], usc[:]).then_inc(s_dv.h, 1)
            s_dv.inc(3)
            bank_free.append(s_dv.v)
            z_done[oc] = s_dv.v
            if oc >= 32:
                zt = s_dv.v
                spl = s_spl.inc(16)

                @blk.scalar
                def _(s, oc=oc, zt=zt):
                    s.wait_ge(s_dv.h, zt)
                    if oc >= 33:
                        s.wait_ge(s_spl.h, spill_t[oc - 1])
                    s.dma_start(out=zsp[(oc - 32) * 128:(oc - 31) * 128, :],
                                in_=ztmp[:, oc % 2, :]).then_inc(s_spl.h, 16)
                spill_t[oc] = spl

    ctx.barrier("b_gu")
    es_gu.close()
    es_xq.close()
    es_scope_gu.close()

    for half in range(2):
        es_scope_dn = ExitStack(); es_scope_dn.enter_context(_ns(f"B_dn{half}"))
        es_dn = ExitStack()
        zh = ctx.sb(f"b_zh{half}", [128, 32, 512], BF16, es_dn)
        brows2 = ctx.sb(f"b_rw2{half}", [1, 2048], F32, es_dn)
        rep2 = ctx.sb(f"b_rep2{half}", [128, 1024], F32, es_dn)
        bmxf = ctx.sb(f"b_mxf{half}", [128, 8], F32, es_dn)
        bmxb = ctx.sb(f"b_mxb{half}", [128, 8], BF16, es_dn)
        bmxr = ctx.sb(f"b_mxr{half}", [8, 128], BF16, es_dn)
        mxd = {"f": bmxf, "b": bmxb, "rows": bmxr}
        wTd = [ctx.sb(f"b_dwt{half}{i}", [128, NI8, 128], BF16, es_dn)
               for i in range(2)]
        o_fm = ctx.sb(f"b_ofm{half}", [128, NKC, 512], BF16, es_dn)
        pT = [ctx.ps(f"b_dnt{half}0", [128, 1024], BF16, es_dn),
              ctx.ps(f"b_dnt{half}1", [128, 1024], BF16, es_dn)]
        pb = [ctx.ps(f"b_dnb{half}{i}", [128, 512], F32, es_dn) for i in range(4)]
        ts0 = half * 512

        def zap(c, ts0=ts0, zh=zh):
            return z03[:, c, ts0:ts0 + 512] if c < 32 else zh[:, c - 32, :]

        spl_all = s_spl.v
        ld_t = s_ld.inc(16)

        ld_pre = ld_t - 16

        @blk.sync
        def _(s, ts0=ts0, zh=zh, spl_all=spl_all, ld_pre=ld_pre):
            s.wait_ge(s_spl.h, spl_all)
            s.wait_ge(s_ld.h, ld_pre)
            s.dma_start(out=zh[:],
                        in_=zsp.rearrange("(c p) t -> p c t", p=128)[:, :, ts0:ts0 + 512]
                        ).then_inc(s_ld.h, 16)

        for dec in (blk.tensor, blk.vector, blk.scalar, blk.gpsimd):
            @dec
            def _(e, ld_t=ld_t):
                e.wait_ge(s_ld.h, ld_t)

        emit_fwht_step1(ctx, zap, NI8, 512, pb)
        ctx.barrier(f"b_fw2_{half}")
        emit_butterflies(ctx, zap, NI8)
        bt1 = emit_absmax_row(
            ctx,
            [z03[:, :, ts0:ts0 + 512].rearrange("p c t -> p t c"),
             zh[:].rearrange("p c t -> p t c")],
            512, sq[:], tmb[:, 0:512], mxd, pT, brows2[0:1, 1536:2048])
        d0 = s_dv.v

        @blk.vector
        def _(v, d0=d0, brows2=brows2, bt1=bt1):
            v.wait_ge(ctx.sem("bnq").h, bt1)
            v.wait_ge(s_dv.h, d0)
            v.tensor_scalar(brows2[0:1, 1024:1536], brows2[0:1, 1536:2048],
                            1e-5 * RT8, None, op0=ALU.max).then_inc(s_dv.h, 1)
            v.wait_ge(s_dv.h, d0 + 1)
            v.reciprocal(brows2[0:1, 0:512], brows2[0:1, 1024:1536]).then_inc(s_dv.h, 1)
            v.wait_ge(s_dv.h, d0 + 2)
            v.tensor_scalar(brows2[0:1, 0:512], brows2[0:1, 0:512], 127.0, None,
                            op0=ALU.mult).then_inc(s_dv.h, 1)
            v.wait_ge(s_dv.h, d0 + 3)
            v.tensor_scalar(brows2[0:1, 512:1024], brows2[0:1, 1536:2048],
                            bstat[0:1, 2:3], 1.0 / (127.0 * RT8),
                            op0=ALU.mult, op1=ALU.mult).then_inc(s_dv.h, 1)
        rd = s_dv.inc(4)
        bt2 = emit_bcast_rows(ctx, brows2[0:1, 0:1024], 1024, rep2[:, 0:1024],
                              wait_fn=lambda e, rd=rd: e.wait_ge(s_dv.h, rd))
        emit_quant_chunks(ctx, zap, NI8, 512, rep2[:, 0:512], sq,
                          lambda v, bt2=bt2: v.wait_ge(s_b.h, bt2))
        ctx.barrier(f"b_q2_{half}")

        ld_base2 = s_ld.v
        mm_pe2 = {}
        out_dv2 = {}
        for oc in range(NKC):
            rk2, l2 = oc // 2, oc % 2
            dcol = 32768 + l2 * 8192

            @blk.sync
            def _(s, oc=oc, rk2=rk2, dcol=dcol):
                s.wait_ge(s_ld.h, ld_base2 + 16 * oc)
                if oc >= 2:
                    s.wait_ge(s_pe.h, mm_pe2[oc - 2])
                s.dma_start(out=wTd[oc % 2][:],
                            in_=ctx.wall_dram[rk2 * 128:(rk2 + 1) * 128,
                                              dcol:dcol + 8192]
                            .rearrange("p (kc j) -> p kc j", j=128)
                            ).then_inc(s_ld.h, 16)
            s_ld.inc(16)

            @blk.tensor
            def _(t, oc=oc):
                t.wait_ge(s_ld.h, ld_base2 + 16 * (oc + 1))
                if oc >= 2:
                    t.wait_ge(s_dv.h, out_dv2[oc - 2])
                bank = pb[oc % 2]
                last = None
                for kc in range(NI8):
                    last = t.matmul(bank[:], wTd[oc % 2][:, kc, :], zap(kc),
                                    start=(kc == 0), stop=(kc == NI8 - 1))
                last.then_inc(s_pe.h, 1)
            mmt = s_pe.inc()
            mm_pe2[oc] = mmt
            d0 = s_dv.v

            @blk.vector
            def _(v, oc=oc, mmt=mmt, d0=d0):
                v.wait_ge(s_pe.h, mmt)
                v.wait_ge(s_dv.h, d0)
                v.tensor_mul(o_fm[:, oc, :], pb[oc % 2][:],
                             rep2[:, 512:1024]).then_inc(s_dv.h, 1)
            s_dv.inc(1)
            out_dv2[oc] = s_dv.v

        ctx.barrier(f"b_dn_{half}")
        emit_out_epilogue(
            ctx,
            lambda fc, tt, half=half, o_fm=o_fm:
                o_fm[:, fc, (tt - half * 4) * 128:(tt - half * 4 + 1) * 128],
            pT, tiles=range(half * 4, half * 4 + 4))
        es_dn.close()
        es_scope_dn.close()

    es_ph.close()



def emit_phase_D(ctx, hbm):
    blk = ctx.block
    cpack = ctx.t["s_cpack"]
    sq = ctx.t["sq_scratch"]
    spack = ctx.t["s_d_spack"]
    NI = 48

    es_ph = ExitStack()
    xn_fm = ctx.sb("d_xnfm", [128, NKC, T], BF16, es_ph)
    fuse = ctx.sb("d_fuse", [128, NI, 512], BF16, es_ph)
    o_fm = ctx.sb("d_ofm", [128, NKC, 512], BF16, es_ph)

    s_pe = ctx.sem("pe")
    s_ev = ctx.sem("ev")
    s_dv = ctx.sem("dv")
    s_sc = ctx.sem("sc")
    s_cast = ctx.sem("cast")

    es_ln = ExitStack()
    pT = [ctx.ps("d_pt0", [128, 1024], BF16, es_ln),
          ctx.ps("d_pt1", [128, 1024], BF16, es_ln)]
    emit_ln_to_fm(ctx, xn_fm, pT)
    es_ln.close()

    for half in range(2):
        ts0 = half * 512
        es_gu = ExitStack()
        g8 = [ctx.sb(f"d_g8{half}{i}", [128, NKC, 256], FP8, es_gu) for i in range(2)]
        u8 = [ctx.sb(f"d_u8{half}{i}", [128, NKC, 256], FP8, es_gu) for i in range(2)]
        pb = [ctx.ps(f"d_pb{half}{i}", [128, 512], F32, es_gu) for i in range(8)]
        NOG = 24

        cast_base = s_cast.v
        pe_base = s_pe.v
        sc_base = s_sc.v
        dv_base = s_dv.v

        @blk.gpsimd
        def _(g, half=half, cast_base=cast_base, pe_base=pe_base):
            for og in range(NOG):
                g.wait_ge(s_cast.h, cast_base + 32 * og)
                if False:
                    pass
                if og >= 2:
                    g.wait_ge(s_pe.h, pe_base + 4 * (og - 1))
                g.dma_start(
                    out=g8[og % 2][:],
                    in_=hbm["L1_ffn_gate_w"][:, og * 256:(og + 1) * 256]
                    .rearrange("(kc p) o -> p kc o", p=128)).then_inc(s_cast.h, 16)
                g.wait_ge(s_cast.h, cast_base + 32 * og + 16)
                g.dma_start(
                    out=u8[og % 2][:],
                    in_=hbm["L1_ffn_up_w"][:, og * 256:(og + 1) * 256]
                    .rearrange("(kc p) o -> p kc o", p=128)).then_inc(s_cast.h, 16)
        s_cast.inc(32 * NOG)

        @blk.tensor
        def _(t, half=half, cast_base=cast_base, dv_base=dv_base):
            for og in range(NOG):
                t.wait_ge(s_cast.h, cast_base + 32 * (og + 1))
                if og >= 2:
                    t.wait_ge(s_dv.h, dv_base + 2 * (og - 1))
                for which, wb in ((0, g8[og % 2]), (1, u8[og % 2])):
                    for oc2 in range(2):
                        bank = pb[(og % 2) * 4 + which * 2 + oc2]
                        last = None
                        for kc in range(NKC):
                            last = t.matmul(
                                bank[:], wb[:, kc, oc2 * 128:(oc2 + 1) * 128],
                                xn_fm[:, kc, ts0:ts0 + 512],
                                start=(kc == 0), stop=(kc == NKC - 1))
                        last.then_inc(s_pe.h, 1)
        s_pe.inc(4 * NOG)

        @blk.scalar
        def _(sc_e, half=half, pe_base=pe_base, sc_base=sc_base, dv_base=dv_base):
            for og in range(NOG):
                for oc2 in range(2):
                    oc = og * 2 + oc2
                    gbank = pb[(og % 2) * 4 + oc2]
                    ubank = pb[(og % 2) * 4 + 2 + oc2]
                    sc_e.wait_ge(s_pe.h, pe_base + 4 * og + 4)
                    if og >= 1:
                        sc_e.wait_ge(s_dv.h, dv_base + 2 * (og - 1) + oc2 + 1)
                    sg = sq[:, oc2 * 1024:oc2 * 1024 + 512]
                    su = sq[:, oc2 * 1024 + 512:oc2 * 1024 + 1024]
                    sc_e.activation(sg, gbank[:], AF.Silu,
                                    scale=spack[:, oc:oc + 1])
                    sc_e.activation(su, ubank[:], AF.Copy,
                                    scale=spack[:, 48 + oc:48 + oc + 1]
                                    ).then_inc(s_sc.h, 1)
        s_sc.inc(2 * NOG)

        @blk.vector
        def _(v, half=half, sc_base=sc_base):
            for og in range(NOG):
                for oc2 in range(2):
                    oc = og * 2 + oc2
                    v.wait_ge(s_sc.h, sc_base + 2 * og + oc2 + 1)
                    sg = sq[:, oc2 * 1024:oc2 * 1024 + 512]
                    su = sq[:, oc2 * 1024 + 512:oc2 * 1024 + 1024]
                    v.tensor_mul(fuse[:, oc, :], sg, su).then_inc(s_dv.h, 1)
        s_dv.inc(2 * NOG)

        ctx.barrier(f"d_gu_{half}")
        es_gu.close()

        es_dn = ExitStack()
        d8 = [ctx.sb(f"d_d8{half}{i}", [128, NI, 256], FP8, es_dn) for i in range(2)]
        pb = [ctx.ps(f"d_pd{half}{i}", [128, 512], F32, es_dn) for i in range(4)]
        NOGD = 8
        cast_base = s_cast.v
        pe_base = s_pe.v
        sc_base = s_sc.v

        @blk.gpsimd
        def _(g, half=half, pe_base=pe_base, cast_base=cast_base):
            for og in range(NOGD):
                g.wait_ge(s_cast.h, cast_base + 16 * og)
                if og >= 2:
                    g.wait_ge(s_pe.h, pe_base + 2 * (og - 1))
                g.dma_start(
                    out=d8[og % 2][:],
                    in_=hbm["L1_ffn_down_w"][:, og * 256:(og + 1) * 256]
                    .rearrange("(kc p) o -> p kc o", p=128)).then_inc(s_cast.h, 16)
        s_cast.inc(16 * NOGD)

        @blk.tensor
        def _(t, half=half, cast_base=cast_base, sc_base=sc_base):
            for og in range(NOGD):
                t.wait_ge(s_cast.h, cast_base + 16 * (og + 1))
                for oc2 in range(2):
                    if og >= 2:
                        t.wait_ge(s_sc.h, sc_base + 2 * (og - 2) + oc2 + 1)
                    bank = pb[(og % 2) * 2 + oc2]
                    wb = d8[og % 2]
                    last = None
                    for kc in range(NI):
                        last = t.matmul(bank[:], wb[:, kc, oc2 * 128:(oc2 + 1) * 128],
                                        fuse[:, kc, :],
                                        start=(kc == 0), stop=(kc == NI - 1))
                    last.then_inc(s_pe.h, 1)
        s_pe.inc(2 * NOGD)

        @blk.scalar
        def _(sc_e, half=half, pe_base=pe_base):
            for og in range(NOGD):
                for oc2 in range(2):
                    oc = og * 2 + oc2
                    bank = pb[(og % 2) * 2 + oc2]
                    sc_e.wait_ge(s_pe.h, pe_base + 2 * og + oc2 + 1)
                    sc_e.activation(o_fm[:, oc, :], bank[:], AF.Copy,
                                    scale=spack[:, 96 + oc:96 + oc + 1]
                                    ).then_inc(s_sc.h, 1)
        s_sc.inc(2 * NOGD)

        ctx.barrier(f"d_dn_{half}")
        es_dn.close()

        es_ep = ExitStack()
        pT = [ctx.ps(f"d_et{half}0", [128, 1024], BF16, es_ep),
              ctx.ps(f"d_et{half}1", [128, 1024], BF16, es_ep)]
        emit_out_epilogue(
            ctx, lambda fc, tt: o_fm[:, fc, (tt - half * 4) * 128:(tt - half * 4 + 1) * 128],
            pT, tiles=range(half * 4, half * 4 + 4))
        es_ep.close()

    es_ph.close()



def build_program(phases, want_input=True, want_output=True):
    nc = bass.Bass("TRN2", target_bir_lowering=False)
    hbm = {}

    def din(name, shape):
        hbm[name] = nc.dram_tensor(name, list(shape), F32, kind="ExternalInput").ap()

    din("x", (T, D))
    din("cpack", (128, 8))
    hbm["ident"] = nc.dram_tensor("ident", [128, 128], BF16, kind="ExternalInput").ap()
    y = nc.dram_tensor("y", [T, D], F32, kind="ExternalOutput").ap()

    if "C" in phases:
        din("L1_attn_B", (1024, 2048))
        din("L1_attn_C", (2048, 1024))
        din("c_dpack", (128, 16))
    if "D" in phases:
        din("L1_ffn_gate_w", (2048, 6144))
        din("L1_ffn_up_w", (2048, 6144))
        din("L1_ffn_down_w", (6144, 2048))
        din("d_spack", (128, 112))
    if "A" in phases:
        din("L0_attn_w_qkv", (2048, 6144))
        din("L0_attn_w_lm_qkv", (2048, 6144))
        din("L0_attn_w_lm_out", (2048, 2048))
        din("L0_attn_w_l2t_q", (2048, 2048))
        din("L0_attn_w_l2t_out", (2048, 2048))
        din("L0_attn_w_lm_kv", (2048, 4096))
        din("L0_attn_w_out", (2048, 2048))
        din("L0_attn_w_gate", (2048, 32))
        din("a_bias", (128, 208))
        din("a_rope", (128, 2048))
        din("a_mloc", (64, 64))
        din("a_mlat", (128, 128))
        din("a_gmask", (1024, 256))
        din("a_pswap", (128, 128))
        din("a_poolq", (128, 128))
        din("a_id2", (64, 128))
    if "B" in phases:
        din("b_wsh_g", (1024, 2048))
        din("b_wsh_u", (1024, 2048))
        din("b_wsh_d", (256, 8192))
        hbm["h128"] = nc.dram_tensor("h128", [128, 128], BF16,
                                     kind="ExternalInput").ap()

    es = ExitStack()
    with nc.Block() as block:
        ctx = Ctx(nc, block, es)
        ctx.sb("x_tile", [128, 2048], F32)
        ctx.sb("s_ident", [128, 128], BF16)
        ctx.sb("s_cpack", [128, 8], F32)
        ctx.sb("sdummy", [1, 4], F32)
        ctx.sb("sq_scratch", [128, 2048], F32)
        ctx.sb("stats", [128, 8], F32)
        ctx.sb("tm_scratch", [128, 2048], BF16)
        if "C" in phases:
            ctx.sb("s_c_dpack", [128, 16], F32)
        if "D" in phases:
            ctx.sb("s_d_spack", [128, 112], F32)
        if "A" in phases:
            ctx.sb("s_abias", [128, 208], F32)
            ctx.qspill_dram = nc.dram_tensor("a_qsp", [128, 16384], BF16,
                                             kind="Internal").ap()
            ctx.ag1_in = nc.dram_tensor("a_ag1i", [128, 1024], BF16,
                                        kind="Internal").ap()
            ctx.ag1_out = nc.dram_tensor("a_ag1o", [256, 1024], BF16,
                                         kind="Internal").ap()
            ctx.ag2_in = nc.dram_tensor("a_ag2i", [128, 2048], BF16,
                                        kind="Internal").ap()
            ctx.ag2_out = nc.dram_tensor("a_ag2o", [256, 2048], BF16,
                                         kind="Internal").ap()
        if "B" in phases:
            ctx.sb("s_h128", [128, 128], BF16)
            ctx.bounce_dram = nc.dram_tensor("b_bnc", [1, 4096], F32,
                                             kind="Internal").ap()
            ctx.zspill_dram = nc.dram_tensor("b_zsp", [4096, 1024], BF16,
                                             kind="Internal").ap()
            ctx.wsh_dram = nc.dram_tensor("b_wshd", [128, 49152], BF16,
                                          kind="Internal").ap()
            ctx.wall_dram = nc.dram_tensor("b_wall", [1024, 49152], BF16,
                                           kind="Internal",
                                           addr_space="Shared").ap()
            ctx.rs_in = nc.dram_tensor("b_rsi", [1, 8], F32,
                                       kind="Internal").ap()
            ctx.rs_out = nc.dram_tensor("b_rso", [1, 8], F32,
                                        kind="Internal").ap()

        s_ld = ctx.sem("ld")
        s_st = ctx.sem("st")

        @block.sync
        def _(s):
            s.dma_start(out=ctx.t["s_ident"][:], in_=hbm["ident"]).then_inc(s_ld.h, 16)
            s.dma_start(out=ctx.t["s_cpack"][:], in_=hbm["cpack"]).then_inc(s_ld.h, 16)
            if "C" in phases:
                s.dma_start(out=ctx.t["s_c_dpack"][:], in_=hbm["c_dpack"]).then_inc(s_ld.h, 16)
            if "D" in phases:
                s.dma_start(out=ctx.t["s_d_spack"][:], in_=hbm["d_spack"]).then_inc(s_ld.h, 16)
            if "B" in phases:
                s.dma_start(out=ctx.t["s_h128"][:], in_=hbm["h128"]).then_inc(s_ld.h, 16)
            if "A" in phases:
                s.dma_start(out=ctx.t["s_abias"][:], in_=hbm["a_bias"]).then_inc(s_ld.h, 16)
        n_pre = (2 + (1 if "C" in phases else 0) + (1 if "D" in phases else 0)
                 + (1 if "B" in phases else 0) + (1 if "A" in phases else 0))
        s_ld.inc(16 * n_pre)
        pre_target = s_ld.v

        @block.scalar
        def _(e):
            e.memzero(ctx.t["sdummy"][:])

        for dec in (block.sync, block.tensor, block.vector, block.scalar, block.gpsimd):
            @dec
            def _(e):
                e.wait_ge(s_ld.h, pre_target)

        xh = nc.dram_tensor("xh", [T, D], F32, kind="Internal").ap() \
            if len(phases) > 1 else None
        for i, ph in enumerate(phases):
            ctx.x_src = hbm["x"] if i == 0 else xh
            ctx.x_dst = y if i == len(phases) - 1 else xh
            if ph == "A":
                emit_phase_A(ctx, hbm)
            elif ph == "B":
                emit_phase_B(ctx, hbm)
            elif ph == "C":
                emit_phase_C(ctx, hbm)
            elif ph == "D":
                emit_phase_D(ctx, hbm)

        @block.sync
        def _(s):
            s.wait_ge(s_st.h, s_st.v)

    es.close()
    return nc



def _consts_pack():
    c = np.zeros((128, 8), np.float32)
    c[:, C_MAGIC] = MAGIC
    c[:, C_NMAGIC] = -MAGIC
    c[:, C_EPS] = 1e-5
    c[:, C_ONE] = 1.0
    c[:, C_NEG1E30] = -1e30
    c[:, C_127] = 127.0
    c[:, C_EPS9] = 1e-9
    return c


def _ident_bf():
    return np.eye(128).astype(ml_dtypes.bfloat16)


def _a_packs(core):
    h = core % 2
    DH, M, S = 64, 4, 64
    pos = (h * 1024 + np.arange(1024)).astype(np.float64)
    f = np.arange(128)
    inv = 1.0 / (10000.0 ** (2 * ((f % 64) // 2) / 64.0))
    ang = pos[None, :] * inv[:, None]
    cos = np.cos(ang)
    sgn = np.where(f % 2 == 0, -1.0, 1.0)
    sin = np.sin(ang) * sgn[:, None]
    rope = np.concatenate([cos, sin], 1).astype(np.float32)
    psw = np.zeros((128, 128), np.float32)
    psw[np.arange(128), np.arange(128) ^ 1] = 1.0
    NEG = -1e30
    mloc = np.where(np.tril(np.ones((64, 64), bool)), 0.0, NEG).astype(np.float32)
    li = np.arange(128)
    mlat = np.where((li[:, None] // 4) >= (li[None, :] // 4), 0.0, NEG
                    ).astype(np.float32)
    gm = np.zeros((1024, 256), np.float32)
    for tt in range(8):
        tok = h * 1024 + tt * 128 + np.arange(128)
        tb = tok // 64
        valid = tb[:, None] > (li[None, :] // 4)
        gm[tt * 128:(tt + 1) * 128, 0:128] = np.where(valid, 0.0, NEG)
        gm[tt * 128:(tt + 1) * 128, 128:256] = valid.astype(np.float32)
    id2 = np.concatenate([np.eye(64), np.eye(64)], 1).astype(np.float32)
    return dict(a_rope=rope, a_pswap=psw, a_mloc=mloc, a_mlat=mlat, a_gmask=gm,
                a_id2=id2)


def _a_bias_pack(inputs):
    b = np.zeros((128, 208), np.float32)

    def put(vec, c0):
        v = np.asarray(vec, np.float32)
        n = v.size // 128
        b[:, c0:c0 + n] = v.reshape(n, 128).T
    put(inputs["L0_attn_b_qkv"], 0)
    put(inputs["L0_attn_b_lm_qkv"], 48)
    put(inputs["L0_attn_b_lm_out"], 96)
    put(inputs["L0_attn_b_l2t_q"], 112)
    put(inputs["L0_attn_b_l2t_out"], 128)
    put(inputs["L0_attn_b_lm_kv"], 144)
    put(inputs["L0_attn_b_out"], 176)
    b[0:32, 192] = np.asarray(inputs["L0_attn_b_gate"], np.float32)
    return b


def _a_poolq_pack(inputs):
    pq = np.asarray(inputs["L0_attn_pool_q"], np.float32).reshape(4, 32, 64)
    out = np.zeros((128, 128), np.float32)
    for hp in range(16):
        for m in range(4):
            out[0:64, hp * 8 + m] = pq[m, 2 * hp, :]
            out[64:128, hp * 8 + 4 + m] = pq[m, 2 * hp + 1, :]
    return out


def _h128_bf():
    i = np.arange(128)
    pc = np.array([bin(v).count("1") for v in range(128)])
    h = 1.0 - 2.0 * (pc[i[:, None] & i[None, :]] % 2)
    return h.astype(ml_dtypes.bfloat16)


def kernel(**inputs):
    return _kernel(inputs, groups=["ABCD"])


def _kernel(inputs, groups):
    x_full = np.ascontiguousarray(inputs["L0_attn_x"], dtype=np.float32)
    B, S, _ = x_full.shape
    cpack = _consts_pack()
    ident = _ident_bf()

    core_common = []
    for c in range(NCORES):
        s, h = c // 2, c % 2
        core_common.append(dict(
            cpack=cpack, ident=ident,
        ))

    x_cur = x_full.reshape(NCORES, T, D).copy()
    for gi, phases in enumerate(groups):
        nc = build_program(phases)
        in_maps = []
        for c in range(NCORES):
            m = dict(core_common[c])
            m["x"] = x_cur[c]
            if "C" in phases:
                m["L1_attn_B"] = np.ascontiguousarray(inputs["L1_attn_B"], np.float32)
                m["L1_attn_C"] = np.ascontiguousarray(inputs["L1_attn_C"], np.float32)
                m["c_dpack"] = np.ascontiguousarray(
                    inputs["L1_attn_D"].reshape(16, 128).T, np.float32)
            if "A" in phases:
                for nm_ in ("w_qkv", "w_lm_qkv", "w_lm_out", "w_l2t_q",
                            "w_l2t_out", "w_lm_kv", "w_out", "w_gate"):
                    m["L0_attn_" + nm_] = np.ascontiguousarray(
                        inputs["L0_attn_" + nm_], np.float32)
                m.update(_a_packs(c))
                m["a_bias"] = _a_bias_pack(inputs)
                m["a_poolq"] = _a_poolq_pack(inputs)
            if "B" in phases:
                m["b_wsh_g"] = np.ascontiguousarray(
                    inputs["L0_ffn_w_gate"][c * 1024:(c + 1) * 1024, :], np.float32)
                m["b_wsh_u"] = np.ascontiguousarray(
                    inputs["L0_ffn_w_up"][c * 1024:(c + 1) * 1024, :], np.float32)
                m["b_wsh_d"] = np.ascontiguousarray(
                    inputs["L0_ffn_w_down"][c * 256:(c + 1) * 256, :], np.float32)
                m["h128"] = _h128_bf()
            if "D" in phases:
                m["L1_ffn_gate_w"] = np.ascontiguousarray(inputs["L1_ffn_gate_w"], np.float32)
                m["L1_ffn_up_w"] = np.ascontiguousarray(inputs["L1_ffn_up_w"], np.float32)
                m["L1_ffn_down_w"] = np.ascontiguousarray(inputs["L1_ffn_down_w"], np.float32)
                sp = np.zeros((128, 112), np.float32)
                sp[:, 0:48] = inputs["L1_ffn_gate_scale"].reshape(48, 128).T
                sp[:, 48:96] = inputs["L1_ffn_up_scale"].reshape(48, 128).T
                sp[:, 96:112] = inputs["L1_ffn_down_scale"].reshape(16, 128).T
                m["d_spack"] = sp
            in_maps.append(m)
        res = run_bass_kernel_spmd(nc, in_maps, core_ids=list(range(NCORES)))
        x_cur = np.stack([np.asarray(r["y"]) for r in res.results], 0)

    return x_cur.reshape(B, S, D).astype(np.float32)


# version 06
# speedup vs baseline: 2.3910x; 1.2025x over previous
import numpy as np
import ml_dtypes
from contextlib import ExitStack

import concourse.bass as bass
import concourse.mybir as mybir
from concourse.bass_utils import run_bass_kernel_spmd

F32 = mybir.dt.float32
BF16 = mybir.dt.bfloat16
FP8 = mybir.dt.float8e4
AF = mybir.ActivationFunctionType
ALU = mybir.AluOpType
AX = mybir.AxisListType

D = 2048
T = 1024
NT = T // 128
NKC = D // 128
MAGIC = 12582912.0

C_MAGIC, C_NMAGIC, C_EPS, C_ONE, C_NEG1E30, C_127, C_EPS9, C_ZERO = range(8)

NCORES = 8


_TAPS = [False]


class Sem:
    def __init__(self, h):
        self.h = h
        self.v = 0

    def inc(self, n=1):
        self.v += n
        return self.v


class Ctx:
    def __init__(self, nc, block, es):
        self.nc = nc
        self.block = block
        self.es = es
        self._sems = {}
        self.t = {}

    def sem(self, name) -> Sem:
        if name not in self._sems:
            h = self.es.enter_context(self.nc.semaphore(name))
            self._sems[name] = Sem(h)
        return self._sems[name]

    def sb(self, name, shape, dt, scope=None):
        h = (scope or self.es).enter_context(self.nc.sbuf_tensor(name, shape, dt))
        self.t[name] = h
        return h

    def ps(self, name, shape, dt, scope=None):
        h = (scope or self.es).enter_context(self.nc.psum_tensor(name, shape, dt))
        self.t[name] = h
        return h

    def tap(self, name, ap):
        if not _TAPS[0]:
            return
        nc, blk = self.nc, self.block
        d = nc.dram_tensor("tap_" + name, list(ap.shape), ap.dtype,
                           kind="ExternalOutput").ap()
        s_t = self.sem("tapq")
        t0 = s_t.inc(16)

        @blk.sync
        def _(s, d=d, ap=ap, t0=t0):
            s.dma_start(out=d, in_=ap).then_inc(s_t.h, 16)
            s.wait_ge(s_t.h, t0)

    def barrier(self, tag):
        bar = self.sem("bar")
        target = bar.v + 5
        blk = self.block
        dummy = self.t["sdummy"]

        @blk.sync
        def _(e):
            e.nop().then_inc(bar.h, 1)
            e.wait_ge(bar.h, target)

        @blk.tensor
        def _(e):
            e.nop().then_inc(bar.h, 1)
            e.wait_ge(bar.h, target)

        @blk.vector
        def _(e):
            e.engine_nop().then_inc(bar.h, 1)
            e.wait_ge(bar.h, target)

        @blk.scalar
        def _(e):
            e.copy(dummy[0:1, 0:1], dummy[0:1, 1:2]).then_inc(bar.h, 1)
            e.wait_ge(bar.h, target)

        @blk.gpsimd
        def _(e):
            e.engine_nop().then_inc(bar.h, 1)
            e.wait_ge(bar.h, target)

        bar.v = target



def emit_ln_tile(ctx, src_ap_f, dst_ap, sq_scratch, stats, cpack, wait_before=None,
                 done_sem=None, src_is_bf16=False):
    blk = ctx.block
    v_sem = ctx.sem("lnv")
    s_sem = ctx.sem("lns")
    vt = v_sem.inc()

    @blk.vector
    def _(v):
        if wait_before is not None:
            wait_before(v)
        v.reduce_sum(stats[:, 0:1], src_ap_f, axis=AX.X).then_inc(v_sem.h, 1)

    st = s_sem.inc()

    @blk.scalar
    def _(sc):
        if wait_before is not None:
            wait_before(sc)
        sc.activation(sq_scratch[:], src_ap_f, AF.Square,
                      accum_out=stats[:, 1:2]).then_inc(s_sem.h, 1)

    vt2 = v_sem.inc(3)

    @blk.vector
    def _(v):
        v.wait_ge(s_sem.h, st)
        v.wait_ge(v_sem.h, vt)
        v.tensor_scalar_mul(stats[:, 2:3], stats[:, 0:1], 1.0 / 2048.0).then_inc(v_sem.h, 1)
        v.wait_ge(v_sem.h, vt + 1)
        v.tensor_scalar(stats[:, 4:5], stats[:, 2:3], stats[:, 2:3], None,
                        op0=ALU.mult).then_inc(v_sem.h, 1)
        v.wait_ge(v_sem.h, vt + 2)
        v.tensor_scalar(stats[:, 5:6], stats[:, 1:2], 1.0 / 2048.0, stats[:, 4:5],
                        op0=ALU.mult, op1=ALU.subtract).then_inc(v_sem.h, 1)

    st2 = s_sem.inc()

    @blk.scalar
    def _(sc):
        sc.wait_ge(v_sem.h, vt2)
        sc.activation(stats[:, 6:7], stats[:, 5:6], AF.Sqrt,
                      bias=cpack[:, C_EPS:C_EPS + 1]).then_inc(s_sem.h, 1)

    vt3 = v_sem.inc()

    @blk.vector
    def _(v):
        v.wait_ge(s_sem.h, st2)
        v.reciprocal(stats[:, 7:8], stats[:, 6:7]).then_inc(v_sem.h, 1)
        v.wait_ge(v_sem.h, vt3)
        ts = v.tensor_scalar(dst_ap, src_ap_f, stats[:, 2:3], stats[:, 7:8],
                             op0=ALU.subtract, op1=ALU.mult)
        if done_sem is not None:
            ts.then_inc(done_sem.h, 1)
    if done_sem is not None:
        done_sem.inc()


def emit_tm_to_fm(ctx, tm_tiles, fm_dst, pT, ident, pe_sem, ev_sem, tag,
                  wait_pe=None, tile_done=None):
    blk = ctx.block
    ngroups = NT // 4
    plan = [(fc, g) for fc in range(NKC) for g in range(ngroups)]
    pe_base = pe_sem.v
    ev_base = ev_sem.v

    @blk.tensor
    def _(t):
        for i, (fc, g) in enumerate(plan):
            bank = pT[i % 2]
            if i >= 2:
                t.wait_ge(ev_sem.h, ev_base + i - 1)
            if wait_pe is not None:
                wait_pe(t, fc, g)
            last = None
            for j in range(4):
                tt = g * 4 + j
                src = tm_tiles[tt]
                last = t.transpose(bank[:, j * 128:(j + 1) * 128],
                                   src[:, fc * 128:(fc + 1) * 128], ident[:])
            last.then_inc(pe_sem.h, 1)
    pe_sem.inc(len(plan))

    @blk.scalar
    def _(sc):
        for i, (fc, g) in enumerate(plan):
            bank = pT[i % 2]
            sc.wait_ge(pe_sem.h, pe_base + i + 1)
            sc.copy(fm_dst[:, fc, g * 512:(g + 1) * 512],
                    bank[:, 0:512]).then_inc(ev_sem.h, 1)
    ev_sem.inc(len(plan))


def emit_fm_to_tm_tile(ctx, fm_src_col, tm_dst, pT, ident, pe_sem, ev_sem,
                       n_chunks=NKC, evac_engine="scalar"):
    blk = ctx.block
    ngroups = (n_chunks + 7) // 8
    pe_base = pe_sem.v
    ev_base = ev_sem.v

    @blk.tensor
    def _(t):
        for g in range(ngroups):
            bank = pT[g % 2]
            if g >= 2:
                t.wait_ge(ev_sem.h, ev_base + g - 1)
            last = None
            for j in range(min(8, n_chunks - g * 8)):
                fc = g * 8 + j
                last = t.transpose(bank[:, j * 128:(j + 1) * 128], fm_src_col(fc),
                                   ident[:])
            last.then_inc(pe_sem.h, 1)
    pe_sem.inc(ngroups)

    dec = blk.scalar if evac_engine == "scalar" else blk.vector

    @dec
    def _(e):
        for g in range(ngroups):
            bank = pT[g % 2]
            e.wait_ge(pe_sem.h, pe_base + g + 1)
            n = min(8, n_chunks - g * 8) * 128
            cp = e.copy if evac_engine == "scalar" else e.tensor_copy
            cp(tm_dst[:, g * 1024:g * 1024 + n], bank[:, 0:n]).then_inc(ev_sem.h, 1)
    ev_sem.inc(ngroups)


def emit_ln_to_fm(ctx, xn_fm, pT):
    blk = ctx.block
    x_src = ctx.x_src
    xtile = ctx.t["x_tile"]
    ident = ctx.t["s_ident"]
    cpack = ctx.t["s_cpack"]
    sq = ctx.t["sq_scratch"]
    stats = ctx.t["stats"]
    tmb = ctx.t["tm_scratch"]
    s_pe = ctx.sem("pe")
    s_ev = ctx.sem("ev")
    s_dv = ctx.sem("dv")
    s_ld = ctx.sem("ld")
    s_st = ctx.sem("st")
    for tt in range(NT):
        st_target = s_st.v
        dv_before = s_dv.v
        ld_t = s_ld.inc(16)

        @blk.sync
        def _(s, tt=tt, st_target=st_target, dv_before=dv_before):
            s.wait_ge(s_st.h, st_target)
            s.wait_ge(s_dv.h, dv_before)
            s.dma_start(out=xtile[:], in_=x_src[tt * 128:(tt + 1) * 128, :]
                        ).then_inc(s_ld.h, 16)

        def wait_x(e, ld_t=ld_t):
            e.wait_ge(s_ld.h, ld_t)

        emit_ln_tile(ctx, xtile[:], tmb[:], sq, stats, cpack,
                     done_sem=s_dv, wait_before=wait_x)
        dv_target = s_dv.v
        pe_base = s_pe.v
        ev_base = s_ev.v

        @blk.tensor
        def _(t, tt=tt, dv_target=dv_target, pe_base=pe_base, ev_base=ev_base):
            for g in range(2):
                bank = pT[g % 2]
                t.wait_ge(s_dv.h, dv_target)
                if tt > 0 or g > 0:
                    t.wait_ge(s_ev.h, max(0, ev_base + g - 1))
                last = None
                for j in range(8):
                    fc = g * 8 + j
                    last = t.transpose(bank[:, j * 128:(j + 1) * 128],
                                       tmb[:, fc * 128:(fc + 1) * 128], ident[:])
                last.then_inc(s_pe.h, 1)
        s_pe.inc(2)

        @blk.scalar
        def _(sc, tt=tt, pe_base=pe_base, ev_base=ev_base):
            for g in range(2):
                bank = pT[g % 2]
                sc.wait_ge(s_pe.h, pe_base + g + 1)
                sc.copy(xn_fm[:, g * 8:(g + 1) * 8, tt * 128:(tt + 1) * 128],
                        bank[:].rearrange("p (a b) -> p a b", a=8)
                        ).then_inc(s_ev.h, 1)
        s_ev.inc(2)
        ctx.barrier(f"lnfm_{tt}")


def emit_out_epilogue(ctx, o_fm_col, pT, tiles=None):
    blk = ctx.block
    x_src = ctx.x_src
    x_dst = ctx.x_dst
    xtile = ctx.t["x_tile"]
    ident = ctx.t["s_ident"]
    cpack = ctx.t["s_cpack"]
    sq = ctx.t["sq_scratch"]
    stats = ctx.t["stats"]
    tmb = ctx.t["tm_scratch"]
    s_pe = ctx.sem("pe")
    s_ev = ctx.sem("ev")
    s_dv = ctx.sem("dv")
    s_ld = ctx.sem("ld")
    s_st = ctx.sem("st")
    for tt in (tiles if tiles is not None else range(NT)):
        st_before = s_st.v
        ld_t = s_ld.inc(16)

        @blk.sync
        def _(s, tt=tt, st_before=st_before):
            s.wait_ge(s_st.h, st_before)
            s.dma_start(out=xtile[:], in_=x_src[tt * 128:(tt + 1) * 128, :]
                        ).then_inc(s_ld.h, 16)

        emit_fm_to_tm_tile(ctx, lambda fc, tt=tt: o_fm_col(fc, tt),
                           tmb[:], pT, ident, s_pe, s_ev)
        ctx.barrier(f"oe_t_{tt}")
        emit_ln_tile(ctx, tmb[:], sq[:], sq, stats, cpack, done_sem=s_dv)
        dv_t = s_dv.v
        dv_t2 = s_dv.inc()

        @blk.vector
        def _(v, tt=tt, dv_t=dv_t, ld_t=ld_t):
            v.wait_ge(s_dv.h, dv_t)
            v.wait_ge(s_ld.h, ld_t)
            v.tensor_add(xtile[:], xtile[:], sq[:]).then_inc(s_dv.h, 1)
        st_t = s_st.inc(16)

        @blk.scalar
        def _(s, tt=tt, dv_t2=dv_t2):
            s.wait_ge(s_dv.h, dv_t2)
            s.dma_start(out=x_dst[tt * 128:(tt + 1) * 128, :], in_=xtile[:]
                        ).then_inc(s_st.h, 16)
        ctx.barrier(f"oe_r_{tt}")



def emit_phase_C(ctx, hbm):
    nc, blk = ctx.nc, ctx.block
    es_ph = ExitStack()
    ident = ctx.t["s_ident"]
    cpack = ctx.t["s_cpack"]
    sq = ctx.t["sq_scratch"]
    stats = ctx.t["stats"]
    tmb = ctx.t["tm_scratch"]
    NST = 8

    xn_fm = ctx.sb("c_xnfm", [128, NKC, T], BF16, es_ph)
    state_fm = ctx.sb("c_state", [128, NST, T], BF16, es_ph)
    dpack = ctx.t["s_c_dpack"]

    pT = [ctx.ps("c_pt0", [128, 1024], BF16, es_ph),
          ctx.ps("c_pt1", [128, 1024], BF16, es_ph)]
    pb = [ctx.ps(f"c_pb{i}", [128, 512], F32, es_ph) for i in range(4)]

    s_pe = ctx.sem("pe")
    s_ev = ctx.sem("ev")
    s_dv = ctx.sem("dv")
    s_ld = ctx.sem("ld")
    s_cast = ctx.sem("cast")

    emit_ln_to_fm(ctx, xn_fm, pT)

    es_b = ExitStack()
    b_buf = ctx.sb("c_bbuf", [128, NST, 2048], BF16, es_b)
    bt = ctx.sb("c_bt", [128, NKC, 1024], BF16, es_b)

    cast_base = s_cast.v

    @blk.gpsimd
    def _(g, cast_base=cast_base):
        g.wait_ge(s_cast.h, cast_base)
        g.dma_start(out=b_buf[:],
                    in_=hbm["L1_attn_B"].rearrange("(sc p) d -> p sc d", p=128)
                    ).then_inc(s_cast.h, 16)
    s_cast.inc(16)

    pe_base = s_pe.v
    ev_base = s_ev.v

    @blk.tensor
    def _(t):
        t.wait_ge(s_cast.h, cast_base + 16)
        for kc in range(NKC):
            bank = pT[kc % 2]
            if kc >= 2:
                t.wait_ge(s_ev.h, ev_base + kc - 1)
            last = None
            for sc in range(NST):
                last = t.transpose(bank[:, sc * 128:(sc + 1) * 128],
                                   b_buf[:, sc, kc * 128:(kc + 1) * 128], ident[:])
            last.then_inc(s_pe.h, 1)
    s_pe.inc(NKC)

    @blk.vector
    def _(v):
        for kc in range(NKC):
            bank = pT[kc % 2]
            v.wait_ge(s_pe.h, pe_base + kc + 1)
            v.tensor_copy(bt[:, kc, :], bank[:]).then_inc(s_ev.h, 1)
    s_ev.inc(NKC)

    ctx.barrier("c_bt")

    pe_base = s_pe.v
    ev_base = s_ev.v

    @blk.tensor
    def _(t):
        for sc in range(NST):
            for fc in range(2):
                bank = pb[(sc % 2) * 2 + fc]
                gi = sc * 2 + fc
                if gi >= 4:
                    t.wait_ge(s_ev.h, ev_base + gi - 3)
                last = None
                for kc in range(NKC):
                    last = t.matmul(bank[:], bt[:, kc, sc * 128:(sc + 1) * 128],
                                    xn_fm[:, kc, fc * 512:(fc + 1) * 512],
                                    start=(kc == 0), stop=(kc == NKC - 1))
                last.then_inc(s_pe.h, 1)
    s_pe.inc(NST * 2)

    @blk.scalar
    def _(sc_e):
        for sc in range(NST):
            for fc in range(2):
                bank = pb[(sc % 2) * 2 + fc]
                gi = sc * 2 + fc
                sc_e.wait_ge(s_pe.h, pe_base + gi + 1)
                sc_e.activation(state_fm[:, sc, fc * 512:(fc + 1) * 512],
                                bank[:], AF.Tanh).then_inc(s_ev.h, 1)
    s_ev.inc(NST * 2)

    ctx.barrier("c_state")
    es_b.close()

    es_c = ExitStack()
    c_buf = ctx.sb("c_cbuf", [128, 8, 1024], BF16, es_c)
    ct = ctx.sb("c_ct", [128, NST, 2048], BF16, es_c)
    o_fm = ctx.sb("c_ofm", [128, NKC, T], BF16, es_c)

    for r in range(2):
        cast_base = s_cast.v

        @blk.gpsimd
        def _(g, r=r, cast_base=cast_base):
            g.wait_ge(s_cast.h, cast_base)
            g.dma_start(out=c_buf[:],
                        in_=hbm["L1_attn_C"][r * 1024:(r + 1) * 1024, :]
                        .rearrange("(oc p) s -> p oc s", p=128)
                        ).then_inc(s_cast.h, 16)
        s_cast.inc(16)

        pe_base = s_pe.v
        ev_base = s_ev.v

        @blk.tensor
        def _(t, r=r, cast_base=cast_base, pe_base=pe_base, ev_base=ev_base):
            t.wait_ge(s_cast.h, cast_base + 16)
            for sc in range(NST):
                bank = pT[sc % 2]
                if sc >= 2:
                    t.wait_ge(s_ev.h, ev_base + sc - 1)
                last = None
                for j in range(8):
                    last = t.transpose(bank[:, j * 128:(j + 1) * 128],
                                       c_buf[:, j, sc * 128:(sc + 1) * 128], ident[:])
                last.then_inc(s_pe.h, 1)
        s_pe.inc(NST)

        @blk.vector
        def _(v, r=r, pe_base=pe_base):
            for sc in range(NST):
                bank = pT[sc % 2]
                v.wait_ge(s_pe.h, pe_base + sc + 1)
                v.tensor_copy(ct[:, sc, r * 1024:(r + 1) * 1024],
                              bank[:]).then_inc(s_ev.h, 1)
        s_ev.inc(NST)
        ctx.barrier(f"c_ct_{r}")

    pe_base = s_pe.v
    ev_base = s_ev.v
    sc_sem = ctx.sem("sc")
    sc_base = sc_sem.v
    stt_base = s_ev.v
    osc = [ctx.t["sq_scratch"]]

    @blk.tensor
    def _(t):
        for oc in range(NKC):
            for fc in range(2):
                bank = pb[(oc % 2) * 2 + fc]
                gi = oc * 2 + fc
                if gi >= 4:
                    t.wait_ge(s_ev.h, ev_base + gi - 3)
                last = None
                for sc in range(NST):
                    last = t.matmul(bank[:], ct[:, sc, oc * 128:(oc + 1) * 128],
                                    state_fm[:, sc, fc * 512:(fc + 1) * 512],
                                    start=(sc == 0), stop=(sc == NST - 1))
                last.then_inc(s_pe.h, 1)
    s_pe.inc(NKC * 2)

    @blk.scalar
    def _(sc_e):
        for oc in range(NKC):
            for fc in range(2):
                bank = pb[(oc % 2) * 2 + fc]
                gi = oc * 2 + fc
                sc_e.wait_ge(s_pe.h, pe_base + gi + 1)
                if gi >= 2:
                    sc_e.wait_ge(s_ev.h, stt_base + gi - 1)
                sc_e.copy(osc[0][:, fc * 512:(fc + 1) * 512], bank[:]).then_inc(
                    sc_sem.h, 1)
    sc_sem.inc(NKC * 2)

    @blk.vector
    def _(v):
        for oc in range(NKC):
            for fc in range(2):
                gi = oc * 2 + fc
                v.wait_ge(sc_sem.h, sc_base + gi + 1)
                v.scalar_tensor_tensor(
                    o_fm[:, oc, fc * 512:(fc + 1) * 512],
                    xn_fm[:, oc, fc * 512:(fc + 1) * 512],
                    dpack[:, oc:oc + 1],
                    osc[0][:, fc * 512:(fc + 1) * 512],
                    op0=ALU.mult, op1=ALU.add).then_inc(s_ev.h, 1)
    s_ev.inc(NKC * 2)

    ctx.barrier("c_out")

    emit_out_epilogue(ctx, lambda fc, tt: o_fm[:, fc, tt * 128:(tt + 1) * 128], pT)

    es_c.close()
    es_ph.close()


def emit_bcast_rows(ctx, row_ap, width, dst_ap, wait_fn=None):
    blk = ctx.block
    s_b = ctx.sem("bnc")
    scr = ctx.bounce_dram
    t0 = s_b.inc(16)
    t1 = s_b.inc(16)

    @blk.scalar
    def _(s):
        if wait_fn is not None:
            wait_fn(s)
        s.dma_start(out=scr[0:1, 0:width], in_=row_ap).then_inc(s_b.h, 16)
        s.wait_ge(s_b.h, t0)
        s.dma_start(out=dst_ap,
                    in_=scr[0:1, 0:width].broadcast_to([128, width])
                    ).then_inc(s_b.h, 16)
    return t1


def emit_partition_sum(ctx, col_ap, pair_bf, ones_bf, psum2, row2, out_ap,
                       wait_fn=None):
    blk = ctx.block
    s_dv = ctx.sem("dv")
    s_pe = ctx.sem("pe")
    s_ev = ctx.sem("ev")
    d0 = s_dv.v
    d2 = s_dv.inc(2)

    @blk.vector
    def _(v):
        if wait_fn is not None:
            wait_fn(v)
        v.wait_ge(s_dv.h, d0)
        v.tensor_copy(pair_bf[:, 0:1], col_ap).then_inc(s_dv.h, 1)
        v.wait_ge(s_dv.h, d0 + 1)
        v.tensor_sub(pair_bf[:, 1:2], col_ap, pair_bf[:, 0:1]).then_inc(s_dv.h, 1)
    p0 = s_pe.inc()

    @blk.tensor
    def _(t):
        t.wait_ge(s_dv.h, d2)
        t.matmul(psum2, ones_bf, pair_bf[:], start=True, stop=True
                 ).then_inc(s_pe.h, 1)
    e0 = s_ev.inc()

    @blk.scalar
    def _(s):
        s.wait_ge(s_pe.h, p0)
        s.copy(row2, psum2).then_inc(s_ev.h, 1)
    d1 = s_dv.inc()

    @blk.vector
    def _(v):
        v.wait_ge(s_ev.h, e0)
        v.reduce_sum(out_ap, row2, axis=AX.X).then_inc(s_dv.h, 1)
    return d1



def emit_softmax_seg(ctx, bank_ap, rows_lo, nrows, nseg, seglen, scr, ebf_dst,
                     nm, den, bias_ap=None, bias_shared=False, valid_ap=None,
                     valid_shared=False, scale=0.125, pe_wait=None, slot_w=None,
                     slot_off=None):
    blk = ctx.block
    s_dv = ctx.sem("dv")
    s_sc = ctx.sem("sc")
    if slot_w is None:
        slot_w = seglen
    if slot_off is None:
        slot_off = lambda g: 0
    d0 = s_dv.v

    @blk.vector
    def _(v, d0=d0):
        if pe_wait is not None:
            pe_wait(v)
        v.wait_ge(s_dv.h, d0)
        if bias_ap is not None:
            if bias_shared and nseg > 1:
                b3 = bias_ap.rearrange("q (o k) -> q o k", o=1).broadcast_to(
                    [nrows, nseg, seglen])
                v.tensor_tensor(scr.rearrange("q (s k) -> q s k", s=nseg),
                                bank_ap.rearrange("q (s k) -> q s k", s=nseg),
                                b3, op=ALU.add).then_inc(s_dv.h, 1)
            else:
                v.tensor_tensor(scr, bank_ap, bias_ap, op=ALU.add).then_inc(s_dv.h, 1)
        else:
            v.tensor_copy(scr, bank_ap).then_inc(s_dv.h, 1)
        v.wait_ge(s_dv.h, d0 + 1)
        v.reduce_max(nm[:, 0:nseg], scr.rearrange("q (s k) -> q s k", s=nseg),
                     axis=AX.X, negate=True).then_inc(s_dv.h, 1)
        v.wait_ge(s_dv.h, d0 + 2)
        v.tensor_scalar(nm[:, 0:nseg], nm[:, 0:nseg], scale, None,
                        op0=ALU.mult).then_inc(s_dv.h, 1)
    s_dv.inc(3)
    sc0 = s_sc.v

    @blk.scalar
    def _(s, d0=d0, sc0=sc0):
        s.wait_ge(s_dv.h, d0 + 3)
        for g in range(nseg):
            c0 = g * slot_w + slot_off(g)
            s.activation(ebf_dst[:, c0:c0 + seglen],
                         scr[:, g * seglen:(g + 1) * seglen], AF.Exp,
                         bias=nm[:, g:g + 1], scale=scale,
                         accum_out=den[:, g:g + 1]).then_inc(s_sc.h, 1)
    s_sc.inc(nseg)
    d1 = s_dv.v

    @blk.vector
    def _(v, d1=d1, sc0=sc0):
        v.wait_ge(s_sc.h, sc0 + nseg)
        v.wait_ge(s_dv.h, d1)
        v.tensor_scalar(den[:, 0:nseg], den[:, 0:nseg], 1e-9, None,
                        op0=ALU.max).then_inc(s_dv.h, 1)
        v.wait_ge(s_dv.h, d1 + 1)
        v.reciprocal(den[:, 0:nseg], den[:, 0:nseg]).then_inc(s_dv.h, 1)
        if valid_ap is not None:
            v.wait_ge(s_dv.h, d1 + 2)
            if valid_shared and nseg > 1:
                v3 = valid_ap.rearrange("q (o k) -> q o k", o=1).broadcast_to(
                    [nrows, nseg, seglen])
                v.tensor_tensor(ebf_dst.rearrange("q (s k) -> q s k", s=nseg),
                                ebf_dst.rearrange("q (s k) -> q s k", s=nseg),
                                v3, op=ALU.mult).then_inc(s_dv.h, 1)
            else:
                v.tensor_tensor(ebf_dst, ebf_dst, valid_ap, op=ALU.mult
                                ).then_inc(s_dv.h, 1)
        v.wait_ge(s_dv.h, d1 + 2 + (1 if valid_ap is not None else 0))
        dn3 = den[:, 0:nseg].rearrange("q (s o) -> q s o", o=1).broadcast_to(
            [nrows, nseg, slot_w])
        v.tensor_tensor(ebf_dst.rearrange("q (s k) -> q s k", s=nseg),
                        ebf_dst.rearrange("q (s k) -> q s k", s=nseg),
                        dn3, op=ALU.mult).then_inc(s_dv.h, 1)
    s_dv.inc(3 + (1 if valid_ap is not None else 0))
    return s_dv.v


def emit_allgather_pair(ctx, src_fm, d_in, d_out, dst_fm, wait_dv, W=64):
    blk = ctx.block
    s_cast = ctx.sem("cast")
    s_cc = ctx.sem("cc")
    t0 = s_cast.inc(16)
    c0 = s_cc.inc()
    t1 = s_cast.inc(32)

    @blk.gpsimd
    def _(g, t0=t0, c0=c0, wait_dv=wait_dv, W=W):
        wait_dv(g)
        g.wait_ge(s_cast.h, t0 - 16)
        g.dma_start(out=d_in.rearrange("p (c l) -> p c l", c=NKC),
                    in_=src_fm).then_inc(s_cast.h, 16)
        g.wait_ge(s_cast.h, t0)
        g.collective_compute(
            "AllGather", ALU.bypass,
            replica_groups=[[i, i + 1] for i in range(0, NCORES, 2)],
            ins=[d_in.opt()], outs=[d_out.opt()]).then_inc(s_cc.h)
        g.wait_ge(s_cc.h, c0)
        for r in range(2):
            g.wait_ge(s_cast.h, t0 + 16 * r)
            rsrc = d_out[r * 128:(r + 1) * 128, :].rearrange(
                "p (c l) -> p c l", c=NKC)
            if W == 128:
                rsrc = rsrc[:, :, r * 64:(r + 1) * 64]
            g.dma_start(out=dst_fm[:, :, r * 64:(r + 1) * 64],
                        in_=rsrc).then_inc(s_cast.h, 16)
    return t1


def emit_latproj(ctx, w_ap, col0, n_oc, rhs_fm, nlat, wbuf, pb, out_cb):
    blk = ctx.block
    s_cast = ctx.sem("cast")
    s_pe = ctx.sem("pe")
    nog = n_oc // 2
    cast_base = s_cast.v
    pe_base0 = s_pe.v
    ev_done = []

    @blk.gpsimd
    def _(g, cast_base=cast_base, pe_base0=pe_base0):
        for og in range(nog):
            g.wait_ge(s_cast.h, cast_base + 16 * og)
            if og >= 2:
                g.wait_ge(s_pe.h, pe_base0 + 2 * (og - 1))
            g.dma_start(out=wbuf[og % 2][:],
                        in_=w_ap[:, col0 + og * 256:col0 + (og + 1) * 256]
                        .rearrange("(kc p) o -> p kc o", p=128)
                        ).then_inc(s_cast.h, 16)
    s_cast.inc(16 * nog)

    pe_mm = []
    for og in range(nog):
        for j in range(2):
            oc = og * 2 + j

            @blk.tensor
            def _(t, og=og, j=j, oc=oc, cast_base=cast_base):
                t.wait_ge(s_cast.h, cast_base + 16 * (og + 1))
                if oc >= 2:
                    t.wait_ge(ctx.sem("ev").h, ev_done[oc - 2])
                bank = pb[oc % 2]
                last = None
                for kc in range(NKC):
                    last = t.matmul(bank[:, 0:nlat],
                                    wbuf[og % 2][:, kc, j * 128:(j + 1) * 128],
                                    rhs_fm[:, kc, 0:nlat],
                                    start=(kc == 0), stop=(kc == NKC - 1))
                last.then_inc(s_pe.h, 1)
            pe_mm.append(s_pe.inc())
            ev_done.append(out_cb(oc, pb[oc % 2][:, 0:nlat], pe_mm[-1]))


def emit_phase_A(ctx, hbm):
    import os as _os
    A_STOP = int(_os.environ.get("A_STOP", "99"))
    _scopes = []

    def _stk():
        s = ExitStack()
        _scopes.append(s)
        return s

    def _close_all():
        if _sc_cur[0] is not None:
            _sc_cur[0].close()
            _sc_cur[0] = None
        for s in reversed(_scopes):
            s.close()

    blk = ctx.block
    cpack = ctx.t["s_cpack"]
    sq = ctx.t["sq_scratch"]
    tmb = ctx.t["tm_scratch"]
    abias = ctx.t["s_abias"]
    s_pe = ctx.sem("pe")
    s_ev = ctx.sem("ev")
    s_dv = ctx.sem("dv")
    s_sc = ctx.sem("sc")
    s_ld = ctx.sem("ld")
    s_cast = ctx.sem("cast")
    s_qsp = ctx.sem("spl")
    NB = 16

    _ns = ctx.nc.named_scope
    _sc_cur = [None]

    def _mark(name):
        if _sc_cur[0] is not None:
            _sc_cur[0].close()
            _sc_cur[0] = None
        if name is not None:
            _sc_cur[0] = ExitStack()
            _sc_cur[0].enter_context(_ns(name))

    es_ph = _stk()
    ol_tm = ctx.sb("a_ol", [128, NT, 2048], BF16, es_ph)
    gk_fm = ctx.sb("a_gk", [128, NKC, 128], BF16, es_ph)
    gv_tm = ctx.sb("a_gv", [128, 2048], BF16, es_ph)
    g_tm = ctx.sb("a_gtm", [128, NT, 32], BF16, es_ph)
    nmsc = ctx.sb("a_nm", [128, 32], F32, es_ph)
    densc = ctx.sb("a_den", [128, 32], F32, es_ph)

    es_msk = _stk()
    mloc = ctx.sb("s_a_mloc", [64, 64], BF16, es_msk)
    mlat = ctx.sb("s_a_mlat", [128, 128], BF16, es_msk)
    pswap = ctx.sb("s_a_pswap", [128, 128], BF16, es_msk)
    poolq = ctx.sb("s_a_poolq", [128, 128], BF16, es_msk)
    id2 = ctx.sb("s_a_id2", [64, 128], BF16, es_msk)

    cb0 = s_cast.v

    @blk.gpsimd
    def _(g, cb0=cb0):
        g.wait_ge(s_cast.h, cb0)
        g.dma_start(out=mloc[:], in_=hbm["a_mloc"]).then_inc(s_cast.h, 16)
        g.wait_ge(s_cast.h, cb0 + 16)
        g.dma_start(out=mlat[:], in_=hbm["a_mlat"]).then_inc(s_cast.h, 16)
        g.wait_ge(s_cast.h, cb0 + 32)
        g.dma_start(out=pswap[:], in_=hbm["a_pswap"]).then_inc(s_cast.h, 16)
        g.wait_ge(s_cast.h, cb0 + 48)
        g.dma_start(out=poolq[:], in_=hbm["a_poolq"]).then_inc(s_cast.h, 16)
        g.wait_ge(s_cast.h, cb0 + 64)
        g.dma_start(out=id2[:], in_=hbm["a_id2"]).then_inc(s_cast.h, 16)
    msk_t = s_cast.inc(16 * 5)

    _mark("A_a2")
    es_kv = _stk()
    k_fm = ctx.sb("a_k", [128, NKC, T], BF16, es_kv)
    v_tm = ctx.sb("a_v", [128, NT, 2048], BF16, es_kv)
    es_q = _stk()
    q_ro = ctx.sb("a_qro", [128, NKC, T], BF16, es_q)

    es_xn = _stk()
    xn_fm = ctx.sb("a_xn", [128, NKC, T], BF16, es_xn)
    rope = ctx.sb("s_a_rope", [128, 2048], BF16, es_xn)
    qkwb = [ctx.sb(f"a_wq{i}", [128, NKC, 128], BF16, es_xn) for i in range(2)]
    qraw = ctx.sb("a_qraw", [128, 1024], BF16, es_xn)
    vfm = qraw
    rsc = ctx.t["sq_scratch"]
    pT = [ctx.ps("a_t0", [128, 1024], BF16, es_xn),
          ctx.ps("a_t1", [128, 1024], BF16, es_xn)]
    pb = [ctx.ps(f"a_b{i}", [128, 512], F32, es_xn) for i in range(4)]
    pswp = ctx.ps("a_swp", [128, 1024], F32, es_xn)

    rt0 = s_cast.inc(16)

    @blk.gpsimd
    def _(g, msk_t=msk_t):
        g.wait_ge(s_cast.h, msk_t)
        g.dma_start(out=rope[:], in_=hbm["a_rope"]).then_inc(s_cast.h, 16)

    emit_ln_to_fm(ctx, xn_fm, pT)
    ctx.barrier("a_xn")

    cast_base = s_cast.v
    NSTEP = 48
    mm_done = []
    vtr_done = []
    cons_done = []

    def _qkv_load(st, wait_mm):
        @blk.gpsimd
        def _(g, st=st, wait_mm=wait_mm):
            g.wait_ge(s_cast.h, cast_base + 16 * st)
            if wait_mm is not None:
                g.wait_ge(s_pe.h, wait_mm)
            g.dma_start(out=qkwb[st % 2][:],
                        in_=hbm["L0_attn_w_qkv"][:, st * 128:(st + 1) * 128]
                        .rearrange("(kc p) o -> p kc o", p=128)
                        ).then_inc(s_cast.h, 16)
        s_cast.inc(16)

    _qkv_load(0, None)
    _qkv_load(1, None)

    for st in range(NSTEP):
        oc = st
        pe0 = s_pe.v

        @blk.tensor
        def _(t, st=st, oc=oc, cast_base=cast_base, pe0=pe0):
            t.wait_ge(s_cast.h, cast_base + 16 * (st + 1))
            if oc >= 2:
                sem_n, val = cons_done[oc - 2]
                t.wait_ge((s_dv if sem_n == "dv" else s_sc).h, val)
            for fc in range(2):
                bank = pb[(oc % 2) * 2 + fc]
                last = None
                for kc in range(NKC):
                    last = t.matmul(bank[:], qkwb[st % 2][:, kc, :],
                                    xn_fm[:, kc, fc * 512:(fc + 1) * 512],
                                    start=(kc == 0), stop=(kc == NKC - 1))
                last.then_inc(s_pe.h, 1)
        mm2 = s_pe.inc(2)
        mm_done.append(mm2)
        if st + 2 < NSTEP:
            _qkv_load(st + 2, mm2)

        if oc < 32:
            isq = oc < 16
            cc = oc % 16
            prev_rope = [cd for cd in cons_done if cd[0] == "dv"]
            sc_t = s_sc.inc()

            @blk.scalar
            def _(s, oc=oc, cc=cc, mm2=mm2, prev_rope=prev_rope):
                s.wait_ge(s_pe.h, mm2)
                if prev_rope:
                    s.wait_ge(s_dv.h, prev_rope[-1][1])
                s.activation(qraw[:, 0:512], pb[(oc % 2) * 2][:], AF.Identity,
                             bias=abias[:, oc:oc + 1])
                s.activation(qraw[:, 512:1024], pb[(oc % 2) * 2 + 1][:],
                             AF.Identity, bias=abias[:, oc:oc + 1]
                             ).then_inc(s_sc.h, 1)

            @blk.tensor
            def _(t, sc_t=sc_t, oc=oc, prev_rope=prev_rope):
                t.wait_ge(s_sc.h, sc_t)
                if prev_rope:
                    t.wait_ge(s_dv.h, prev_rope[-1][1])
                t.matmul(pswp[:, 0:512], pswap[:], qraw[:, 0:512],
                         start=True, stop=True)
                t.matmul(pswp[:, 512:1024], pswap[:], qraw[:, 512:1024],
                         start=True, stop=True).then_inc(s_pe.h, 1)
            pe1 = s_pe.inc()
            d0 = s_dv.v

            @blk.vector
            def _(v, pe1=pe1, d0=d0, isq=isq, cc=cc, rt0=rt0):
                v.wait_ge(s_pe.h, pe1)
                v.wait_ge(s_dv.h, d0)
                if cc == 0 and isq:
                    v.wait_ge(s_cast.h, rt0)
                v.tensor_mul(rsc[:, 0:1024], qraw[:], rope[:, 0:1024]
                             ).then_inc(s_dv.h, 1)
                v.wait_ge(s_dv.h, d0 + 1)
                v.tensor_mul(rsc[:, 1024:2048], pswp[:], rope[:, 1024:2048]
                             ).then_inc(s_dv.h, 1)
                v.wait_ge(s_dv.h, d0 + 2)
                dst = (q_ro if isq else k_fm)[:, cc, :]
                v.tensor_add(dst, rsc[:, 0:1024], rsc[:, 1024:2048]
                             ).then_inc(s_dv.h, 1)
            cons = s_dv.inc(3)
            cons_done.append(("dv", cons))
        else:
            cc = oc - 32
            prev_rope = [cd for cd in cons_done if cd[0] == "dv"]
            sc_t = s_sc.inc()

            @blk.scalar
            def _(s, oc=oc, mm2=mm2, cc=cc, prev_rope=prev_rope):
                s.wait_ge(s_pe.h, mm2)
                if cc == 0 and prev_rope:
                    s.wait_ge(s_dv.h, prev_rope[-1][1])
                if cc > 0:
                    s.wait_ge(s_pe.h, vtr_done[-1])
                s.activation(vfm[:, 0:512], pb[(oc % 2) * 2][:], AF.Identity,
                             bias=abias[:, oc:oc + 1])
                s.activation(vfm[:, 512:1024], pb[(oc % 2) * 2 + 1][:],
                             AF.Identity, bias=abias[:, oc:oc + 1]
                             ).then_inc(s_sc.h, 1)
            pe1b = s_pe.v
            ev0 = s_ev.v

            @blk.tensor
            def _(t, sc_t=sc_t, ev0=ev0, oc=oc):
                t.wait_ge(s_sc.h, sc_t)
                if oc > 32:
                    t.wait_ge(s_ev.h, ev0)
                for tt in range(NT):
                    ban = pT[tt % 2]
                    if tt >= 2:
                        t.wait_ge(s_ev.h, ev0 + tt - 1)
                    t.transpose(ban[:, 0:128], vfm[:, tt * 128:(tt + 1) * 128],
                                ident_of(ctx)).then_inc(s_pe.h, 1)
            s_pe.inc(NT)
            vtr_done.append(s_pe.v)

            @blk.vector
            def _(v, pe1b=pe1b, cc=cc):
                for tt in range(NT):
                    ban = pT[tt % 2]
                    v.wait_ge(s_pe.h, pe1b + tt + 1)
                    v.tensor_copy(v_tm[:, tt, cc * 128:(cc + 1) * 128],
                                  ban[:, 0:128]).then_inc(s_ev.h, 1)
            s_ev.inc(NT)
            cons_done.append(("sc", sc_t))

    gw = ctx.sb("a_gw", [128, NKC, 32], BF16, es_xn)
    gfm = qraw
    gt0 = s_cast.inc(16)

    @blk.gpsimd
    def _(g, gt0=gt0):
        g.wait_ge(s_cast.h, gt0 - 16)
        g.dma_start(out=gw[:],
                    in_=hbm["L0_attn_w_gate"].rearrange("(kc p) o -> p kc o", p=128)
                    ).then_inc(s_cast.h, 16)
    ctx.barrier("a_qkv")
    ctx.tap("q_ro", q_ro[:])
    ctx.tap("k_fm", k_fm[:])
    ctx.tap("v_tm", v_tm[:])
    pe0 = s_pe.v

    @blk.tensor
    def _(t, gt0=gt0, pe0=pe0):
        t.wait_ge(s_cast.h, gt0)
        for fc in range(2):
            bank = pb[fc]
            last = None
            for kc in range(NKC):
                last = t.matmul(bank[0:32, :], gw[:, kc, :],
                                xn_fm[:, kc, fc * 512:(fc + 1) * 512],
                                start=(kc == 0), stop=(kc == NKC - 1))
            last.then_inc(s_pe.h, 1)
    s_pe.inc(2)
    sc_t = s_sc.inc()

    @blk.scalar
    def _(s, pe0=pe0):
        s.wait_ge(s_pe.h, pe0 + 2)
        s.activation(gfm[0:32, 0:512], pb[0][0:32, :], AF.Sigmoid,
                     bias=abias[0:32, 192:193])
        s.activation(gfm[0:32, 512:1024], pb[1][0:32, :], AF.Sigmoid,
                     bias=abias[0:32, 192:193]).then_inc(s_sc.h, 1)
    pe1 = s_pe.v
    ev0 = s_ev.v

    @blk.tensor
    def _(t, sc_t=sc_t, ev0=ev0):
        t.wait_ge(s_sc.h, sc_t)
        for tt in range(NT):
            ban = pT[tt % 2]
            if tt >= 2:
                t.wait_ge(s_ev.h, ev0 + tt - 1)
            t.transpose(ban[:, 0:32], gfm[0:32, tt * 128:(tt + 1) * 128],
                        ident_of(ctx)[0:32, 0:32]).then_inc(s_pe.h, 1)
    s_pe.inc(NT)

    @blk.vector
    def _(v, pe1=pe1):
        for tt in range(NT):
            ban = pT[tt % 2]
            v.wait_ge(s_pe.h, pe1 + tt + 1)
            v.tensor_copy(g_tm[:, tt, :], ban[:, 0:32]).then_inc(s_ev.h, 1)
    s_ev.inc(NT)
    ctx.barrier("a_a2")
    ctx.tap("g_tm", g_tm[:])
    es_xn.close()
    if A_STOP <= 1:
        _close_all()
        return

    _mark("A_loc")
    es_loc = _stk()
    q_fm = q_ro
    qz = [ctx.sb(f"a_qz{i}", [128, 512], BF16, es_loc) for i in range(2)]
    ebf = ctx.sb("a_ebf", [64, 8, 128], BF16, es_loc)
    etb = ctx.sb("a_etb", [128, 8, 128], BF16, es_loc)
    pS = ctx.ps("a_ps", [128, 512], F32, es_loc)
    pEf = ctx.ps("a_pef", [128, 1024], F32, es_loc)
    pO = ctx.ps("a_po", [128, 512], F32, es_loc)

    @blk.vector
    def _(v):
        v.memset(qz[0][:], 0.0)
        v.memset(qz[1][:], 0.0).then_inc(s_dv.h, 1)
        v.memset(ebf[:], 0.0).then_inc(s_dv.h, 1)
    s_dv.inc(2)

    for h in range(32):
        hp, hh = h // 2, (h % 2) * 64
        for bg in range(2):
            dq = s_dv.inc()

            @blk.vector
            def _(v, h=h, hp=hp, hh=hh, bg=bg, dq=dq):
                v.wait_ge(s_dv.h, dq - 1)
                v.tensor_copy(qz[h % 2][hh:hh + 64, :],
                              q_fm[hh:hh + 64, hp, bg * 512:(bg + 1) * 512]
                              ).then_inc(s_dv.h, 1)

            @blk.tensor
            def _(t, hp=hp, h=h, bg=bg, dq=dq):
                t.wait_ge(s_dv.h, dq)
                if h == 0 and bg == 0:
                    t.wait_ge(s_cast.h, msk_t)
                last = None
                for b8 in range(8):
                    blk_i = bg * 8 + b8
                    last = t.matmul(
                        pS[0:64, b8 * 64:(b8 + 1) * 64],
                        qz[h % 2][:, b8 * 64:(b8 + 1) * 64],
                        k_fm[:, hp, blk_i * 64:(blk_i + 1) * 64],
                        start=True, stop=True)
                last.then_inc(s_pe.h, 1)
            mmt = s_pe.inc()
            sm_t = emit_softmax_seg(
                ctx, pS[0:64, :], 0, 64, 8, 64, sq[0:64, 0:512],
                ebf[0:64, :, :].rearrange("q s k -> q (s k)"),
                nmsc[0:64, 0:8], densc[0:64, 0:8],
                bias_ap=mloc[:], bias_shared=True,
                pe_wait=(lambda v, mmt=mmt: v.wait_ge(s_pe.h, mmt)),
                slot_w=128, slot_off=lambda g: (g % 2) * 64)

            @blk.tensor
            def _(t, sm_t=sm_t):
                t.wait_ge(s_dv.h, sm_t)
                last = None
                for b8 in range(8):
                    last = t.matmul(pEf[:, b8 * 128:(b8 + 1) * 128],
                                    ebf[0:64, b8, :], id2[:],
                                    start=True, stop=True)
                last.then_inc(s_pe.h, 1)
            pt = s_pe.inc()
            se = s_sc.inc()

            @blk.scalar
            def _(s, pt=pt):
                s.wait_ge(s_pe.h, pt)
                s.copy(etb[:], pEf[:].rearrange("p (s k) -> p s k", s=8)
                       ).then_inc(s_sc.h, 1)

            @blk.tensor
            def _(t, se=se, hp=hp, h=h, bg=bg):
                t.wait_ge(s_sc.h, se)
                last = None
                for b8 in range(8):
                    blk_i = bg * 8 + b8
                    last = t.matmul(
                        pO[:, b8 * 64:(b8 + 1) * 64],
                        etb[:, b8, :],
                        v_tm[:, blk_i // 2, h * 64:(h + 1) * 64],
                        start=True, stop=True)
                last.then_inc(s_pe.h, 1)
            av = s_pe.inc()

            @blk.vector
            def _(v, av=av, h=h, bg=bg):
                v.wait_ge(s_pe.h, av)
                last = None
                for b8 in range(8):
                    blk_i = bg * 8 + b8
                    rb = (blk_i % 2) * 64
                    last = v.tensor_copy(
                        ol_tm[rb:rb + 64, blk_i // 2, h * 64:(h + 1) * 64],
                        pO[rb:rb + 64, b8 * 64:(b8 + 1) * 64])
                last.then_inc(s_dv.h, 1)
            s_dv.inc()

    ctx.barrier("a_loc")
    ctx.tap("ol_tm", ol_tm[:])
    es_loc.close()
    if A_STOP <= 2:
        _close_all()
        return

    _mark("A_spill")
    qt = s_qsp.inc(16)

    @blk.scalar
    def _(s):
        s.dma_start(out=ctx.qspill_dram.rearrange("p (c t) -> p c t", c=NKC),
                    in_=q_ro[:]).then_inc(s_qsp.h, 16)

    @blk.sync
    def _(s, qt=qt):
        s.wait_ge(s_qsp.h, qt)
    ctx.barrier("a_qspill")
    es_q.close()

    _mark("A_pool")
    es_lat = _stk()
    lmf_full = ctx.sb("a_lmf", [128, NKC, 128], BF16, es_lat)
    lv_tm = ctx.sb("a_lvt", [128, 2048], BF16, es_lat)
    lmf2 = ctx.sb("a_lm2", [128, NKC, 128], BF16, es_lat)
    lmf3 = ctx.sb("a_lm3", [128, NKC, 128], BF16, es_lat)
    ro_full = ctx.sb("a_rof", [128, NKC, 128], BF16, es_lat)
    gv_fm = ctx.sb("a_gvf", [128, NKC, 128], BF16, es_lat)
    lsc = ctx.sb("a_lsc", [128, 128], BF16, es_lat)
    etb2 = ctx.sb("a_etb2b", [128, 128], BF16, es_lat)
    latw = [ctx.sb(f"a_lw{i}", [128, NKC, 256], BF16, es_lat) for i in range(2)]
    pS = ctx.ps("a_ps4", [128, 512], F32, es_lat)
    pS2 = ctx.ps("a_ps4b", [128, 512], F32, es_lat)
    pE = ctx.ps("a_pe4", [128, 1024], BF16, es_lat)
    pE2 = ctx.ps("a_pe4b", [128, 1024], BF16, es_lat)
    pEf = ctx.ps("a_pef4", [128, 512], F32, es_lat)
    pO = ctx.ps("a_po4", [128, 512], F32, es_lat)
    pb2 = [ctx.ps(f"a_pb4{i}", [128, 512], F32, es_lat) for i in range(2)]

    es_lat1 = _stk()
    lmf_own = ctx.sb("a_lmo", [128, NKC, 64], BF16, es_lat1)
    lq = ctx.sb("a_lq", [128, NKC, 128], BF16, es_lat1)
    lk = ctx.sb("a_lk", [128, NKC, 128], BF16, es_lat1)
    lv_fm = ctx.sb("a_lvf", [128, NKC, 128], BF16, es_lat1)
    lo_tm = ctx.sb("a_lot", [128, 2048], BF16, es_lat1)
    lo_fm = ctx.sb("a_lof", [128, NKC, 128], BF16, es_lat1)
    ebf = ctx.sb("a_ebf2", [128, 16, 128], BF16, es_lat1)
    etb = ctx.sb("a_etb2", [128, 16, 8], BF16, es_lat1)
    lqz = [ctx.sb(f"a_lqz{i}", [128, 128], BF16, es_lat1) for i in range(2)]

    @blk.vector
    def _(v):
        v.memset(ebf[:], 0.0)
        v.memset(lqz[0][:], 0.0)
        v.memset(lqz[1][:], 0.0).then_inc(s_dv.h, 1)
    s_dv.inc()

    for hp in range(NKC):
        pe0 = s_pe.v

        @blk.tensor
        def _(t, hp=hp, pe0=pe0):
            t.wait_ge(s_dv.h, s_dv.v)
            if hp == 0:
                t.wait_ge(s_cast.h, msk_t)
            last = None
            for fc in range(2):
                bank = pS if fc == 0 else pS2
                last = t.matmul(bank[0:8, :], poolq[:, hp * 8:(hp + 1) * 8],
                                k_fm[:, hp, fc * 512:(fc + 1) * 512],
                                start=True, stop=True)
            last.then_inc(s_pe.h, 1)
        mmt = s_pe.inc()
        for fc in range(2):
            bank = pS if fc == 0 else pS2
            emit_softmax_seg(
                ctx, bank[0:8, :], 0, 8, 8, 64, sq[0:8, 0:512],
                ebf[0:8, :, :].rearrange("q s k -> q (s k)")
                [:, fc * 1024:(fc + 1) * 1024],
                nmsc[0:8, fc * 8:fc * 8 + 8], densc[0:8, fc * 8:fc * 8 + 8],
                pe_wait=(lambda v, mmt=mmt: v.wait_ge(s_pe.h, mmt)),
                slot_w=128, slot_off=lambda g: (g % 2) * 64)
        sm_t = s_dv.v

        @blk.tensor
        def _(t, sm_t=sm_t):
            t.wait_ge(s_dv.h, sm_t)
            last = None
            for b in range(NB):
                last = t.matmul(pEf[:, b * 8:(b + 1) * 8],
                                ebf[0:8, b, :], ident_of(ctx)[0:8, 0:8],
                                start=True, stop=True)
            last.then_inc(s_pe.h, 1)
        pt = s_pe.inc()
        se = s_sc.inc()

        @blk.scalar
        def _(s, pt=pt):
            s.wait_ge(s_pe.h, pt)
            s.copy(etb[:], pEf[:, 0:128].rearrange("p (s k) -> p s k", s=NB)
                   ).then_inc(s_sc.h, 1)

        @blk.tensor
        def _(t, se=se, hp=hp):
            t.wait_ge(s_sc.h, se)
            last = None
            for b in range(NB):
                last = t.matmul(pO[:, b * 8:(b + 1) * 8],
                                v_tm[:, b // 2, hp * 128:(hp + 1) * 128],
                                etb[:, b, :], start=True, stop=True)
            last.then_inc(s_pe.h, 1)
        av = s_pe.inc()

        @blk.vector
        def _(v, av=av, hp=hp):
            v.wait_ge(s_pe.h, av)
            last = None
            for b in range(NB):
                last = v.tensor_copy(lmf_own[0:64, hp, b * 4:(b + 1) * 4],
                                     pO[0:64, b * 8:b * 8 + 4])
                last = v.tensor_copy(lmf_own[64:128, hp, b * 4:(b + 1) * 4],
                                     pO[64:128, b * 8 + 4:(b + 1) * 8])
            last.then_inc(s_dv.h, 1)
        s_dv.inc()

    ctx.barrier("a_pool")
    ctx.tap("lmf_own", lmf_own[:])
    if A_STOP <= 3:
        _close_all()
        return

    _mark("A_ag1")
    ag_t = emit_allgather_pair(ctx, lmf_own[:], ctx.ag1_in, ctx.ag1_out,
                               lmf_full, lambda g: None)

    @blk.sync
    def _(s, ag_t=ag_t):
        s.wait_ge(s_cast.h, ag_t)
    ctx.barrier("a_ag1")
    ctx.tap("lmf_full", lmf_full[:])
    if A_STOP <= 4:
        _close_all()
        return

    _mark("A_lat")
    def bias_evac_cb(dst, bcol0, oc_off=0):
        def cb(oc, bank_ap, mm_t):
            se = s_sc.inc()

            @blk.scalar
            def _(s, oc=oc, bank_ap=bank_ap, mm_t=mm_t, se=se):
                s.wait_ge(s_pe.h, mm_t)
                s.activation(dst[:, oc - oc_off, :], bank_ap, AF.Identity,
                             bias=abias[:, bcol0 + oc:bcol0 + oc + 1]
                             ).then_inc(s_sc.h, 1)
            ev = s_ev.inc()

            @blk.vector
            def _(v, se=se):
                v.wait_ge(s_sc.h, se)
                v.engine_nop().then_inc(s_ev.h, 1)
            return ev
        return cb

    def resid_cb(dst, src_resid, bcol0):
        prev_d = [None]

        def cb(oc, bank_ap, mm_t):
            se = s_sc.inc()

            @blk.scalar
            def _(s, oc=oc, bank_ap=bank_ap, mm_t=mm_t):
                s.wait_ge(s_pe.h, mm_t)
                if prev_d[0] is not None:
                    s.wait_ge(s_dv.h, prev_d[0])
                s.activation(lsc[:, 0:128], bank_ap, AF.Identity,
                             bias=abias[:, bcol0 + oc:bcol0 + oc + 1]
                             ).then_inc(s_sc.h, 1)
            d0 = s_dv.inc()
            prev_d[0] = d0

            @blk.vector
            def _(v, oc=oc, se=se):
                v.wait_ge(s_sc.h, se)
                v.tensor_add(dst[:, oc, :], src_resid[:, oc, :], lsc[:, 0:128]
                             ).then_inc(s_dv.h, 1)
            ev = s_ev.inc()

            @blk.vector
            def _(v, d0=d0):
                v.wait_ge(s_dv.h, d0)
                v.engine_nop().then_inc(s_ev.h, 1)
            return ev
        return cb

    def lmqkv_cb(oc, bank_ap, mm_t):
        if oc < 16:
            return bias_evac_cb(lq, 48)(oc, bank_ap, mm_t)
        elif oc < 32:
            return bias_evac_cb(lk, 48, 16)(oc, bank_ap, mm_t)
        return bias_evac_cb(lv_fm, 48, 32)(oc, bank_ap, mm_t)

    emit_latproj(ctx, hbm["L0_attn_w_lm_qkv"], 0, 48, lmf_full, 128,
                 latw, pb2, lmqkv_cb)
    ctx.barrier("a_lmqkv")
    ctx.tap("lq", lq[:])
    ctx.tap("lk", lk[:])
    ctx.tap("lv_fm", lv_fm[:])

    pe0 = s_pe.v
    ev0 = s_ev.v

    @blk.tensor
    def _(t, pe0=pe0, ev0=ev0):
        for c in range(NKC):
            if c >= 2:
                t.wait_ge(s_ev.h, ev0 + c - 1)
            t.transpose([pE, pE2][c % 2][:, 0:128],
                        lv_fm[:, c, :], ident_of(ctx)).then_inc(s_pe.h, 1)
    s_pe.inc(NKC)

    @blk.scalar
    def _(s, pe0=pe0):
        for c in range(NKC):
            s.wait_ge(s_pe.h, pe0 + c + 1)
            s.copy(lv_tm[:, c * 128:(c + 1) * 128],
                   [pE, pE2][c % 2][:, 0:128]).then_inc(s_ev.h, 1)
    s_ev.inc(NKC)
    ctx.barrier("a_lvt")
    ctx.tap("lv_tm", lv_tm[:])

    for h in range(32):
        hp, hh = h // 2, (h % 2) * 64
        dq = s_dv.inc()

        @blk.vector
        def _(v, hp=hp, hh=hh, h=h, dq=dq):
            v.wait_ge(s_dv.h, dq - 1)
            v.tensor_copy(lqz[h % 2][hh:hh + 64, :], lq[hh:hh + 64, hp, :]
                          ).then_inc(s_dv.h, 1)

        @blk.tensor
        def _(t, hp=hp, h=h, dq=dq):
            t.wait_ge(s_dv.h, dq)
            t.matmul(pS[:, 0:128], lqz[h % 2][:], lk[:, hp, :],
                     start=True, stop=True).then_inc(s_pe.h, 1)
        mmt = s_pe.inc()
        dbg_lat = _os.environ.get("A_DBG_LAT", "0") == "1" and h in (0, 1, 17)
        if dbg_lat:
            ctx.barrier(f"dbgs{h}")
            dd = s_dv.inc()

            @blk.vector
            def _(v, dd=dd, mmt=mmt):
                v.wait_ge(s_pe.h, mmt)
                v.tensor_copy(sq[:, 1024:1152], pS[:, 0:128]).then_inc(s_dv.h, 1)
            ctx.barrier(f"dbgs2{h}")
            ctx.tap(f"lat_s{h}", sq[:, 1024:1152])
            ctx.tap(f"lat_qz{h}", lqz[h % 2][:])
            ctx.barrier(f"dbgs3{h}")
        sm_t = emit_softmax_seg(
            ctx, pS[:, 0:128], 0, 128, 1, 128, sq[:, 0:128], etb2[:],
            nmsc[:, 0:1], densc[:, 0:1], bias_ap=mlat[:],
            pe_wait=(lambda v, mmt=mmt: v.wait_ge(s_pe.h, mmt)))
        if dbg_lat:
            ctx.barrier(f"dbge{h}")
            ctx.tap(f"lat_e{h}", etb2[:])
            ctx.barrier(f"dbge2{h}")

        @blk.tensor
        def _(t, sm_t=sm_t):
            t.wait_ge(s_dv.h, sm_t)
            t.transpose(pE[:, 0:128], etb2[:], ident_of(ctx)
                        ).then_inc(s_pe.h, 1)
        pt = s_pe.inc()
        se = s_sc.inc()

        @blk.scalar
        def _(s, pt=pt):
            s.wait_ge(s_pe.h, pt)
            s.copy(etb2[:], pE[:, 0:128]).then_inc(s_sc.h, 1)

        @blk.tensor
        def _(t, se=se, h=h):
            t.wait_ge(s_sc.h, se)
            t.matmul(pO[:, 0:64], etb2[:], lv_tm[:, h * 64:(h + 1) * 64],
                     start=True, stop=True).then_inc(s_pe.h, 1)
        av = s_pe.inc()
        if dbg_lat:
            ctx.barrier(f"dbgo{h}")
            dd = s_dv.inc()

            @blk.vector
            def _(v, dd=dd, av=av):
                v.wait_ge(s_pe.h, av)
                v.tensor_copy(sq[:, 1024:1088], pO[:, 0:64]).then_inc(s_dv.h, 1)
            ctx.barrier(f"dbgo2{h}")
            ctx.tap(f"lat_eT{h}", etb2[:])
            ctx.tap(f"lat_o{h}", sq[:, 1024:1088])
            ctx.barrier(f"dbgo3{h}")

        @blk.vector
        def _(v, av=av, h=h):
            v.wait_ge(s_pe.h, av)
            v.tensor_copy(lo_tm[:, h * 64:(h + 1) * 64], pO[:, 0:64]
                          ).then_inc(s_dv.h, 1)
        s_dv.inc()
    ctx.barrier("a_lattn")
    ctx.tap("lo_tm", lo_tm[:])

    pe0 = s_pe.v
    ev0 = s_ev.v

    @blk.tensor
    def _(t, pe0=pe0, ev0=ev0):
        for c in range(NKC):
            if c >= 2:
                t.wait_ge(s_ev.h, ev0 + c - 1)
            t.transpose([pE, pE2][c % 2][:, 0:128],
                        lo_tm[:, c * 128:(c + 1) * 128], ident_of(ctx)
                        ).then_inc(s_pe.h, 1)
    s_pe.inc(NKC)

    @blk.scalar
    def _(s, pe0=pe0):
        for c in range(NKC):
            s.wait_ge(s_pe.h, pe0 + c + 1)
            s.copy(lo_fm[:, c, :], [pE, pE2][c % 2][:, 0:128]
                   ).then_inc(s_ev.h, 1)
    s_ev.inc(NKC)
    ctx.barrier("a_lofm")

    emit_latproj(ctx, hbm["L0_attn_w_lm_out"], 0, 16, lo_fm, 128,
                 latw, pb2, resid_cb(lmf2, lmf_full, 96))
    ctx.barrier("a_lmf2")
    ctx.tap("lmf2", lmf2[:])
    es_lat1.close()

    _mark("A_ra")
    es_lat2 = _stk()
    l2q = ctx.sb("a_l2q", [128, NKC, 128], BF16, es_lat2)
    ro_all = ctx.sb("a_roo", [128, NKC, 128], BF16, es_lat2)
    ebr = ctx.sb("a_ebr", [16, 16, 128], BF16, es_lat2)
    l2qp = ctx.sb("a_l2qp", [128, 256], BF16, es_lat2)
    etb2x = ctx.sb("a_etb2x", [128, 256], BF16, es_lat2)

    @blk.vector
    def _(v):
        v.wait_ge(s_dv.h, s_dv.v)
        v.memset(ebr[:], 0.0)
        v.memset(l2qp[:], 0.0).then_inc(s_dv.h, 1)
    s_dv.inc()

    emit_latproj(ctx, hbm["L0_attn_w_l2t_q"], 0, 16, lmf2, 128,
                 latw, pb2, bias_evac_cb(l2q, 112))
    ctx.barrier("a_l2q")
    ctx.tap("l2q", l2q[:])

    for hp in range(NKC):
        dq = s_dv.inc(4)

        @blk.vector
        def _(v, hp=hp, dq=dq):
            v.wait_ge(s_dv.h, dq - 4)
            last = None
            for oi in range(2):
                for h2 in range(2):
                    r0 = oi * 8 + h2 * 4
                    last = v.tensor_copy(
                        l2qp[h2 * 64:(h2 + 1) * 64, :]
                        .rearrange("p (bl r) -> p bl r", r=16)[:, :, r0:r0 + 4],
                        l2q[h2 * 64:(h2 + 1) * 64, hp, oi * 64:(oi + 1) * 64]
                        .rearrange("p (bl m) -> p bl m", m=4))
                    last.then_inc(s_dv.h, 1)

        @blk.tensor
        def _(t, hp=hp, dq=dq):
            t.wait_ge(s_dv.h, dq)
            last = None
            for bl in range(NB):
                bank = pS if bl < 8 else pS2
                last = t.matmul(
                    bank[0:16, (bl % 8) * 64:(bl % 8 + 1) * 64],
                    l2qp[:, bl * 16:(bl + 1) * 16],
                    k_fm[:, hp, bl * 64:(bl + 1) * 64],
                    start=True, stop=True)
            last.then_inc(s_pe.h, 1)
        mmt = s_pe.inc()
        for bk in range(2):
            bank = pS if bk == 0 else pS2
            emit_softmax_seg(
                ctx, bank[0:16, 0:512], 0, 16, 8, 64, sq[0:16, 0:512],
                ebr[0:16, :, :].rearrange("q s k -> q (s k)")
                [:, bk * 1024:(bk + 1) * 1024],
                nmsc[0:16, 0:8], densc[0:16, 0:8],
                pe_wait=(lambda v, mmt=mmt: v.wait_ge(s_pe.h, mmt)),
                slot_w=128, slot_off=lambda g: (g % 2) * 64)
        sm_t = s_dv.v

        @blk.tensor
        def _(t, sm_t=sm_t):
            t.wait_ge(s_dv.h, sm_t)
            last = None
            for bl in range(NB):
                last = t.matmul(pEf[:, bl * 16:(bl + 1) * 16],
                                ebr[0:16, bl, :], ident_of(ctx)[0:16, 0:16],
                                start=True, stop=True)
            last.then_inc(s_pe.h, 1)
        pt = s_pe.inc()
        se = s_sc.inc()

        @blk.scalar
        def _(s, pt=pt):
            s.wait_ge(s_pe.h, pt)
            s.copy(etb2x[:], pEf[:, 0:256]).then_inc(s_sc.h, 1)

        @blk.tensor
        def _(t, se=se, hp=hp):
            t.wait_ge(s_sc.h, se)
            last = None
            for bl in range(NB):
                last = t.matmul(pO[:, bl * 16:(bl + 1) * 16],
                                v_tm[:, bl // 2, hp * 128:(hp + 1) * 128],
                                etb2x[:, bl * 16:(bl + 1) * 16],
                                start=True, stop=True)
            last.then_inc(s_pe.h, 1)
        av = s_pe.inc()

        @blk.vector
        def _(v, av=av, hp=hp):
            v.wait_ge(s_pe.h, av)
            last = None
            for oi in range(2):
                for h2 in range(2):
                    r0 = oi * 8 + h2 * 4
                    last = v.tensor_copy(
                        ro_all[h2 * 64:(h2 + 1) * 64, hp, oi * 64:(oi + 1) * 64]
                        .rearrange("p (bl m) -> p bl m", m=4),
                        pO[h2 * 64:(h2 + 1) * 64, 0:256]
                        .rearrange("p (bl r) -> p bl r", r=16)[:, :, r0:r0 + 4])
            last.then_inc(s_dv.h, 1)
        s_dv.inc()
    ctx.barrier("a_ra")
    ctx.tap("ro_all", ro_all[:])
    if A_STOP <= 5:
        _close_all()
        return

    _mark("A_l2o")
    ag_t = emit_allgather_pair(ctx, ro_all[:], ctx.ag2_in, ctx.ag2_out,
                               ro_full, lambda g: None, W=128)

    @blk.sync
    def _(s, ag_t=ag_t):
        s.wait_ge(s_cast.h, ag_t)
    ctx.barrier("a_ag2")
    ctx.tap("ro_full", ro_full[:])
    es_lat2.close()

    emit_latproj(ctx, hbm["L0_attn_w_l2t_out"], 0, 16, ro_full, 128,
                 latw, pb2, resid_cb(lmf3, lmf2, 128))
    ctx.barrier("a_lmf3")
    ctx.tap("lmf3", lmf3[:])

    _mark("A_gkv")
    def lmkv_cb(oc, bank_ap, mm_t):
        if oc < 16:
            return bias_evac_cb(gk_fm, 144)(oc, bank_ap, mm_t)
        return bias_evac_cb(gv_fm, 144, 16)(oc, bank_ap, mm_t)

    emit_latproj(ctx, hbm["L0_attn_w_lm_kv"], 0, 32, lmf3, 128,
                 latw, pb2, lmkv_cb)
    ctx.barrier("a_lmkv")
    ctx.tap("gk_fm", gk_fm[:])
    ctx.tap("gv_fm", gv_fm[:])

    pe0 = s_pe.v
    ev0 = s_ev.v

    @blk.tensor
    def _(t, pe0=pe0, ev0=ev0):
        for c in range(NKC):
            if c >= 2:
                t.wait_ge(s_ev.h, ev0 + c - 1)
            t.transpose([pE, pE2][c % 2][:, 0:128],
                        gv_fm[:, c, :], ident_of(ctx)).then_inc(s_pe.h, 1)
    s_pe.inc(NKC)

    @blk.scalar
    def _(s, pe0=pe0):
        for c in range(NKC):
            s.wait_ge(s_pe.h, pe0 + c + 1)
            s.copy(gv_tm[:, c * 128:(c + 1) * 128],
                   [pE, pE2][c % 2][:, 0:128]).then_inc(s_ev.h, 1)
    s_ev.inc(NKC)
    ctx.barrier("a_gkv")
    if A_STOP <= 6:
        _close_all()
        return
    es_lat.close()
    es_kv.close()

    _mark("A_glob")
    es_late = _stk()
    og_tm = ctx.sb("a_og", [128, NT, 2048], BF16, es_late)
    es_q9 = _stk()
    gmask = ctx.sb("s_a_gmask", [128, NT, 256], BF16, es_q9)
    qre = ctx.sb("a_qre", [128, NKC, T], BF16, es_q9)
    qgz = [ctx.sb(f"a_qgz{i}", [128, 128], BF16, es_q9) for i in range(4)]
    ebf3 = ctx.sb("a_ebf3", [128, 4, 128], BF16, es_q9)
    etb3 = ctx.sb("a_etb3", [128, 4, 128], BF16, es_q9)
    pS = ctx.ps("a_ps9", [128, 512], F32, es_q9)
    pE = ctx.ps("a_pe9", [128, 1024], BF16, es_q9)
    pO = ctx.ps("a_po9", [128, 512], F32, es_q9)
    qre_t = s_ld.inc(16)

    @blk.sync
    def _(s):
        s.dma_start(out=qre[:],
                    in_=ctx.qspill_dram.rearrange("p (c t) -> p c t", c=NKC)
                    ).then_inc(s_ld.h, 16)
    gm_t = s_cast.inc(16)

    @blk.gpsimd
    def _(g, gm_t=gm_t):
        g.wait_ge(s_cast.h, gm_t - 16)
        g.dma_start(out=gmask[:],
                    in_=hbm["a_gmask"].rearrange("(tt p) c -> p tt c", p=128)
                    ).then_inc(s_cast.h, 16)

    dz = s_dv.inc()

    @blk.vector
    def _(v, dz=dz):
        v.wait_ge(s_dv.h, dz - 1)
        v.memset(qgz[0][:], 0.0)
        v.memset(qgz[1][:], 0.0)
        v.memset(qgz[2][:], 0.0)
        v.memset(qgz[3][:], 0.0).then_inc(s_dv.h, 1)

    for tt in range(NT):
        for hg in range(8):
            first = (tt == 0 and hg == 0)
            dq = s_dv.inc()

            @blk.vector
            def _(v, hg=hg, tt=tt, dq=dq, first=first):
                v.wait_ge(s_dv.h, dq - 1)
                if first:
                    v.wait_ge(s_ld.h, qre_t)
                last = None
                for hi in range(4):
                    h = hg * 4 + hi
                    hp, hh = h // 2, (h % 2) * 64
                    last = v.tensor_copy(
                        qgz[hi][hh:hh + 64, :],
                        qre[hh:hh + 64, hp, tt * 128:(tt + 1) * 128])
                last.then_inc(s_dv.h, 1)

            @blk.tensor
            def _(t, hg=hg, tt=tt, dq=dq, first=first):
                t.wait_ge(s_dv.h, dq)
                if first:
                    t.wait_ge(s_cast.h, gm_t)
                last = None
                for hi in range(4):
                    hp = (hg * 4 + hi) // 2
                    last = t.matmul(pS[:, hi * 128:(hi + 1) * 128],
                                    qgz[hi][:], gk_fm[:, hp, :],
                                    start=True, stop=True)
                last.then_inc(s_pe.h, 1)
            mmt = s_pe.inc()
            sm_t = emit_softmax_seg(
                ctx, pS[:, 0:512], 0, 128, 4, 128, sq[:, 0:512],
                ebf3[:].rearrange("q s k -> q (s k)"),
                nmsc[:, 0:4], densc[:, 0:4],
                bias_ap=gmask[:, tt, 0:128], bias_shared=True,
                valid_ap=gmask[:, tt, 128:256], valid_shared=True,
                pe_wait=(lambda v, mmt=mmt: v.wait_ge(s_pe.h, mmt)))

            @blk.tensor
            def _(t, sm_t=sm_t):
                t.wait_ge(s_dv.h, sm_t)
                last = None
                for hi in range(4):
                    last = t.transpose(pE[:, hi * 128:(hi + 1) * 128],
                                       ebf3[:, hi, :], ident_of(ctx))
                last.then_inc(s_pe.h, 1)
            pt = s_pe.inc()
            se = s_sc.inc()

            @blk.scalar
            def _(s, pt=pt):
                s.wait_ge(s_pe.h, pt)
                s.copy(etb3[:], pE[:, 0:512].rearrange("p (s k) -> p s k", s=4)
                       ).then_inc(s_sc.h, 1)

            @blk.tensor
            def _(t, se=se, hg=hg):
                t.wait_ge(s_sc.h, se)
                last = None
                for hi in range(4):
                    h = hg * 4 + hi
                    last = t.matmul(pO[:, hi * 64:(hi + 1) * 64], etb3[:, hi, :],
                                    gv_tm[:, h * 64:(h + 1) * 64],
                                    start=True, stop=True)
                last.then_inc(s_pe.h, 1)
            av = s_pe.inc()

            @blk.vector
            def _(v, av=av, hg=hg, tt=tt):
                v.wait_ge(s_pe.h, av)
                v.tensor_copy(og_tm[:, tt, hg * 256:(hg + 1) * 256],
                              pO[:, 0:256]).then_inc(s_dv.h, 1)
            s_dv.inc()
    ctx.barrier("a_glob")
    ctx.tap("og_tm", og_tm[:])
    if A_STOP <= 7:
        _close_all()
        return
    es_q9.close()

    _mark("A_a10")
    c_fm = ctx.sb("a_cfm", [128, NKC, T], BF16, es_late)
    o_fm = ctx.sb("a_ofm", [128, NKC, T], BF16, es_late)
    wob = [ctx.sb(f"a_wo{i}", [128, NKC, 512], BF16, es_late) for i in range(2)]
    pE = ctx.ps("a_pec", [128, 1024], BF16, es_late)
    pb3 = [ctx.ps(f"a_pb9{i}", [128, 512], F32, es_late) for i in range(4)]

    for tt in range(NT):
        d0 = s_dv.v

        @blk.vector
        def _(v, tt=tt, d0=d0):
            v.wait_ge(s_dv.h, d0)
            v.tensor_sub(sq[:], og_tm[:, tt, :], ol_tm[:, tt, :]).then_inc(s_dv.h, 1)
            v.wait_ge(s_dv.h, d0 + 1)
            v.tensor_tensor(
                sq[:].rearrange("p (h d) -> p h d", h=32),
                sq[:].rearrange("p (h d) -> p h d", h=32),
                g_tm[:, tt, :].rearrange("p (h o) -> p h o", o=1)
                .broadcast_to([128, 32, 64]), op=ALU.mult).then_inc(s_dv.h, 1)
            v.wait_ge(s_dv.h, d0 + 2)
            v.tensor_add(tmb[:], sq[:], ol_tm[:, tt, :]).then_inc(s_dv.h, 1)
        dv_c = s_dv.inc(3)
        pe0 = s_pe.v
        ev0 = s_ev.v

        @blk.tensor
        def _(t, tt=tt, dv_c=dv_c, pe0=pe0, ev0=ev0):
            for g in range(2):
                t.wait_ge(s_dv.h, dv_c)
                t.wait_ge(s_ev.h, ev0 + g)
                last = None
                for j in range(8):
                    fc = g * 8 + j
                    last = t.transpose(pE[:, j * 128:(j + 1) * 128],
                                       tmb[:, fc * 128:(fc + 1) * 128],
                                       ident_of(ctx))
                last.then_inc(s_pe.h, 1)
        s_pe.inc(2)

        @blk.scalar
        def _(s, tt=tt, pe0=pe0):
            for g in range(2):
                s.wait_ge(s_pe.h, pe0 + g + 1)
                s.copy(c_fm[:, g * 8:(g + 1) * 8, tt * 128:(tt + 1) * 128],
                       pE[:].rearrange("p (a b) -> p a b", a=8)).then_inc(s_ev.h, 1)
        s_ev.inc(2)
        ctx.barrier(f"a_cmb_{tt}")

    cast_base = s_cast.v
    pe_base0 = s_pe.v

    @blk.gpsimd
    def _(g, cast_base=cast_base, pe_base0=pe_base0):
        for og in range(4):
            g.wait_ge(s_cast.h, cast_base + 16 * og)
            if og >= 2:
                g.wait_ge(s_pe.h, pe_base0 + 8 * (og - 1))
            g.dma_start(out=wob[og % 2][:],
                        in_=hbm["L0_attn_w_out"][:, og * 512:(og + 1) * 512]
                        .rearrange("(kc p) o -> p kc o", p=128)
                        ).then_inc(s_cast.h, 16)
    s_cast.inc(64)

    sc_evs = []
    for og in range(4):
        for j in range(4):
            oc = og * 4 + j

            @blk.tensor
            def _(t, og=og, j=j, oc=oc, cast_base=cast_base):
                t.wait_ge(s_cast.h, cast_base + 16 * (og + 1))
                if oc >= 2:
                    t.wait_ge(s_sc.h, sc_evs[oc - 2])
                for fc in range(2):
                    bank = pb3[(oc % 2) * 2 + fc]
                    last = None
                    for kc in range(NKC):
                        last = t.matmul(bank[:],
                                        wob[og % 2][:, kc, j * 128:(j + 1) * 128],
                                        c_fm[:, kc, fc * 512:(fc + 1) * 512],
                                        start=(kc == 0), stop=(kc == NKC - 1))
                    last.then_inc(s_pe.h, 1)
            mm2 = s_pe.inc(2)
            se = s_sc.inc()

            @blk.scalar
            def _(s, oc=oc, mm2=mm2):
                s.wait_ge(s_pe.h, mm2)
                s.activation(o_fm[:, oc, 0:512], pb3[(oc % 2) * 2][:],
                             AF.Identity, bias=abias[:, 176 + oc:177 + oc])
                s.activation(o_fm[:, oc, 512:1024], pb3[(oc % 2) * 2 + 1][:],
                             AF.Identity, bias=abias[:, 176 + oc:177 + oc]
                             ).then_inc(s_sc.h, 1)
            sc_evs.append(se)
    ctx.barrier("a_wout")
    ctx.tap("o_fm", o_fm[:])

    pTe = [ctx.ps("a_et0", [128, 1024], BF16, es_late),
           ctx.ps("a_et1", [128, 1024], BF16, es_late)]
    _mark("A_epi")
    emit_out_epilogue(ctx, lambda fc, tt: o_fm[:, fc, tt * 128:(tt + 1) * 128],
                      pTe)
    _mark(None)
    es_late.close()
    es_msk.close()
    es_ph.close()



def ident_of(ctx):
    return ctx.t["s_ident"][:]


def emit_butterflies(ctx, zap, n):
    blk = ctx.block
    s_dv = ctx.sem("dv")
    h = 1
    while h < n:
        for c in range(n):
            if c & h:
                continue
            d0 = s_dv.v
            s_dv.inc(2)

            @blk.vector
            def _(v, a=zap(c), b=zap(c + h), d0=d0):
                v.wait_ge(s_dv.h, d0)
                v.tensor_add(a, a, b).then_inc(s_dv.h, 1)
                v.wait_ge(s_dv.h, d0 + 1)
                v.scalar_tensor_tensor(b, b, -2.0, a, op0=ALU.mult,
                                       op1=ALU.add).then_inc(s_dv.h, 1)
        h *= 2


def emit_fwht_step1(ctx, zap, nslices, width, pb):
    blk = ctx.block
    s_pe = ctx.sem("pe")
    s_ev = ctx.sem("ev")
    h128 = ctx.t["s_h128"]
    pe_base = s_pe.v
    ev_base = s_ev.v

    @blk.tensor
    def _(t):
        for c in range(nslices):
            bank = pb[c % 4]
            if c >= 4:
                t.wait_ge(s_ev.h, ev_base + c - 3)
            t.matmul(bank[:, 0:width], h128[:], zap(c), start=True,
                     stop=True).then_inc(s_pe.h, 1)
    s_pe.inc(nslices)

    @blk.scalar
    def _(s):
        for c in range(nslices):
            bank = pb[c % 4]
            s.wait_ge(s_pe.h, pe_base + c + 1)
            s.copy(zap(c), bank[:, 0:width]).then_inc(s_ev.h, 1)
    s_ev.inc(nslices)


def emit_absmax_row(ctx, srcs, width, mcolf, mcol, mxd, pT, row_dst):
    blk = ctx.block
    s_dv = ctx.sem("dv")
    s_pe = ctx.sem("pe")
    s_ev = ctx.sem("ev")
    s_b = ctx.sem("bnc")
    ntile = width // 128
    d0 = s_dv.v

    @blk.vector
    def _(v, d0=d0):
        for i, ap in enumerate(srcs):
            v.wait_ge(s_dv.h, d0 + i)
            v.reduce_max(mcolf[:, i * width:(i + 1) * width], ap, axis=AX.X,
                         apply_absolute_value=True).then_inc(s_dv.h, 1)
        v.wait_ge(s_dv.h, d0 + len(srcs))
        if len(srcs) == 2:
            v.tensor_max(mcolf[:, 0:width], mcolf[:, 0:width],
                         mcolf[:, width:2 * width]).then_inc(s_dv.h, 1)
        else:
            v.engine_nop().then_inc(s_dv.h, 1)
        v.wait_ge(s_dv.h, d0 + len(srcs) + 1)
        v.tensor_copy(mcol, mcolf[:, 0:width]).then_inc(s_dv.h, 1)
    s_dv.inc(len(srcs) + 2)
    tree_done = s_dv.v

    pe0 = s_pe.v
    dv0 = s_dv.v

    @blk.tensor
    def _(t, tree_done=tree_done, pe0=pe0, dv0=dv0):
        for tt in range(ntile):
            t.wait_ge(s_dv.h, tree_done)
            if tt >= 2:
                t.wait_ge(s_dv.h, dv0 + tt - 1)
            t.transpose(pT[tt % 2][:, 0:128], mcol[:, tt * 128:(tt + 1) * 128],
                        ident_of(ctx)).then_inc(s_pe.h, 1)
    s_pe.inc(ntile)

    @blk.vector
    def _(v, pe0=pe0, dv0=dv0):
        for tt in range(ntile):
            v.wait_ge(s_pe.h, pe0 + tt + 1)
            v.wait_ge(s_dv.h, dv0 + tt)
            v.reduce_max(mxd["f"][:, tt:tt + 1], pT[tt % 2][:, 0:128],
                         axis=AX.X).then_inc(s_dv.h, 1)
    s_dv.inc(ntile)
    red_done = s_dv.v
    d1 = s_dv.inc()

    @blk.vector
    def _(v, red_done=red_done):
        v.wait_ge(s_dv.h, red_done)
        v.tensor_copy(mxd["b"][:, 0:ntile], mxd["f"][:, 0:ntile]).then_inc(s_dv.h, 1)
    pe1 = s_pe.inc()

    @blk.tensor
    def _(t, d1=d1):
        t.wait_ge(s_dv.h, d1)
        t.transpose(pT[0][0:ntile, 0:128], mxd["b"][:, 0:ntile],
                    ident_of(ctx)).then_inc(s_pe.h, 1)
    ev1 = s_ev.inc()

    @blk.scalar
    def _(s, pe1=pe1):
        s.wait_ge(s_pe.h, pe1)
        s.copy(mxd["rows"][0:ntile, :], pT[0][0:ntile, 0:128]).then_inc(s_ev.h, 1)
    scr = ctx.bounce_dram
    s_bq = ctx.sem("bnq")
    bt0 = s_bq.inc(16)
    bt = s_bq.inc(16)

    @blk.gpsimd
    def _(g, ev1=ev1, bt0=bt0):
        g.wait_ge(s_ev.h, ev1)
        g.dma_start(out=scr[0:1, 2048:2048 + width].rearrange(
                        "o (a b) -> (o a) b", b=128),
                    in_=mxd["rows"][0:ntile, :]).then_inc(s_bq.h, 16)
        g.wait_ge(s_bq.h, bt0)
        g.dma_start(out=row_dst, in_=scr[0:1, 2048:2048 + width]
                    ).then_inc(s_bq.h, 16)
    return bt


def emit_quant_chunks(ctx, zap, n, width, scale_rep, sq, wait_first):
    blk = ctx.block
    s_dv = ctx.sem("dv")
    cpack = ctx.t["s_cpack"]
    d0 = s_dv.v

    @blk.vector
    def _(v, d0=d0):
        for c in range(n):
            v.wait_ge(s_dv.h, d0 + 2 * c)
            if c == 0:
                wait_first(v)
            v.tensor_mul(sq[:, 0:width], zap(c), scale_rep).then_inc(s_dv.h, 1)
            v.wait_ge(s_dv.h, d0 + 2 * c + 1)
            v.tensor_scalar(zap(c), sq[:, 0:width],
                            cpack[:, C_MAGIC:C_MAGIC + 1],
                            cpack[:, C_MAGIC:C_MAGIC + 1],
                            op0=ALU.add, op1=ALU.subtract).then_inc(s_dv.h, 1)
    s_dv.inc(2 * n)


def emit_phase_B(ctx, hbm):
    blk = ctx.block
    cpack = ctx.t["s_cpack"]
    sq = ctx.t["sq_scratch"]
    tmb = ctx.t["tm_scratch"]
    NI8 = 64
    RT2 = float(np.sqrt(2048.0))
    RT8 = float(np.sqrt(8192.0))
    NW = float(8192 * 2048)

    s_pe = ctx.sem("pe")
    s_ev = ctx.sem("ev")
    s_dv = ctx.sem("dv")
    s_sc = ctx.sem("sc")
    s_sil = ctx.sem("sil")
    s_po = ctx.sem("po")
    s_ld = ctx.sem("ld")
    s_b = ctx.sem("bnc")
    s_spl = ctx.sem("spl")

    es_ph = ExitStack()
    bstat = ctx.sb("b_stat", [128, 8], F32, es_ph)
    brow = ctx.sb("b_brow", [1, 64], F32, es_ph)
    z03 = ctx.sb("b_z03", [128, 32, T], BF16, es_ph)
    rp = ctx.sb("b_rp", [128, 2048], F32, es_ph)
    t1b = ctx.sb("b_t1", [128, 2048], BF16, es_ph)
    zsp = ctx.zspill_dram

    _ns = ctx.nc.named_scope
    es_scope_p1 = ExitStack(); es_scope_p1.enter_context(_ns("B_p1"))
    es_p1 = ExitStack()
    slab1 = [ctx.sb(f"b_p1s{i}", [128, 8192], F32, es_p1) for i in range(2)]
    acc = ctx.sb("b_acc", [128, 8], F32, es_p1)

    dv0 = s_dv.inc()

    @blk.vector
    def _(v):
        v.memset(brow[:], 0.0)
        v.memset(acc[:, 0:3], 0.0).then_inc(s_dv.h, 1)

    names = ("b_wsh_g", "b_wsh_u", "b_wsh_d")
    plan = [(m, j) for m in range(3) for j in range(2)]
    ld_base = s_ld.v
    sc_base = s_sc.v
    dv_base = s_dv.v

    @blk.sync
    def _(s):
        for i, (m, j) in enumerate(plan):
            s.wait_ge(s_ld.h, ld_base + 16 * i)
            if i >= 2:
                s.wait_ge(s_sc.h, sc_base + i - 1)
            if m < 2:
                src = hbm[names[m]][j * 512:(j + 1) * 512, :].rearrange(
                    "(a p) h -> p a h", p=128)
                dst = slab1[i % 2][:].rearrange("p (a h) -> p a h", a=4)
            else:
                src = hbm[names[m]][j * 128:(j + 1) * 128, :]
                dst = slab1[i % 2][:]
            s.dma_start(out=dst, in_=src).then_inc(s_ld.h, 16)
    s_ld.inc(16 * len(plan))

    @blk.scalar
    def _(s):
        for i, (m, j) in enumerate(plan):
            s.wait_ge(s_ld.h, ld_base + 16 * (i + 1))
            if i >= 2:
                s.wait_ge(s_dv.h, dv_base + i - 1)
            s.activation(slab1[i % 2][:], slab1[i % 2][:], AF.Abs,
                         accum_out=acc[:, 4 + i % 2:5 + i % 2]).then_inc(s_sc.h, 1)
    s_sc.inc(len(plan))

    @blk.vector
    def _(v):
        v.wait_ge(s_dv.h, dv0)
        for i, (m, j) in enumerate(plan):
            v.wait_ge(s_sc.h, sc_base + i + 1)
            v.wait_ge(s_dv.h, dv_base + i)
            v.tensor_add(acc[:, m:m + 1], acc[:, m:m + 1],
                         acc[:, 4 + i % 2:5 + i % 2]).then_inc(s_dv.h, 1)
    s_dv.inc(len(plan))
    acc_done = s_dv.v

    pair_bf = ctx.sb("b_pair", [128, 2], BF16, es_p1)
    ones_bf = ctx.sb("b_ones", [128, 1], BF16, es_p1)
    psum2 = ctx.ps("b_ps2", [1, 2], F32, es_p1)
    d_on = s_dv.inc()

    @blk.vector
    def _(v):
        v.wait_ge(s_dv.h, acc_done)
        v.memset(ones_bf[:], 1.0).then_inc(s_dv.h, 1)
    for m in range(3):
        emit_partition_sum(
            ctx, acc[:, m:m + 1], pair_bf, ones_bf[:], psum2[:],
            brow[0:1, 32:34], brow[0:1, m:m + 1],
            wait_fn=(lambda v, t=d_on: v.wait_ge(s_dv.h, t)))

    d0 = s_dv.v
    s_cast = ctx.sem("cast")
    s_cc = ctx.sem("cc")
    ex0 = s_cast.inc(16)
    cc0 = s_cc.inc()
    ex1 = s_cast.inc(16)

    @blk.gpsimd
    def _(g, d0=d0, ex0=ex0, cc0=cc0):
        g.wait_ge(s_dv.h, d0)
        g.wait_ge(s_cast.h, ex0 - 16)
        g.dma_start(out=ctx.rs_in, in_=brow[0:1, 0:8]).then_inc(s_cast.h, 16)
        g.wait_ge(s_cast.h, ex0)
        g.collective_compute("AllReduce", ALU.add,
                             replica_groups=[list(range(NCORES))],
                             ins=[ctx.rs_in.opt()], outs=[ctx.rs_out.opt()]
                             ).then_inc(s_cc.h)
        g.wait_ge(s_cc.h, cc0)
        g.dma_start(out=brow[0:1, 0:8], in_=ctx.rs_out).then_inc(s_cast.h, 16)

    @blk.vector
    def _(v, d0=d0, ex1=ex1):
        v.wait_ge(s_cast.h, ex1)
        for m in range(3):
            v.wait_ge(s_dv.h, d0 + 2 * m)
            v.tensor_scalar(brow[0:1, 8 + m:9 + m], brow[0:1, m:m + 1],
                            1.0 / NW, 1e-5, op0=ALU.mult, op1=ALU.max
                            ).then_inc(s_dv.h, 1)
            v.wait_ge(s_dv.h, d0 + 2 * m + 1)
            v.reciprocal(brow[0:1, 12 + m:13 + m], brow[0:1, 8 + m:9 + m]
                         ).then_inc(s_dv.h, 1)
    row_done = s_dv.inc(6)
    bstat_t = emit_bcast_rows(ctx, brow[0:1, 8:16], 8, bstat[:, 0:8],
                              wait_fn=lambda e: e.wait_ge(s_dv.h, row_done))

    @blk.scalar
    def _(s):
        s.wait_ge(s_b.h, bstat_t)
    es_p1.close()
    ctx.barrier("b_p1")
    es_scope_p1.close()

    es_scope_w2 = ExitStack(); es_scope_w2.enter_context(_ns("B_w2"))
    es_w2 = ExitStack()
    wslab = [ctx.sb(f"b_w2s{i}", [128, 2048], F32, es_w2) for i in range(2)]
    w2t2 = [ctx.sb(f"b_w2t2{i}", [128, 2048], BF16, es_w2) for i in range(2)]
    w2wT = [ctx.sb(f"b_w2wt{i}", [128, NKC, 128], BF16, es_w2) for i in range(2)]
    pT = [ctx.ps("b_w2p0", [128, 1024], BF16, es_w2),
          ctx.ps("b_w2p1", [128, 1024], BF16, es_w2)]

    wplan = ([("b_wsh_g", i * 128, 0, 0, i * 2048) for i in range(8)]
             + [("b_wsh_u", i * 128, 0, 1, 16384 + i * 2048) for i in range(8)]
             + [("b_wsh_d", o * 128, q * 2048, 2, 32768 + o * 8192 + q * 2048)
                for o in range(2) for q in range(4)])
    ld_b = s_ld.v
    sc_b = s_sc.v
    dv_b = s_dv.v
    cast_b = s_cast.v
    pe_b = s_pe.v
    ev_b = s_ev.v
    for i, (nm, r0, c0, m, dcol) in enumerate(wplan):

        @blk.sync
        def _(s, i=i, nm=nm, r0=r0, c0=c0):
            s.wait_ge(s_ld.h, ld_b + 16 * i)
            if i >= 2:
                s.wait_ge(s_sc.h, sc_b + 3 * (i - 2) + 1)
            s.dma_start(out=wslab[i % 2][:],
                        in_=hbm[nm][r0:r0 + 128, c0:c0 + 2048]
                        ).then_inc(s_ld.h, 16)
        s_ld.inc(16)

        @blk.scalar
        def _(s, i=i, m=m):
            s.wait_ge(s_ld.h, ld_b + 16 * (i + 1))
            if i >= 1:
                s.wait_ge(s_dv.h, dv_b + 2 * i - 1)
            s.activation(rp[:], wslab[i % 2][:], AF.Identity,
                         bias=cpack[:, C_MAGIC:C_MAGIC + 1],
                         scale=bstat[:, 4 + m:5 + m]).then_inc(s_sc.h, 1)
        rp_t = s_sc.inc()

        @blk.vector
        def _(v, i=i, rp_t=rp_t):
            v.wait_ge(s_sc.h, rp_t)
            v.wait_ge(s_dv.h, dv_b + 2 * i)
            v.tensor_scalar(t1b[:], rp[:], cpack[:, C_MAGIC:C_MAGIC + 1], 1.0,
                            op0=ALU.subtract, op1=ALU.min).then_inc(s_dv.h, 1)
            if i >= 2:
                v.wait_ge(s_pe.h, pe_b + 2 * (i - 1))
            v.wait_ge(s_dv.h, dv_b + 2 * i + 1)
            v.tensor_scalar(w2t2[i % 2][:], t1b[:], -1.0, None,
                            op0=ALU.max).then_inc(s_dv.h, 1)
        s_dv.inc(2)

        @blk.tensor
        def _(t, i=i):
            for g in range(2):
                t.wait_ge(s_dv.h, dv_b + 2 * (i + 1))
                if i >= 1:
                    t.wait_ge(s_ev.h, ev_b + 2 * (i - 1) + g + 1)
                last = None
                for j in range(8):
                    kc = g * 8 + j
                    last = t.transpose(pT[g][:, j * 128:(j + 1) * 128],
                                       w2t2[i % 2][:, kc * 128:(kc + 1) * 128],
                                       ident_of(ctx))
                last.then_inc(s_pe.h, 1)
        s_pe.inc(2)

        @blk.scalar
        def _(s, i=i):
            for g in range(2):
                s.wait_ge(s_pe.h, pe_b + 2 * i + g + 1)
                if i >= 2:
                    s.wait_ge(s_cast.h, cast_b + 16 * (i - 1))
                s.copy(w2wT[i % 2][:, g * 8:(g + 1) * 8, :],
                       pT[g][:].rearrange("p (a b) -> p a b", a=8)
                       ).then_inc(s_sc.h, 1)
        s_sc.inc(2)
        ev_cp = s_ev.inc(2)

        @blk.vector
        def _(v, i=i):
            for g in range(2):
                v.wait_ge(s_sc.h, sc_b + 3 * i + 2 + g)
                v.engine_nop().then_inc(s_ev.h, 1)

        @blk.gpsimd
        def _(g_, i=i, dcol=dcol, ev_cp=ev_cp):
            g_.wait_ge(s_ev.h, ev_cp)
            g_.wait_ge(s_cast.h, cast_b + 16 * i)
            g_.dma_start(out=ctx.wsh_dram[:, dcol:dcol + 2048],
                         in_=w2wT[i % 2][:].rearrange("p a b -> p (a b)")
                         ).then_inc(s_cast.h, 16)
        s_cast.inc(16)

    cc_gu = s_cc.inc()

    @blk.gpsimd
    def _(g, cast_b=cast_b):
        g.wait_ge(s_cast.h, cast_b + 16 * len(wplan))
        g.collective_compute("AllGather", ALU.bypass,
                             replica_groups=[list(range(NCORES))],
                             ins=[ctx.wsh_dram.opt()], outs=[ctx.wall_dram.opt()]
                             ).then_inc(s_cc.h)
    es_w2.close()
    es_scope_w2.close()
    es_scope_q1 = ExitStack(); es_scope_q1.enter_context(_ns("B_q1"))

    es_xq = ExitStack()
    xq1 = ctx.sb("b_xq1", [128, NKC, T], BF16, es_xq)
    rep = ctx.sb("b_rep", [128, 3 * T], F32, es_xq)
    brows = ctx.sb("b_rows", [1, 4096], F32, es_xq)
    bmxf = ctx.sb("b_mxf", [128, 8], F32, es_xq)
    bmxb = ctx.sb("b_mxb", [128, 8], BF16, es_xq)
    bmxr = ctx.sb("b_mxr", [8, 128], BF16, es_xq)
    mxd = {"f": bmxf, "b": bmxb, "rows": bmxr}

    es_q1p = ExitStack()
    pT = [ctx.ps("b_q1t0", [128, 1024], BF16, es_q1p),
          ctx.ps("b_q1t1", [128, 1024], BF16, es_q1p)]
    pb = [ctx.ps(f"b_q1b{i}", [128, 512], F32, es_q1p) for i in range(4)]

    emit_ln_to_fm(ctx, xq1, pT)
    emit_fwht_step1(ctx, lambda c: xq1[:, c // 2, (c % 2) * 512:(c % 2 + 1) * 512],
                    NKC * 2, 512, pb)
    ctx.barrier("b_fw1")
    emit_butterflies(ctx, lambda c: xq1[:, c, :], NKC)
    bt1 = emit_absmax_row(ctx, [xq1[:].rearrange("p c t -> p t c")], T,
                          sq[:], tmb[:, 0:T], mxd, pT, brows[0:1, 3072:4096])
    d0 = s_dv.v

    @blk.vector
    def _(v, d0=d0, bt1=bt1):
        v.wait_ge(ctx.sem("bnq").h, bt1)
        v.wait_ge(s_dv.h, d0)
        v.tensor_scalar(brows[0:1, 1024:2048], brows[0:1, 3072:4096],
                        1e-5 * RT2, None, op0=ALU.max).then_inc(s_dv.h, 1)
        v.wait_ge(s_dv.h, d0 + 1)
        v.reciprocal(brows[0:1, 0:1024], brows[0:1, 1024:2048]).then_inc(s_dv.h, 1)
        v.wait_ge(s_dv.h, d0 + 2)
        v.tensor_scalar(brows[0:1, 0:1024], brows[0:1, 0:1024], 127.0, None,
                        op0=ALU.mult).then_inc(s_dv.h, 1)
        v.wait_ge(s_dv.h, d0 + 3)
        v.tensor_scalar(brows[0:1, 1024:2048], brows[0:1, 3072:4096],
                        bstat[0:1, 0:1], 1.0 / (127.0 * RT2),
                        op0=ALU.mult, op1=ALU.mult).then_inc(s_dv.h, 1)
        v.wait_ge(s_dv.h, d0 + 4)
        v.tensor_scalar(brows[0:1, 2048:3072], brows[0:1, 3072:4096],
                        bstat[0:1, 1:2], 1.0 / (127.0 * RT2),
                        op0=ALU.mult, op1=ALU.mult).then_inc(s_dv.h, 1)
    rd = s_dv.inc(5)
    bt2 = emit_bcast_rows(ctx, brows[0:1, 0:3072], 3072, rep[:, 0:3072],
                          wait_fn=lambda e: e.wait_ge(s_dv.h, rd))
    emit_quant_chunks(ctx, lambda c: xq1[:, c, :], NKC, T, rep[:, 0:T],
                      sq, lambda v, bt2=bt2: v.wait_ge(s_b.h, bt2))
    ctx.barrier("b_q1")
    es_q1p.close()
    es_scope_q1.close()
    es_scope_gu = ExitStack(); es_scope_gu.enter_context(_ns("B_gu"))

    es_gu = ExitStack()
    ztmp = ctx.sb("b_ztmp", [128, 2, T], BF16, es_gu)
    wTs = [ctx.sb(f"b_wts{i}", [128, NKC, 128], BF16, es_gu) for i in range(2)]
    gsc = ctx.sb("b_gsc", [128, T], F32, es_gu)
    usc = ctx.sb("b_usc", [128, T], F32, es_gu)
    pb = [ctx.ps(f"b_gub{i}", [128, 512], F32, es_gu) for i in range(4)]

    gnames = ("L0_ffn_w_gate", "L0_ffn_w_up")
    steps = [(oc, m) for oc in range(NI8) for m in range(2)]
    ld_base = s_ld.v
    sc_tern_base = s_sc.v
    tern_dv = []
    tern_po = []
    tr_pe = []
    mm_pe = []
    bank_free = []
    z_done = {}
    spill_t = {}

    for i, (oc, m) in enumerate(steps):
        ld_t = ld_base + 16 * (i + 1)
        rk, sl = oc // 8, oc % 8
        dcol = m * 16384 + sl * 2048

        @blk.sync
        def _(s, i=i, rk=rk, dcol=dcol):
            s.wait_ge(s_ld.h, ld_base + 16 * i)
            if i == 0:
                s.wait_ge(s_cc.h, cc_gu)
            if i >= 2:
                s.wait_ge(s_pe.h, mm_pe[i - 2])
            s.dma_start(out=wTs[i % 2][:],
                        in_=ctx.wall_dram[rk * 128:(rk + 1) * 128,
                                          dcol:dcol + 2048]
                        .rearrange("p (kc j) -> p kc j", j=128)
                        ).then_inc(s_ld.h, 16)
        s_ld.inc(16)

        @blk.tensor
        def _(t, i=i, m=m, ld_t=ld_t):
            t.wait_ge(s_ld.h, ld_t)
            if i >= 2:
                t.wait_ge(s_dv.h, bank_free[i - 2])
            for fc in range(2):
                bank = pb[m * 2 + fc]
                last = None
                for kc in range(NKC):
                    last = t.matmul(bank[:], wTs[i % 2][:, kc, :],
                                    xq1[:, kc, fc * 512:(fc + 1) * 512],
                                    start=(kc == 0), stop=(kc == NKC - 1))
                last.then_inc(s_pe.h, 1)
        s_pe.inc(2)
        mm_pe.append(s_pe.v)

        if m == 0:
            d0 = s_dv.v

            @blk.vector
            def _(v, i=i, oc=oc, d0=d0):
                v.wait_ge(s_dv.h, d0)
                if oc >= 1:
                    v.wait_ge(s_dv.h, z_done[oc - 1])
                for fc in range(2):
                    v.wait_ge(s_pe.h, mm_pe[i] - 1 + fc)
                    v.tensor_mul(gsc[:, fc * 512:(fc + 1) * 512], pb[fc][:],
                                 rep[:, T + fc * 512:T + (fc + 1) * 512]
                                 ).then_inc(s_dv.h, 1)
            s_dv.inc(2)
            bank_free.append(s_dv.v)
            g_dv = s_dv.v
            sil_t = s_sil.inc()

            @blk.scalar
            def _(s, g_dv=g_dv):
                s.wait_ge(s_dv.h, g_dv)
                s.activation(gsc[:], gsc[:], AF.Silu).then_inc(s_sil.h, 1)
        else:
            d0 = s_dv.v
            sil_t = s_sil.v

            @blk.vector
            def _(v, i=i, oc=oc, d0=d0, sil_t=sil_t):
                v.wait_ge(s_dv.h, d0)
                for fc in range(2):
                    v.wait_ge(s_pe.h, mm_pe[i] - 1 + fc)
                    v.tensor_mul(usc[:, fc * 512:(fc + 1) * 512], pb[2 + fc][:],
                                 rep[:, 2 * T + fc * 512:2 * T + (fc + 1) * 512]
                                 ).then_inc(s_dv.h, 1)
                v.wait_ge(s_sil.h, sil_t)
                v.wait_ge(s_dv.h, d0 + 2)
                if oc >= 34:
                    v.wait_ge(s_spl.h, spill_t[oc - 2])
                dst = z03[:, oc, :] if oc < 32 else ztmp[:, oc % 2, :]
                v.tensor_mul(dst, gsc[:], usc[:]).then_inc(s_dv.h, 1)
            s_dv.inc(3)
            bank_free.append(s_dv.v)
            z_done[oc] = s_dv.v
            if oc >= 32:
                zt = s_dv.v
                spl = s_spl.inc(16)

                @blk.scalar
                def _(s, oc=oc, zt=zt):
                    s.wait_ge(s_dv.h, zt)
                    if oc >= 33:
                        s.wait_ge(s_spl.h, spill_t[oc - 1])
                    s.dma_start(out=zsp[(oc - 32) * 128:(oc - 31) * 128, :],
                                in_=ztmp[:, oc % 2, :]).then_inc(s_spl.h, 16)
                spill_t[oc] = spl

    ctx.barrier("b_gu")
    es_gu.close()
    es_xq.close()
    es_scope_gu.close()

    for half in range(2):
        es_scope_dn = ExitStack(); es_scope_dn.enter_context(_ns(f"B_dn{half}"))
        es_dn = ExitStack()
        zh = ctx.sb(f"b_zh{half}", [128, 32, 512], BF16, es_dn)
        brows2 = ctx.sb(f"b_rw2{half}", [1, 2048], F32, es_dn)
        rep2 = ctx.sb(f"b_rep2{half}", [128, 1024], F32, es_dn)
        bmxf = ctx.sb(f"b_mxf{half}", [128, 8], F32, es_dn)
        bmxb = ctx.sb(f"b_mxb{half}", [128, 8], BF16, es_dn)
        bmxr = ctx.sb(f"b_mxr{half}", [8, 128], BF16, es_dn)
        mxd = {"f": bmxf, "b": bmxb, "rows": bmxr}
        wTd = [ctx.sb(f"b_dwt{half}{i}", [128, NI8, 128], BF16, es_dn)
               for i in range(2)]
        o_fm = ctx.sb(f"b_ofm{half}", [128, NKC, 512], BF16, es_dn)
        pT = [ctx.ps(f"b_dnt{half}0", [128, 1024], BF16, es_dn),
              ctx.ps(f"b_dnt{half}1", [128, 1024], BF16, es_dn)]
        pb = [ctx.ps(f"b_dnb{half}{i}", [128, 512], F32, es_dn) for i in range(4)]
        ts0 = half * 512

        def zap(c, ts0=ts0, zh=zh):
            return z03[:, c, ts0:ts0 + 512] if c < 32 else zh[:, c - 32, :]

        spl_all = s_spl.v
        ld_t = s_ld.inc(16)

        ld_pre = ld_t - 16

        @blk.sync
        def _(s, ts0=ts0, zh=zh, spl_all=spl_all, ld_pre=ld_pre):
            s.wait_ge(s_spl.h, spl_all)
            s.wait_ge(s_ld.h, ld_pre)
            s.dma_start(out=zh[:],
                        in_=zsp.rearrange("(c p) t -> p c t", p=128)[:, :, ts0:ts0 + 512]
                        ).then_inc(s_ld.h, 16)

        for dec in (blk.tensor, blk.vector, blk.scalar, blk.gpsimd):
            @dec
            def _(e, ld_t=ld_t):
                e.wait_ge(s_ld.h, ld_t)

        emit_fwht_step1(ctx, zap, NI8, 512, pb)
        ctx.barrier(f"b_fw2_{half}")
        emit_butterflies(ctx, zap, NI8)
        bt1 = emit_absmax_row(
            ctx,
            [z03[:, :, ts0:ts0 + 512].rearrange("p c t -> p t c"),
             zh[:].rearrange("p c t -> p t c")],
            512, sq[:], tmb[:, 0:512], mxd, pT, brows2[0:1, 1536:2048])
        d0 = s_dv.v

        @blk.vector
        def _(v, d0=d0, brows2=brows2, bt1=bt1):
            v.wait_ge(ctx.sem("bnq").h, bt1)
            v.wait_ge(s_dv.h, d0)
            v.tensor_scalar(brows2[0:1, 1024:1536], brows2[0:1, 1536:2048],
                            1e-5 * RT8, None, op0=ALU.max).then_inc(s_dv.h, 1)
            v.wait_ge(s_dv.h, d0 + 1)
            v.reciprocal(brows2[0:1, 0:512], brows2[0:1, 1024:1536]).then_inc(s_dv.h, 1)
            v.wait_ge(s_dv.h, d0 + 2)
            v.tensor_scalar(brows2[0:1, 0:512], brows2[0:1, 0:512], 127.0, None,
                            op0=ALU.mult).then_inc(s_dv.h, 1)
            v.wait_ge(s_dv.h, d0 + 3)
            v.tensor_scalar(brows2[0:1, 512:1024], brows2[0:1, 1536:2048],
                            bstat[0:1, 2:3], 1.0 / (127.0 * RT8),
                            op0=ALU.mult, op1=ALU.mult).then_inc(s_dv.h, 1)
        rd = s_dv.inc(4)
        bt2 = emit_bcast_rows(ctx, brows2[0:1, 0:1024], 1024, rep2[:, 0:1024],
                              wait_fn=lambda e, rd=rd: e.wait_ge(s_dv.h, rd))
        emit_quant_chunks(ctx, zap, NI8, 512, rep2[:, 0:512], sq,
                          lambda v, bt2=bt2: v.wait_ge(s_b.h, bt2))
        ctx.barrier(f"b_q2_{half}")

        ld_base2 = s_ld.v
        mm_pe2 = {}
        out_dv2 = {}
        for oc in range(NKC):
            rk2, l2 = oc // 2, oc % 2
            dcol = 32768 + l2 * 8192

            @blk.sync
            def _(s, oc=oc, rk2=rk2, dcol=dcol):
                s.wait_ge(s_ld.h, ld_base2 + 16 * oc)
                if oc >= 2:
                    s.wait_ge(s_pe.h, mm_pe2[oc - 2])
                s.dma_start(out=wTd[oc % 2][:],
                            in_=ctx.wall_dram[rk2 * 128:(rk2 + 1) * 128,
                                              dcol:dcol + 8192]
                            .rearrange("p (kc j) -> p kc j", j=128)
                            ).then_inc(s_ld.h, 16)
            s_ld.inc(16)

            @blk.tensor
            def _(t, oc=oc):
                t.wait_ge(s_ld.h, ld_base2 + 16 * (oc + 1))
                if oc >= 2:
                    t.wait_ge(s_dv.h, out_dv2[oc - 2])
                bank = pb[oc % 2]
                last = None
                for kc in range(NI8):
                    last = t.matmul(bank[:], wTd[oc % 2][:, kc, :], zap(kc),
                                    start=(kc == 0), stop=(kc == NI8 - 1))
                last.then_inc(s_pe.h, 1)
            mmt = s_pe.inc()
            mm_pe2[oc] = mmt
            d0 = s_dv.v

            @blk.vector
            def _(v, oc=oc, mmt=mmt, d0=d0):
                v.wait_ge(s_pe.h, mmt)
                v.wait_ge(s_dv.h, d0)
                v.tensor_mul(o_fm[:, oc, :], pb[oc % 2][:],
                             rep2[:, 512:1024]).then_inc(s_dv.h, 1)
            s_dv.inc(1)
            out_dv2[oc] = s_dv.v

        ctx.barrier(f"b_dn_{half}")
        emit_out_epilogue(
            ctx,
            lambda fc, tt, half=half, o_fm=o_fm:
                o_fm[:, fc, (tt - half * 4) * 128:(tt - half * 4 + 1) * 128],
            pT, tiles=range(half * 4, half * 4 + 4))
        es_dn.close()
        es_scope_dn.close()

    es_ph.close()



def emit_phase_D(ctx, hbm):
    blk = ctx.block
    cpack = ctx.t["s_cpack"]
    sq = ctx.t["sq_scratch"]
    spack = ctx.t["s_d_spack"]
    NI = 48

    es_ph = ExitStack()
    xn_fm = ctx.sb("d_xnfm", [128, NKC, T], BF16, es_ph)
    fuse = ctx.sb("d_fuse", [128, NI, 512], BF16, es_ph)
    o_fm = ctx.sb("d_ofm", [128, NKC, 512], BF16, es_ph)

    s_pe = ctx.sem("pe")
    s_ev = ctx.sem("ev")
    s_dv = ctx.sem("dv")
    s_sc = ctx.sem("sc")
    s_cast = ctx.sem("cast")

    es_ln = ExitStack()
    pT = [ctx.ps("d_pt0", [128, 1024], BF16, es_ln),
          ctx.ps("d_pt1", [128, 1024], BF16, es_ln)]
    emit_ln_to_fm(ctx, xn_fm, pT)
    es_ln.close()

    for half in range(2):
        ts0 = half * 512
        es_gu = ExitStack()
        g8 = [ctx.sb(f"d_g8{half}{i}", [128, NKC, 256], FP8, es_gu) for i in range(2)]
        u8 = [ctx.sb(f"d_u8{half}{i}", [128, NKC, 256], FP8, es_gu) for i in range(2)]
        pb = [ctx.ps(f"d_pb{half}{i}", [128, 512], F32, es_gu) for i in range(8)]
        NOG = 24

        cast_base = s_cast.v
        pe_base = s_pe.v
        sc_base = s_sc.v
        dv_base = s_dv.v

        @blk.gpsimd
        def _(g, half=half, cast_base=cast_base, pe_base=pe_base):
            for og in range(NOG):
                g.wait_ge(s_cast.h, cast_base + 32 * og)
                if False:
                    pass
                if og >= 2:
                    g.wait_ge(s_pe.h, pe_base + 4 * (og - 1))
                g.dma_start(
                    out=g8[og % 2][:],
                    in_=hbm["L1_ffn_gate_w"][:, og * 256:(og + 1) * 256]
                    .rearrange("(kc p) o -> p kc o", p=128)).then_inc(s_cast.h, 16)
                g.wait_ge(s_cast.h, cast_base + 32 * og + 16)
                g.dma_start(
                    out=u8[og % 2][:],
                    in_=hbm["L1_ffn_up_w"][:, og * 256:(og + 1) * 256]
                    .rearrange("(kc p) o -> p kc o", p=128)).then_inc(s_cast.h, 16)
        s_cast.inc(32 * NOG)

        @blk.tensor
        def _(t, half=half, cast_base=cast_base, dv_base=dv_base):
            for og in range(NOG):
                t.wait_ge(s_cast.h, cast_base + 32 * (og + 1))
                if og >= 2:
                    t.wait_ge(s_dv.h, dv_base + 2 * (og - 1))
                for which, wb in ((0, g8[og % 2]), (1, u8[og % 2])):
                    for oc2 in range(2):
                        bank = pb[(og % 2) * 4 + which * 2 + oc2]
                        last = None
                        for kc in range(NKC):
                            last = t.matmul(
                                bank[:], wb[:, kc, oc2 * 128:(oc2 + 1) * 128],
                                xn_fm[:, kc, ts0:ts0 + 512],
                                start=(kc == 0), stop=(kc == NKC - 1))
                        last.then_inc(s_pe.h, 1)
        s_pe.inc(4 * NOG)

        @blk.scalar
        def _(sc_e, half=half, pe_base=pe_base, sc_base=sc_base, dv_base=dv_base):
            for og in range(NOG):
                for oc2 in range(2):
                    oc = og * 2 + oc2
                    gbank = pb[(og % 2) * 4 + oc2]
                    ubank = pb[(og % 2) * 4 + 2 + oc2]
                    sc_e.wait_ge(s_pe.h, pe_base + 4 * og + 4)
                    if og >= 1:
                        sc_e.wait_ge(s_dv.h, dv_base + 2 * (og - 1) + oc2 + 1)
                    sg = sq[:, oc2 * 1024:oc2 * 1024 + 512]
                    su = sq[:, oc2 * 1024 + 512:oc2 * 1024 + 1024]
                    sc_e.activation(sg, gbank[:], AF.Silu,
                                    scale=spack[:, oc:oc + 1])
                    sc_e.activation(su, ubank[:], AF.Copy,
                                    scale=spack[:, 48 + oc:48 + oc + 1]
                                    ).then_inc(s_sc.h, 1)
        s_sc.inc(2 * NOG)

        @blk.vector
        def _(v, half=half, sc_base=sc_base):
            for og in range(NOG):
                for oc2 in range(2):
                    oc = og * 2 + oc2
                    v.wait_ge(s_sc.h, sc_base + 2 * og + oc2 + 1)
                    sg = sq[:, oc2 * 1024:oc2 * 1024 + 512]
                    su = sq[:, oc2 * 1024 + 512:oc2 * 1024 + 1024]
                    v.tensor_mul(fuse[:, oc, :], sg, su).then_inc(s_dv.h, 1)
        s_dv.inc(2 * NOG)

        ctx.barrier(f"d_gu_{half}")
        es_gu.close()

        es_dn = ExitStack()
        d8 = [ctx.sb(f"d_d8{half}{i}", [128, NI, 256], FP8, es_dn) for i in range(2)]
        pb = [ctx.ps(f"d_pd{half}{i}", [128, 512], F32, es_dn) for i in range(4)]
        NOGD = 8
        cast_base = s_cast.v
        pe_base = s_pe.v
        sc_base = s_sc.v

        @blk.gpsimd
        def _(g, half=half, pe_base=pe_base, cast_base=cast_base):
            for og in range(NOGD):
                g.wait_ge(s_cast.h, cast_base + 16 * og)
                if og >= 2:
                    g.wait_ge(s_pe.h, pe_base + 2 * (og - 1))
                g.dma_start(
                    out=d8[og % 2][:],
                    in_=hbm["L1_ffn_down_w"][:, og * 256:(og + 1) * 256]
                    .rearrange("(kc p) o -> p kc o", p=128)).then_inc(s_cast.h, 16)
        s_cast.inc(16 * NOGD)

        @blk.tensor
        def _(t, half=half, cast_base=cast_base, sc_base=sc_base):
            for og in range(NOGD):
                t.wait_ge(s_cast.h, cast_base + 16 * (og + 1))
                for oc2 in range(2):
                    if og >= 2:
                        t.wait_ge(s_sc.h, sc_base + 2 * (og - 2) + oc2 + 1)
                    bank = pb[(og % 2) * 2 + oc2]
                    wb = d8[og % 2]
                    last = None
                    for kc in range(NI):
                        last = t.matmul(bank[:], wb[:, kc, oc2 * 128:(oc2 + 1) * 128],
                                        fuse[:, kc, :],
                                        start=(kc == 0), stop=(kc == NI - 1))
                    last.then_inc(s_pe.h, 1)
        s_pe.inc(2 * NOGD)

        @blk.scalar
        def _(sc_e, half=half, pe_base=pe_base):
            for og in range(NOGD):
                for oc2 in range(2):
                    oc = og * 2 + oc2
                    bank = pb[(og % 2) * 2 + oc2]
                    sc_e.wait_ge(s_pe.h, pe_base + 2 * og + oc2 + 1)
                    sc_e.activation(o_fm[:, oc, :], bank[:], AF.Copy,
                                    scale=spack[:, 96 + oc:96 + oc + 1]
                                    ).then_inc(s_sc.h, 1)
        s_sc.inc(2 * NOGD)

        ctx.barrier(f"d_dn_{half}")
        es_dn.close()

        es_ep = ExitStack()
        pT = [ctx.ps(f"d_et{half}0", [128, 1024], BF16, es_ep),
              ctx.ps(f"d_et{half}1", [128, 1024], BF16, es_ep)]
        emit_out_epilogue(
            ctx, lambda fc, tt: o_fm[:, fc, (tt - half * 4) * 128:(tt - half * 4 + 1) * 128],
            pT, tiles=range(half * 4, half * 4 + 4))
        es_ep.close()

    es_ph.close()



def build_program(phases, want_input=True, want_output=True):
    nc = bass.Bass("TRN2", target_bir_lowering=False)
    hbm = {}

    def din(name, shape):
        hbm[name] = nc.dram_tensor(name, list(shape), F32, kind="ExternalInput").ap()

    din("x", (T, D))
    din("cpack", (128, 8))
    hbm["ident"] = nc.dram_tensor("ident", [128, 128], BF16, kind="ExternalInput").ap()
    y = nc.dram_tensor("y", [T, D], F32, kind="ExternalOutput").ap()

    if "C" in phases:
        din("L1_attn_B", (1024, 2048))
        din("L1_attn_C", (2048, 1024))
        din("c_dpack", (128, 16))
    if "D" in phases:
        din("L1_ffn_gate_w", (2048, 6144))
        din("L1_ffn_up_w", (2048, 6144))
        din("L1_ffn_down_w", (6144, 2048))
        din("d_spack", (128, 112))
    if "A" in phases:
        din("L0_attn_w_qkv", (2048, 6144))
        din("L0_attn_w_lm_qkv", (2048, 6144))
        din("L0_attn_w_lm_out", (2048, 2048))
        din("L0_attn_w_l2t_q", (2048, 2048))
        din("L0_attn_w_l2t_out", (2048, 2048))
        din("L0_attn_w_lm_kv", (2048, 4096))
        din("L0_attn_w_out", (2048, 2048))
        din("L0_attn_w_gate", (2048, 32))
        din("a_bias", (128, 208))
        din("a_rope", (128, 2048))
        din("a_mloc", (64, 64))
        din("a_mlat", (128, 128))
        din("a_gmask", (1024, 256))
        din("a_pswap", (128, 128))
        din("a_poolq", (128, 128))
        din("a_id2", (64, 128))
    if "B" in phases:
        din("b_wsh_g", (1024, 2048))
        din("b_wsh_u", (1024, 2048))
        din("b_wsh_d", (256, 8192))
        hbm["h128"] = nc.dram_tensor("h128", [128, 128], BF16,
                                     kind="ExternalInput").ap()

    es = ExitStack()
    with nc.Block() as block:
        ctx = Ctx(nc, block, es)
        ctx.sb("x_tile", [128, 2048], F32)
        ctx.sb("s_ident", [128, 128], BF16)
        ctx.sb("s_cpack", [128, 8], F32)
        ctx.sb("sdummy", [1, 4], F32)
        ctx.sb("sq_scratch", [128, 2048], F32)
        ctx.sb("stats", [128, 8], F32)
        ctx.sb("tm_scratch", [128, 2048], BF16)
        if "C" in phases:
            ctx.sb("s_c_dpack", [128, 16], F32)
        if "D" in phases:
            ctx.sb("s_d_spack", [128, 112], F32)
        if "A" in phases:
            ctx.sb("s_abias", [128, 208], F32)
            ctx.qspill_dram = nc.dram_tensor("a_qsp", [128, 16384], BF16,
                                             kind="Internal").ap()
            ctx.ag1_in = nc.dram_tensor("a_ag1i", [128, 1024], BF16,
                                        kind="Internal").ap()
            ctx.ag1_out = nc.dram_tensor("a_ag1o", [256, 1024], BF16,
                                         kind="Internal").ap()
            ctx.ag2_in = nc.dram_tensor("a_ag2i", [128, 2048], BF16,
                                        kind="Internal").ap()
            ctx.ag2_out = nc.dram_tensor("a_ag2o", [256, 2048], BF16,
                                         kind="Internal").ap()
        if "B" in phases:
            ctx.sb("s_h128", [128, 128], BF16)
            ctx.bounce_dram = nc.dram_tensor("b_bnc", [1, 4096], F32,
                                             kind="Internal").ap()
            ctx.zspill_dram = nc.dram_tensor("b_zsp", [4096, 1024], BF16,
                                             kind="Internal").ap()
            ctx.wsh_dram = nc.dram_tensor("b_wshd", [128, 49152], BF16,
                                          kind="Internal").ap()
            ctx.wall_dram = nc.dram_tensor("b_wall", [1024, 49152], BF16,
                                           kind="Internal",
                                           addr_space="Shared").ap()
            ctx.rs_in = nc.dram_tensor("b_rsi", [1, 8], F32,
                                       kind="Internal").ap()
            ctx.rs_out = nc.dram_tensor("b_rso", [1, 8], F32,
                                        kind="Internal").ap()

        s_ld = ctx.sem("ld")
        s_st = ctx.sem("st")

        @block.sync
        def _(s):
            s.dma_start(out=ctx.t["s_ident"][:], in_=hbm["ident"]).then_inc(s_ld.h, 16)
            s.dma_start(out=ctx.t["s_cpack"][:], in_=hbm["cpack"]).then_inc(s_ld.h, 16)
            if "C" in phases:
                s.dma_start(out=ctx.t["s_c_dpack"][:], in_=hbm["c_dpack"]).then_inc(s_ld.h, 16)
            if "D" in phases:
                s.dma_start(out=ctx.t["s_d_spack"][:], in_=hbm["d_spack"]).then_inc(s_ld.h, 16)
            if "B" in phases:
                s.dma_start(out=ctx.t["s_h128"][:], in_=hbm["h128"]).then_inc(s_ld.h, 16)
            if "A" in phases:
                s.dma_start(out=ctx.t["s_abias"][:], in_=hbm["a_bias"]).then_inc(s_ld.h, 16)
        n_pre = (2 + (1 if "C" in phases else 0) + (1 if "D" in phases else 0)
                 + (1 if "B" in phases else 0) + (1 if "A" in phases else 0))
        s_ld.inc(16 * n_pre)
        pre_target = s_ld.v

        @block.scalar
        def _(e):
            e.memzero(ctx.t["sdummy"][:])

        for dec in (block.sync, block.tensor, block.vector, block.scalar, block.gpsimd):
            @dec
            def _(e):
                e.wait_ge(s_ld.h, pre_target)

        xh = nc.dram_tensor("xh", [T, D], F32, kind="Internal").ap() \
            if len(phases) > 1 else None
        for i, ph in enumerate(phases):
            ctx.x_src = hbm["x"] if i == 0 else xh
            ctx.x_dst = y if i == len(phases) - 1 else xh
            if ph == "A":
                emit_phase_A(ctx, hbm)
            elif ph == "B":
                emit_phase_B(ctx, hbm)
            elif ph == "C":
                emit_phase_C(ctx, hbm)
            elif ph == "D":
                emit_phase_D(ctx, hbm)

        @block.sync
        def _(s):
            s.wait_ge(s_st.h, s_st.v)

    es.close()
    return nc



def _consts_pack():
    c = np.zeros((128, 8), np.float32)
    c[:, C_MAGIC] = MAGIC
    c[:, C_NMAGIC] = -MAGIC
    c[:, C_EPS] = 1e-5
    c[:, C_ONE] = 1.0
    c[:, C_NEG1E30] = -1e30
    c[:, C_127] = 127.0
    c[:, C_EPS9] = 1e-9
    return c


def _ident_bf():
    return np.eye(128).astype(ml_dtypes.bfloat16)


def _a_packs(core):
    h = core % 2
    DH, M, S = 64, 4, 64
    pos = (h * 1024 + np.arange(1024)).astype(np.float64)
    f = np.arange(128)
    inv = 1.0 / (10000.0 ** (2 * ((f % 64) // 2) / 64.0))
    ang = pos[None, :] * inv[:, None]
    cos = np.cos(ang)
    sgn = np.where(f % 2 == 0, -1.0, 1.0)
    sin = np.sin(ang) * sgn[:, None]
    rope = np.concatenate([cos, sin], 1).astype(np.float32)
    psw = np.zeros((128, 128), np.float32)
    psw[np.arange(128), np.arange(128) ^ 1] = 1.0
    NEG = -1e30
    mloc = np.where(np.tril(np.ones((64, 64), bool)), 0.0, NEG).astype(np.float32)
    li = np.arange(128)
    mlat = np.where((li[:, None] // 4) >= (li[None, :] // 4), 0.0, NEG
                    ).astype(np.float32)
    gm = np.zeros((1024, 256), np.float32)
    for tt in range(8):
        tok = h * 1024 + tt * 128 + np.arange(128)
        tb = tok // 64
        valid = tb[:, None] > (li[None, :] // 4)
        gm[tt * 128:(tt + 1) * 128, 0:128] = np.where(valid, 0.0, NEG)
        gm[tt * 128:(tt + 1) * 128, 128:256] = valid.astype(np.float32)
    id2 = np.concatenate([np.eye(64), np.eye(64)], 1).astype(np.float32)
    return dict(a_rope=rope, a_pswap=psw, a_mloc=mloc, a_mlat=mlat, a_gmask=gm,
                a_id2=id2)


def _a_bias_pack(inputs):
    b = np.zeros((128, 208), np.float32)

    def put(vec, c0):
        v = np.asarray(vec, np.float32)
        n = v.size // 128
        b[:, c0:c0 + n] = v.reshape(n, 128).T
    put(inputs["L0_attn_b_qkv"], 0)
    put(inputs["L0_attn_b_lm_qkv"], 48)
    put(inputs["L0_attn_b_lm_out"], 96)
    put(inputs["L0_attn_b_l2t_q"], 112)
    put(inputs["L0_attn_b_l2t_out"], 128)
    put(inputs["L0_attn_b_lm_kv"], 144)
    put(inputs["L0_attn_b_out"], 176)
    b[0:32, 192] = np.asarray(inputs["L0_attn_b_gate"], np.float32)
    return b


def _a_poolq_pack(inputs):
    pq = np.asarray(inputs["L0_attn_pool_q"], np.float32).reshape(4, 32, 64)
    out = np.zeros((128, 128), np.float32)
    for hp in range(16):
        for m in range(4):
            out[0:64, hp * 8 + m] = pq[m, 2 * hp, :]
            out[64:128, hp * 8 + 4 + m] = pq[m, 2 * hp + 1, :]
    return out


def _h128_bf():
    i = np.arange(128)
    pc = np.array([bin(v).count("1") for v in range(128)])
    h = 1.0 - 2.0 * (pc[i[:, None] & i[None, :]] % 2)
    return h.astype(ml_dtypes.bfloat16)


def kernel(**inputs):
    return _kernel(inputs, groups=["ABCD"])


def _kernel(inputs, groups):
    x_full = np.ascontiguousarray(inputs["L0_attn_x"], dtype=np.float32)
    B, S, _ = x_full.shape
    cpack = _consts_pack()
    ident = _ident_bf()

    core_common = []
    for c in range(NCORES):
        s, h = c // 2, c % 2
        core_common.append(dict(
            cpack=cpack, ident=ident,
        ))

    x_cur = x_full.reshape(NCORES, T, D).copy()
    for gi, phases in enumerate(groups):
        nc = build_program(phases)
        in_maps = []
        for c in range(NCORES):
            m = dict(core_common[c])
            m["x"] = x_cur[c]
            if "C" in phases:
                m["L1_attn_B"] = np.ascontiguousarray(inputs["L1_attn_B"], np.float32)
                m["L1_attn_C"] = np.ascontiguousarray(inputs["L1_attn_C"], np.float32)
                m["c_dpack"] = np.ascontiguousarray(
                    inputs["L1_attn_D"].reshape(16, 128).T, np.float32)
            if "A" in phases:
                for nm_ in ("w_qkv", "w_lm_qkv", "w_lm_out", "w_l2t_q",
                            "w_l2t_out", "w_lm_kv", "w_out", "w_gate"):
                    m["L0_attn_" + nm_] = np.ascontiguousarray(
                        inputs["L0_attn_" + nm_], np.float32)
                m.update(_a_packs(c))
                m["a_bias"] = _a_bias_pack(inputs)
                m["a_poolq"] = _a_poolq_pack(inputs)
            if "B" in phases:
                m["b_wsh_g"] = np.ascontiguousarray(
                    inputs["L0_ffn_w_gate"][c * 1024:(c + 1) * 1024, :], np.float32)
                m["b_wsh_u"] = np.ascontiguousarray(
                    inputs["L0_ffn_w_up"][c * 1024:(c + 1) * 1024, :], np.float32)
                m["b_wsh_d"] = np.ascontiguousarray(
                    inputs["L0_ffn_w_down"][c * 256:(c + 1) * 256, :], np.float32)
                m["h128"] = _h128_bf()
            if "D" in phases:
                m["L1_ffn_gate_w"] = np.ascontiguousarray(inputs["L1_ffn_gate_w"], np.float32)
                m["L1_ffn_up_w"] = np.ascontiguousarray(inputs["L1_ffn_up_w"], np.float32)
                m["L1_ffn_down_w"] = np.ascontiguousarray(inputs["L1_ffn_down_w"], np.float32)
                sp = np.zeros((128, 112), np.float32)
                sp[:, 0:48] = inputs["L1_ffn_gate_scale"].reshape(48, 128).T
                sp[:, 48:96] = inputs["L1_ffn_up_scale"].reshape(48, 128).T
                sp[:, 96:112] = inputs["L1_ffn_down_scale"].reshape(16, 128).T
                m["d_spack"] = sp
            in_maps.append(m)
        res = run_bass_kernel_spmd(nc, in_maps, core_ids=list(range(NCORES)))
        x_cur = np.stack([np.asarray(r["y"]) for r in res.results], 0)

    return x_cur.reshape(B, S, D).astype(np.float32)
